# version 07
# baseline (speedup 1.0000x reference)
import numpy as np
import ml_dtypes
from contextlib import ExitStack

import concourse.bass as bass
import concourse.mybir as mybir
from concourse.bass_utils import run_bass_kernel_spmd

F32 = mybir.dt.float32
BF16 = mybir.dt.bfloat16
I32 = mybir.dt.int32
AF = mybir.ActivationFunctionType
ALU = mybir.AluOpType
AX = mybir.AxisListType

D = 1024
DF = 3072
NH = 16
DH = 64
P = 128
NC8 = D // P
OCF = DF // P
RMS_EPS = 1.1920929e-07
LN_EPS = 1e-5
EPS_LIN = 1e-6
QMAX = 32767.0
ACLIP = QMAX * 1e-12
WQMAX = 7.0
WCLIP = WQMAX * 1e-12

SY, SC, VE, TE, GP = "sync", "scalar", "vector", "tensor", "gpsimd"


def A(t, off, *dims):
    w = t.shape[1]
    return bass.AP(t, off, [[w, P]] + [list(d) for d in dims])


class Ctr:
    __slots__ = ("n",)

    def __init__(self):
        self.n = 0


class K:
    def __init__(self, cur, eng, sems):
        self.cur = cur
        self.eng = eng
        self.sems = sems
        self.ctrs = {}

    def ctr(self, name):
        if name not in self.ctrs:
            self.ctrs[name] = Ctr()
        return self.ctrs[name]

    def do(self, engname, fn, inc=None, incn=1, dr=False):
        if self.cur == engname:
            ins = fn(self.eng)
            if inc is not None:
                ins.then_inc(self.sems[inc], incn)
            if dr:
                self.eng.drain()
        if inc is not None:
            self.ctr(inc).n += incn

    def dma(self, engname, out, in_, inc, slow=False):
        self.do(engname,
                lambda e: e.dma_start(out=out, in_=in_,
                                      allow_slow_non_contiguous=slow),
                inc=inc, incn=16)

    def wait(self, engname, ctrname, val=None):
        if self.cur == engname:
            self.eng.wait_ge(self.sems[ctrname],
                             self.ctr(ctrname).n if val is None else val)

    def count(self, ctrname):
        return self.ctr(ctrname).n


def build_nc(T, n_cores, stiles, dbg=False):
    NT = T // P
    NS = NT // stiles
    NO5 = T // 512
    SD = stiles * D
    assert NT % stiles == 0 and T % 512 == 0
    WU_OT = DF // P // n_cores
    WD_OT = D // P // n_cores
    assert DF % (P * n_cores) == 0 and D % (P * n_cores) == 0
    pair_groups = ([[2 * g, 2 * g + 1] for g in range(n_cores // 2)]
                   if n_cores > 1 else [[0]])
    all_group = [list(range(n_cores))]

    nc = bass.Bass("TRN2", target_bir_lowering=False,
                   detect_race_conditions=False)
    ap = {}

    def din(name, shape, dt=F32):
        ap[name] = nc.dram_tensor(name, shape, dt, kind="ExternalInput").ap()

    def dint(name, shape, dt):
        ap[name] = nc.dram_tensor(name, shape, dt, kind="Internal").ap()

    din("x", [T, D])
    din("rc", [T, 512], BF16)
    din("rs", [T, 512], BF16)
    din("wqkvT", [D, 3 * D])
    din("bqkv_rep", [P, 3 * D])
    din("gq_rep", [P, D])
    din("gk_rep", [P, D])
    din("woutT", [D, D])
    din("bout_rep", [P, D])
    din("gout_rep", [P, D])
    din("wu_slice", [WU_OT * P, D])
    din("wg_slice", [WU_OT * P, D])
    din("wd_slice", [WD_OT * P, DF])
    din("ident", [P, P])

    yout = nc.dram_tensor("y", [T, D], F32, kind="ExternalOutput").ap()

    dint("xn_sp", [D, T], BF16)
    dint("ao_sp", [T, D], BF16)
    dint("wq_bf", [D, 3 * D], BF16)
    dint("wo_bf", [D, D], BF16)
    dint("sp_up", [DF, T], BF16)
    dint("sp_gt", [DF, T], BF16)
    dint("sp_g2", [DF, T], BF16)
    dint("sp_o", [DF, T], BF16)
    for nm, w in [("cc1", 1), ("cc3", 2), ("cc4", 1), ("cc5", 1)]:
        dint(nm + "_i", [w, P], F32)
        dint(nm + "_o", [w, P], F32)
    dint("vk_i", [65, NH * DH], F32)
    dint("vk_o", [65, NH * DH], F32)
    if dbg:
        def dout(name, shape, dt):
            ap[name] = nc.dram_tensor(name, shape, dt,
                                      kind="ExternalOutput").ap()
        dout("dbg_q", [P, stiles * D], BF16)
        dout("dbg_k", [P, stiles * D], BF16)
        dout("dbg_v", [P, stiles * D], BF16)
        dout("dbg_kf", [P, D], BF16)
        dout("dbg_vp", [P, NH * 65], BF16)
        dout("dbg_qf", [P, D], BF16)
        dout("dbg_wk1", [P, D], BF16)
        dout("dbg_wk2", [P, D], BF16)
        dout("dbg_att", [P, D], BF16)
        dout("dbg_z", [P, D], F32)
        dout("dbg_xn", [D, T], BF16)
        dout("dbg_vki", [65, NH * DH], F32)
        dout("dbg_ao", [T, D], BF16)
        dout("dbg_up", [DF, T], BF16)
        dout("dbg_scal", [P, 24], F32)
        dout("dbg_st", [P, 64], F32)
        dout("dbg_xntm", [P, D], BF16)
        dout("dbg_xs", [P, D], F32)
    dint("wuag_i", [D, WU_OT * P], BF16)
    dint("wuag_o", [n_cores * D, WU_OT * P], BF16)
    dint("wgag_i", [D, WU_OT * P], BF16)
    dint("wgag_o", [n_cores * D, WU_OT * P], BF16)
    dint("wdag_i", [DF, WD_OT * P], BF16)
    dint("wdag_o", [n_cores * DF, WD_OT * P], BF16)

    with ExitStack() as stack:
        en = stack.enter_context

        def sb(name, shape, dt):
            return en(nc.sbuf_tensor(name, shape, dt))

        ar1_cols = max(NC8 * T, 3 * DF, OCF * D + 2 * DF)
        ar3_cols = max(6 * T, 5 * SD + 3 * D, 4 * DF, 10 * D + 4224)
        AR1 = sb("AR1", [P, ar1_cols], BF16)
        AR3 = sb("AR3", [P, ar3_cols], BF16)
        AR4 = sb("AR4", [P, SD], BF16)
        itcols = max(T, NC8 * 512)
        iti = sb("iti", [P, itcols], I32)
        iti2 = sb("iti2", [P, itcols], I32)
        ident_b = sb("ident_b", [P, P], BF16)
        gq_b = sb("gq_b", [P, D], BF16)
        gk_b = sb("gk_b", [P, D], BF16)
        gout_b = sb("gout_b", [P, D], BF16)
        bo_b = sb("bo_b", [P, D], BF16)
        sq_f = [sb(f"sq_f{i}", [P, D], F32) for i in range(2)]
        sq_bv = [sq_f[i][:].bitcast(BF16) for i in range(2)]
        stats = sb("stats", [P, 64], F32)
        maxacc = sb("maxacc", [P, 8], F32)
        scal = sb("scal", [P, 24], F32)
        xn_tm = [sb(f"xn_tm{i}", [P, D], BF16) for i in range(2)]
        rc_f = [sb(f"rcbuf{i}", [P, 512], BF16) for i in range(2)]
        rs_f = [sb(f"rsbuf{i}", [P, 512], BF16) for i in range(2)]
        wk1_b = sb("wk1_b", [P, D], BF16)
        wk2_b = sb("wk2_b", [P, D], BF16)
        wk3_b = sb("wk3_b", [P, D], BF16)
        kf_b = [sb(f"kf_b{i}", [P, D], BF16) for i in range(2)]
        vp_b = [sb(f"vp_b{i}", [P, NH * 65], BF16) for i in range(2)]
        vkT_b = sb("vkT_b", [P, 8 * 65], BF16)

        wl_b = [sb(f"wl_b{i}", [P, NC8 * P], BF16) for i in range(2)]
        ffn_f = sb("ffn_f", [P, D], F32)
        vk_f = ffn_f[0:65, :]
        wps_f = sb("wps_f", [P, 96], F32)
        wpi_f = sb("wpi_f", [P, 96], F32)
        wag_b = sb("wag_b", [P, P], BF16)
        onesr_f = sb("onesr_f", [1, P], F32)
        rowst = [sb(f"rowst{i}", [1, P], F32) for i in range(2)]
        colst = sb("colst", [1, 2], F32)

        q_tm = AR3[:, 0:SD]
        k_tm = AR3[:, SD:2 * SD]
        v_tm = AR3[:, 2 * SD:3 * SD]
        ld_b = [AR3[:, 0:T], AR3[:, T:2 * T]]
        x_s = AR3[:, 3 * SD:5 * SD].bitcast(F32)
        bq_b = AR3[:, 5 * SD:5 * SD + 3 * D]
        x_f = [AR3[:, 0:2 * D].bitcast(F32),
               AR3[:, 2 * D:4 * D].bitcast(F32)]
        o_raw = [AR3[:, 4 * D:4 * D + DF],
                 AR3[:, 4 * D + DF:4 * D + 2 * DF]]
        iti_p = [iti[:, 0:DF], iti2[:, 0:DF]]
        att_p = [AR3[:, 0:D], AR3[:, D:2 * D]]
        att_fm_p = [AR3[:, 2 * D:3 * D], AR3[:, 3 * D:4 * D]]
        z_p = [AR3[:, 4 * D:5 * D], AR3[:, 5 * D:6 * D]]
        ffn_p = [AR3[:, 6 * D:8 * D].bitcast(F32),
                 AR3[:, 8 * D:10 * D].bitcast(F32)]
        resst_p = [AR3[:, 10 * D:10 * D + 2080].bitcast(F32),
                   AR3[:, 10 * D + 2080:10 * D + 4160].bitcast(F32)]
        den_p = [AR3[:, 10 * D + 4160:10 * D + 4192].bitcast(F32),
                 AR3[:, 10 * D + 4192:10 * D + 4224].bitcast(F32)]
        vk_bf = AR3[0:65, 0:1024]
        ld2_b = [AR3[:, 2 * T:3 * T], AR3[:, 3 * T:4 * T]]
        st_b = [AR3[:, 4 * T:5 * T], AR3[:, 5 * T:6 * T]]
        sig_f = AR3[:, 2 * T:4 * T].bitcast(F32)
        iti_v = [iti[:, 0:T], AR1[:, 0:2 * T].bitcast(I32)]
        iti2_v = [iti2[:, 0:T], AR1[:, 2 * T:4 * T].bitcast(I32)]
        sig_v = [sig_f[:, 0:T], AR1[:, 4 * T:6 * T].bitcast(F32)]
        wp_f = [AR3[:, 0:2 * DF].bitcast(F32),
                AR3[:, 2 * DF:4 * DF].bitcast(F32)]
        tmp_f = AR1[:, 0:2 * DF].bitcast(F32)
        wq_b = AR1[:, 2 * DF:3 * DF]
        wq_b2 = AR1[:, 3 * DF:4 * DF]
        big_a = AR1
        wdr_b = AR1[:, 0:OCF * D]
        o1_b = [AR1[:, OCF * D:OCF * D + DF],
                AR1[:, OCF * D + DF:OCF * D + 2 * DF]]
        wqr_b = [iti[:, 0:itcols // 2].bitcast(BF16),
                 iti[:, itcols // 2:itcols].bitcast(BF16)]
        wor_b = iti2.bitcast(BF16)
        ao_b = [wk1_b, wk2_b]
        ar_b = kf_b
        y_f = sq_f

        psA = [en(nc.psum_tensor(f"psA{i}", [P, 512], F32)) for i in range(4)]
        psT = [en(nc.psum_tensor(f"psT{i}", [P, P], BF16)) for i in range(2)]
        psV = [en(nc.psum_tensor(f"psV{i}", [65, 8 * DH], F32)) for i in range(2)]

        sem_names = [
            "c_prol", "c_prolv", "c_bnc0", "c_bnc1", "c_bncv", "c_bncst",
            "c_wp0", "c_wp1", "c_wpv", "c_wpt", "c_wpte", "c_agst", "c_ag",
            "c_x0", "c_x1", "c_lnv", "c_lna", "c_trp", "c_trpe", "c_xnsp",
            "c_qw0", "c_qw1", "c_qkvp", "c_qkve",
            "c_rt0", "c_rt1", "c_tokv", "c_toka", "c_vkmm", "c_qft", "c_qfte",
            "c_vkev", "c_ccst", "c_cc", "c_vkld", "c_vktr", "c_vktre",
            "c_resp", "c_rese", "c_attt", "c_atte", "c_wop", "c_woe", "c_wor",
            "c_aost",
            "c_xnr0", "c_xnr1", "c_x1c",
            "c_wl0", "c_wl1", "c_ffp", "c_ffe", "c_ffv", "c_ffst",
            "c_ld0", "c_ld1", "c_p3a", "c_p3v", "c_p3st",
            "c_ld20", "c_ld21", "c_p4a", "c_p4v", "c_p4st",
            "c_o1a", "c_o1v", "c_o1st", "c_ol0", "c_ol1", "c_wdr",
            "c_dwp", "c_dwe", "c_xr0", "c_xr1", "c_ar0", "c_ar1",
            "c_res2", "c_yst", "c_misc", "c_scal", "c_gmax", "c_ffst1", "c_p3st1", "c_p4st1", "c_o1st1", "c_yst1", "c_bncst1", "c_p3v1", "c_p4v1", "c_o1v1",
            "c_resd", "c_zcp", "c_aoz",
        ]
        sems = {n: en(nc.semaphore(n)) for n in sem_names}
        block = en(nc.Block())


        def stage_gmax(k, mcol, row):
            k.wait(GP, "c_lnv")
            k.wait(GP, "c_ffv")
            k.wait(GP, "c_p3v")
            k.wait(GP, "c_p4v")
            k.wait(GP, "c_p3v1")
            k.wait(GP, "c_p4v1")
            k.do(GP, lambda g, m_=mcol, r_=row: g.tensor_reduce(
                colst[0:1, r_:r_ + 1], maxacc[:, m_:m_ + 1], AX.C, ALU.max),
                inc="c_gmax")
            k.wait(VE, "c_gmax")
            k.do(VE, lambda v, r_=row: v.tensor_scalar(
                rowst[r_][:, :], onesr_f[:, :], colst[0:1, r_:r_ + 1],
                None, ALU.mult), inc="c_gmax")

        def rsqrt_refined(k, src, dst, n):
            k.do(VE, lambda v: v.reciprocal(stats[:, dst:dst + n],
                                            stats[:, src:src + n]),
                 inc="c_lnv", dr=True)
            k.wait(SC, "c_lnv")
            k.do(SC, lambda s: s.activation(stats[:, dst:dst + n],
                                            stats[:, dst:dst + n], AF.Sqrt),
                 inc="c_lna")
            k.wait(VE, "c_lna")
            k.do(VE, lambda v: v.tensor_tensor(
                stats[:, 56:56 + n], stats[:, dst:dst + n],
                stats[:, dst:dst + n], ALU.mult), dr=True)
            k.do(VE, lambda v: v.tensor_tensor(
                stats[:, 56:56 + n], stats[:, 56:56 + n],
                stats[:, src:src + n], ALU.mult), dr=True)
            k.do(VE, lambda v: v.tensor_scalar(
                stats[:, 56:56 + n], stats[:, 56:56 + n], -0.5, 1.5,
                ALU.mult, ALU.add), dr=True)
            k.do(VE, lambda v: v.tensor_tensor(
                stats[:, dst:dst + n], stats[:, dst:dst + n],
                stats[:, 56:56 + n], ALU.mult), inc="c_lnv", dr=True)

        def program(k: K):
            for (src, dst, cols) in [(ap["ident"], ident_b, P),
                                     (ap["gq_rep"], gq_b, D),
                                     (ap["gk_rep"], gk_b, D),
                                     (ap["gout_rep"], gout_b, D),
                                     (ap["bqkv_rep"], bq_b, 3 * D),
                                     (ap["bout_rep"], bo_b, D)]:
                k.wait(SY, "c_prolv")
                k.dma(SY, tmp_f[:, 0:cols], src, "c_prol")
                k.wait(VE, "c_prol")
                k.do(VE, lambda v, d_=dst, c_=cols: v.tensor_copy(
                    d_[:, 0:c_], tmp_f[:, 0:c_]), inc="c_prolv")
            k.do(VE, lambda v: v.memset(maxacc[:], 0.0))
            k.do(VE, lambda v: v.memset(onesr_f[:], 1.0), inc="c_prolv")

            for (orig, agi, I, nOT) in [
                    (ap["wu_slice"], ap["wuag_i"], D, WU_OT),
                    (ap["wg_slice"], ap["wgag_i"], D, WU_OT),
                    (ap["wd_slice"], ap["wdag_i"], DF, WD_OT)]:
                nblk = I // 32
                nich = I // P
                for ot in range(nOT):
                    sl = ot % 2
                    cw = "c_wp0" if sl == 0 else "c_wp1"
                    k.wait(SY, "c_wpv")
                    k.dma(SY, wp_f[sl][:, 0:I], orig[ot * P:(ot + 1) * P, :],
                          cw)
                    k.wait(VE, cw)
                    k.wait(VE, "c_prolv")
                    k.wait(VE, "c_agst")
                    k.do(VE, lambda v, s_=sl, I_=I, nb=nblk: v.tensor_reduce(
                        wps_f[:, 0:nb],
                        wp_f[s_][:, 0:I_].rearrange("p (a b) -> p a b", b=32),
                        AX.X, ALU.max, apply_absolute_value=True), dr=True)
                    k.do(VE, lambda v, nb=nblk: v.tensor_scalar(
                        wps_f[:, 0:nb], wps_f[:, 0:nb], WCLIP, 1.0 / WQMAX,
                        ALU.max, ALU.mult), dr=True)
                    k.do(VE, lambda v, nb=nblk: v.reciprocal(
                        wpi_f[:, 0:nb], wps_f[:, 0:nb]), dr=True)
                    for b in range(nblk):
                        k.do(VE, lambda v, s_=sl, b_=b: v.tensor_scalar(
                            tmp_f[:, 32 * b_:32 * (b_ + 1)],
                            wp_f[s_][:, 32 * b_:32 * (b_ + 1)],
                            wpi_f[:, b_:b_ + 1], None, ALU.mult),
                            dr=(b == nblk - 1))
                    k.do(VE, lambda v, I_=I: v.tensor_copy(
                        iti2[:, 0:I_], tmp_f[:, 0:I_]), dr=True)
                    for b in range(nblk):
                        k.do(VE, lambda v, b_=b: v.tensor_scalar(
                            wq_b[:, 32 * b_:32 * (b_ + 1)],
                            iti2[:, 32 * b_:32 * (b_ + 1)],
                            wps_f[:, b_:b_ + 1], None, ALU.mult),
                            dr=(b == nblk - 1))
                    k.do(VE, lambda v: v.nop(), inc="c_wpv")
                    for c in range(nich):
                        k.wait(TE, "c_wpv")
                        nev = k.count("c_wpte")
                        if nev >= 2:
                            k.wait(TE, "c_wpte", nev - 1)
                        k.do(TE, lambda t, c_=c: t.transpose(
                            psT[c_ % 2][:, :], wq_b[:, c_ * P:(c_ + 1) * P],
                            ident_b[:]), inc="c_wpt")
                        k.wait(VE, "c_wpt")
                        k.do(VE, lambda v, c_=c: v.tensor_copy(
                            wag_b[:, :], psT[c_ % 2][:, :]), inc="c_wpte")
                        k.wait(SC, "c_wpte")
                        k.do(SC, lambda s, c_=c, o_=ot, a_=agi: s.dma_start(
                            out=a_[c_ * P:(c_ + 1) * P, o_ * P:(o_ + 1) * P],
                            in_=wag_b[:, :]), inc="c_agst", incn=16)
                        k.wait(VE, "c_agst")
            k.wait(GP, "c_agst")
            for nm in ("wu", "wg", "wd"):
                k.do(GP, lambda g, n_=nm: g.collective_compute(
                    "AllGather", ALU.bypass, all_group,
                    ins=[ap[n_ + "ag_i"]], outs=[ap[n_ + "ag_o"]]),
                    inc="c_ag")

            for (src, dst, cols) in [(ap["wqkvT"], ap["wq_bf"], 3 * D),
                                     (ap["woutT"], ap["wo_bf"], D)]:
                for r in range(NC8):
                    sl = r % 2
                    cb = "c_bnc0" if sl == 0 else "c_bnc1"
                    k.wait(SY, "c_bncst")
                    k.wait(SY, "c_wpv")
                    k.dma(SY, wp_f[sl][:, 0:cols], src[r * P:(r + 1) * P, :],
                          cb)
                    k.wait(VE, cb)
                    k.wait(VE, "c_wpt")
                    k.wait(VE, "c_bncst")
                    k.do(VE, lambda v, s_=sl, c_=cols: v.tensor_copy(
                        wq_b[:, 0:c_], wp_f[s_][:, 0:c_]), inc="c_bncv")
                    k.wait(SC, "c_bncv")
                    k.do(SC, lambda s, r_=r, c_=cols, d_=dst: s.dma_start(
                        out=d_[r_ * P:(r_ + 1) * P, :], in_=wq_b[:, 0:c_]),
                        inc="c_bncst", incn=16)

            for eng in (SY, SC, VE, TE):
                k.wait(eng, "c_bncst")
                k.wait(eng, "c_bncst1")
                k.wait(eng, "c_agst")
            for s in range(NS):
                for j in range(stiles):
                    jt = s * stiles + j
                    sl = jt % 2
                    cx = "c_x0" if sl == 0 else "c_x1"
                    if s > 0 and j == 0:
                        k.wait(SY, "c_lnv")
                    k.dma(SY, x_s[:, j * D:(j + 1) * D],
                          ap["x"][jt * P:(jt + 1) * P, :], cx)
                    k.wait(VE, cx)
                    k.do(VE, lambda v, j_=j: v.reduce_sum(
                        stats[:, j_:j_ + 1], x_s[:, j_ * D:(j_ + 1) * D],
                        AX.X))
                    k.wait(SC, cx)
                    k.wait(SC, "c_lnv")
                    k.do(SC, lambda sc, s_=sl, j_=j: sc.activation(
                        sq_f[s_][:], x_s[:, j_ * D:(j_ + 1) * D], AF.Square),
                        inc="c_lna")
                    k.wait(VE, "c_lna")
                    k.do(VE, lambda v, s_=sl, j_=j: v.reduce_sum(
                        stats[:, 8 + j_:9 + j_], sq_f[s_][:], AX.X),
                        inc="c_lnv")
                n = stiles
                k.do(VE, lambda v: v.tensor_scalar(
                    stats[:, 16:16 + n], stats[:, 0:n], 1.0 / D, None,
                    ALU.mult), dr=True)
                k.do(VE, lambda v: v.tensor_scalar(
                    stats[:, 24:24 + n], stats[:, 8:8 + n], 1.0 / D, None,
                    ALU.mult), dr=True)
                k.do(VE, lambda v: v.tensor_tensor(
                    stats[:, 48:48 + n], stats[:, 16:16 + n],
                    stats[:, 16:16 + n], ALU.mult), dr=True)
                k.do(VE, lambda v: v.tensor_sub(
                    stats[:, 24:24 + n], stats[:, 24:24 + n],
                    stats[:, 48:48 + n]), dr=True)
                k.do(VE, lambda v: v.tensor_scalar(
                    stats[:, 24:24 + n], stats[:, 24:24 + n], LN_EPS, None,
                    ALU.add), inc="c_lnv", dr=True)
                rsqrt_refined(k, 24, 32, n)
                k.do(VE, lambda v: v.tensor_tensor(
                    stats[:, 40:40 + n], stats[:, 16:16 + n],
                    stats[:, 32:32 + n], ALU.mult), dr=True)
                k.do(VE, lambda v: v.tensor_scalar(
                    stats[:, 40:40 + n], stats[:, 40:40 + n], -1.0, None,
                    ALU.mult), inc="c_lnv", dr=True)

                if dbg and s == 0:
                    k.wait(SC, "c_lnv")
                    k.do(SC, lambda sc: sc.dma_start(
                        out=ap["dbg_st"], in_=stats[:, :]),
                        inc="c_misc", incn=16)
                    k.do(SC, lambda sc: sc.dma_start(
                        out=ap["dbg_xs"], in_=x_s[:, 0:D]),
                        inc="c_misc", incn=16)
                    k.wait(SC, "c_misc")
                for j in range(stiles):
                    jt = s * stiles + j
                    sl = jt % 2
                    k.wait(VE, "c_lnv")
                    if jt >= 2:
                        k.wait(VE, "c_trpe")
                    k.do(VE, lambda v, s_=sl, j_=j: v.tensor_scalar(
                        xn_tm[s_][:], x_s[:, j_ * D:(j_ + 1) * D],
                        stats[:, 32 + j_:33 + j_],
                        stats[:, 40 + j_:41 + j_],
                        ALU.mult, ALU.add), inc="c_lna")
                    if dbg and jt == 0:
                        k.wait(SC, "c_lna")
                        k.do(SC, lambda sc: sc.dma_start(
                            out=ap["dbg_xntm"], in_=xn_tm[0][:, :]),
                            inc="c_misc", incn=16)
                        k.wait(SC, "c_misc")
                    k.do(VE, lambda v, s_=sl: v.tensor_reduce(
                        stats[:, 48:49], xn_tm[s_][:], AX.X, ALU.max,
                        apply_absolute_value=True), dr=True)
                    k.do(VE, lambda v: v.tensor_tensor(
                        maxacc[:, 0:1], maxacc[:, 0:1], stats[:, 48:49],
                        ALU.max), inc="c_lnv", dr=True)
                    for c in range(NC8):
                        nev = k.count("c_trpe")
                        if nev >= 2:
                            k.wait(TE, "c_trpe", nev - 1)
                        k.wait(TE, "c_lna")
                        k.do(TE, lambda t, s_=sl, c_=c: t.transpose(
                            psT[c_ % 2][:, :],
                            xn_tm[s_][:, c_ * P:(c_ + 1) * P], ident_b[:]),
                            inc="c_trp")
                        k.wait(VE, "c_trp")
                        k.do(VE, lambda v, c_=c, j_=j: v.tensor_copy(
                            AR4[:, (c_ * stiles + j_) * P:
                                (c_ * stiles + j_ + 1) * P],
                            psT[c_ % 2][:, :]), inc="c_trpe")
                k.wait(SC, "c_trpe")
                for c in range(NC8):
                    k.do(SC, lambda sc, c_=c, s_=s: sc.dma_start(
                        out=ap["xn_sp"][c_ * P:(c_ + 1) * P,
                                        s_ * stiles * P:(s_ + 1) * stiles * P],
                        in_=AR4[:, c_ * stiles * P:(c_ + 1) * stiles * P]),
                        inc="c_xnsp", incn=16)

                for og in range(6):
                    sl = og % 2
                    cqw = "c_qw0" if sl == 0 else "c_qw1"
                    if s * 6 + og >= 2:
                        k.wait(SY, "c_qkvp", k.count("c_qkvp") - stiles)
                    for c in range(NC8):
                        k.dma(SY, wqr_b[sl][:, c * 512:(c + 1) * 512],
                              ap["wq_bf"][c * P:(c + 1) * P,
                                          og * 512:(og + 1) * 512], cqw)
                    k.wait(TE, cqw)
                    for j in range(stiles):
                        g = og * stiles + j
                        bank = psA[g % 2]
                        nev = k.count("c_qkve")
                        if nev >= 2:
                            k.wait(TE, "c_qkve", nev - 1)
                        for c in range(NC8):
                            k.do(TE, lambda t, b=bank, s_=sl, c_=c, j_=j:
                                 t.matmul(
                                     b[:, :],
                                     AR4[:, (c_ * stiles + j_) * P:
                                         (c_ * stiles + j_ + 1) * P],
                                     wqr_b[s_][:, c_ * 512:(c_ + 1) * 512],
                                     start=(c_ == 0), stop=(c_ == NC8 - 1)),
                                 inc=("c_qkvp" if c == NC8 - 1 else None))
                        dst = (q_tm, q_tm, k_tm, k_tm, v_tm, v_tm)[og]
                        off = (og % 2) * 512
                        k.wait(VE, "c_qkvp")
                        k.do(VE, lambda v, b=bank, d=dst, j_=j, o_=off,
                             og_=og: v.tensor_tensor(
                                 d[:, j_ * D + o_:j_ * D + o_ + 512], b[:, :],
                                 bq_b[:, og_ * 512:(og_ + 1) * 512], ALU.add),
                             inc="c_qkve")

                if dbg and s == 0:
                    k.wait(SC, "c_qkve")
                    for nm, src in [("dbg_q", q_tm), ("dbg_k", k_tm),
                                    ("dbg_v", v_tm)]:
                        k.do(SC, lambda sc, n_=nm, s2=src: sc.dma_start(
                            out=ap[n_], in_=s2), inc="c_misc", incn=16)
                    k.wait(SC, "c_misc")

                for j in range(stiles):
                    qs = q_tm[:, j * D:(j + 1) * D]
                    ks = k_tm[:, j * D:(j + 1) * D]
                    k.wait(SC, "c_qkve")
                    k.wait(SC, "c_lnv")
                    k.do(SC, lambda sc, q_=qs: sc.activation(
                        sq_f[0][:], q_, AF.Square))
                    k.do(SC, lambda sc, k_=ks: sc.activation(
                        sq_f[1][:], k_, AF.Square), inc="c_lna")
                    k.wait(VE, "c_lna")
                    k.do(VE, lambda v, j_=j: v.reduce_sum(
                        stats[:, 16 + 2 * j_:17 + 2 * j_], sq_f[0][:], AX.X))
                    k.do(VE, lambda v, j_=j: v.reduce_sum(
                        stats[:, 17 + 2 * j_:18 + 2 * j_], sq_f[1][:], AX.X),
                        inc="c_lnv", dr=True)
                k.do(VE, lambda v: v.tensor_scalar(
                    stats[:, 16:16 + 2 * stiles], stats[:, 16:16 + 2 * stiles],
                    1.0 / D, RMS_EPS, ALU.mult, ALU.add), inc="c_lnv",
                    dr=True)
                rsqrt_refined(k, 16, 24, 2 * stiles)

                for j in range(stiles):
                    jt = s * stiles + j
                    sl = jt % 2
                    crt = "c_rt0" if sl == 0 else "c_rt1"
                    if jt >= 2:
                        k.wait(SY, "c_toka", k.count("c_toka") - 1)
                    k.dma(SY, rc_f[sl][:], ap["rc"][jt * P:(jt + 1) * P, :],
                          crt)
                    k.dma(SY, rs_f[sl][:], ap["rs"][jt * P:(jt + 1) * P, :],
                          crt)
                    qs = q_tm[:, j * D:(j + 1) * D]
                    ks = k_tm[:, j * D:(j + 1) * D]
                    vs = v_tm[:, j * D:(j + 1) * D]
                    for (src_tm, gg, dst_hm, ridx) in [
                            (qs, gq_b, wk3_b, 24 + 2 * j),
                            (ks, gk_b, kf_b[sl], 25 + 2 * j)]:
                        k.wait(VE, crt)
                        k.wait(VE, "c_lna")
                        if dst_hm is wk3_b:
                            k.wait(SC, "c_qft")
                        else:
                            k.wait(SC, "c_vkmm")
                        k.do(VE, lambda v, s_=src_tm, g_=gg: v.tensor_tensor(
                            wk1_b[:], s_, g_[:, :], ALU.mult), dr=True)
                        k.do(VE, lambda v, s_=sl: v.tensor_tensor(
                            wk2_b[:, 0:512], wk1_b[:, 0:512], rc_f[s_][:],
                            ALU.mult))
                        k.do(VE, lambda v, s_=sl: v.tensor_tensor(
                            sq_bv[s_][:, 0:512], wk1_b[:, 512:1024],
                            rs_f[s_][:], ALU.mult), dr=True)
                        k.do(VE, lambda v, s_=sl: v.tensor_sub(
                            wk2_b[:, 0:512], wk2_b[:, 0:512],
                            sq_bv[s_][:, 0:512]))
                        k.do(VE, lambda v, s_=sl: v.tensor_tensor(
                            sq_bv[s_][:, 512:1024], wk1_b[:, 0:512],
                            rs_f[s_][:], ALU.mult), dr=True)
                        k.do(VE, lambda v, s_=sl: v.tensor_tensor(
                            wk2_b[:, 512:1024], wk1_b[:, 512:1024],
                            rc_f[s_][:], ALU.mult), dr=True)
                        k.do(VE, lambda v, s_=sl: v.tensor_add(
                            wk2_b[:, 512:1024], wk2_b[:, 512:1024],
                            sq_bv[s_][:, 512:1024]), inc="c_lnv", dr=True)
                        k.wait(SC, "c_lnv")
                        if dbg and jt == 0 and dst_hm is not wk3_b:
                            k.do(SC, lambda sc: sc.dma_start(
                                out=ap["dbg_wk1"], in_=wk1_b[:, :]),
                                inc="c_misc", incn=16)
                            k.do(SC, lambda sc: sc.dma_start(
                                out=ap["dbg_wk2"], in_=wk2_b[:, :]),
                                inc="c_misc", incn=16)
                            k.wait(SC, "c_misc")
                        k.do(SC, lambda sc, d_=dst_hm, r_=ridx: sc.activation(
                            A(d_, 0, [DH, NH], [1, 32]),
                            A(wk2_b, 0, [1, NH], [NH, 32]),
                            AF.Relu, scale=stats[:, r_:r_ + 1]))
                        k.do(SC, lambda sc, d_=dst_hm, r_=ridx: sc.activation(
                            A(d_, 32, [DH, NH], [1, 32]),
                            A(wk2_b, 512, [1, NH], [NH, 32]),
                            AF.Relu, scale=stats[:, r_:r_ + 1]), inc="c_lna")
                    k.do(VE, lambda v: v.nop(), inc="c_toka")
                    k.wait(VE, "c_qkve")
                    k.wait(VE, "c_vkmm")
                    k.do(VE, lambda v, s_=sl: v.memset(vp_b[s_][:], 1.0),
                         dr=True)
                    k.do(VE, lambda v, s_=sl, v_=vs: v.tensor_copy(
                        A(vp_b[s_], 0, [65, NH], [1, 32]),
                        A(v_.tensor, v_.offset, [1, NH], [NH, 32])))
                    k.do(VE, lambda v, s_=sl, v_=vs: v.tensor_copy(
                        A(vp_b[s_], 32, [65, NH], [1, 32]),
                        A(v_.tensor, v_.offset + 512, [1, NH], [NH, 32])), inc="c_tokv")
                    if dbg and jt == 0:
                        k.wait(SC, "c_tokv")
                        k.wait(SC, "c_lna")
                        k.do(SC, lambda sc, s_=sl: sc.dma_start(
                            out=ap["dbg_kf"], in_=kf_b[s_][:, :]),
                            inc="c_misc", incn=16)
                        k.do(SC, lambda sc, s_=sl: sc.dma_start(
                            out=ap["dbg_vp"], in_=vp_b[s_][:, :]),
                            inc="c_misc", incn=16)
                        k.do(SC, lambda sc: sc.dma_start(
                            out=ap["dbg_qf"], in_=wk3_b[:, :]),
                            inc="c_misc", incn=16)
                        k.wait(SC, "c_misc")
                    k.wait(TE, "c_tokv")
                    k.wait(TE, "c_lna")
                    for h in range(NH):
                        k.do(TE, lambda t, s_=sl, h_=h,
                             f=(jt == 0 and h % 8 == 0),
                             l=(jt == NT - 1 and h % 8 == 7): t.matmul(
                                 psV[h_ // 8][:, (h_ % 8) * DH:
                                              (h_ % 8 + 1) * DH],
                                 vp_b[s_][:, h_ * 65:(h_ + 1) * 65],
                                 kf_b[s_][:, h_ * DH:(h_ + 1) * DH],
                                 start=f, stop=l),
                             inc=("c_vkmm" if h == NH - 1 else None))
                    for c in range(NC8):
                        nev = k.count("c_qfte")
                        if nev >= 2:
                            k.wait(TE, "c_qfte", nev - 1)
                        k.do(TE, lambda t, c_=c: t.transpose(
                            psT[c_ % 2][:, :], wk3_b[:, c_ * P:(c_ + 1) * P],
                            ident_b[:]), inc="c_qft")
                        k.wait(VE, "c_qft")
                        k.do(VE, lambda v, c_=c, j_=jt: v.tensor_copy(
                            big_a[:, c_ * T + j_ * P:c_ * T + (j_ + 1) * P],
                            psT[c_ % 2][:, :]), inc="c_qfte")


            stage_gmax(k, 0, 0)
            k.wait(SC, "c_gmax")
            k.do(SC, lambda sc: sc.dma_start(
                out=ap["cc1_i"], in_=rowst[0][:, :]), inc="c_ccst", incn=16)
            k.wait(GP, "c_ag")
            k.wait(GP, "c_ccst")
            k.do(GP, lambda g: g.collective_compute(
                "AllReduce", ALU.max, all_group,
                ins=[ap["cc1_i"]], outs=[ap["cc1_o"]]), inc="c_cc")
            k.wait(SC, "c_vkmm")
            k.do(SC, lambda sc: sc.activation(vk_f[:, 0:8 * DH], psV[0][:, :],
                                              AF.Copy))
            k.do(SC, lambda sc: sc.activation(vk_f[:, 8 * DH:NH * DH],
                                              psV[1][:, :], AF.Copy),
                 inc="c_vkev")
            k.wait(SC, "c_vkev")
            k.do(SC, lambda sc: sc.dma_start(out=ap["vk_i"], in_=vk_f[:, :]),
                 inc="c_ccst", incn=16)
            k.wait(GP, "c_ccst")
            k.do(GP, lambda g: g.collective_compute(
                "AllReduce", ALU.add, pair_groups,
                ins=[ap["vk_i"]], outs=[ap["vk_o"]]), inc="c_cc")
            k.wait(SY, "c_cc")
            k.dma(SY, vk_f[:, :], ap["vk_o"], "c_vkld")
            k.dma(SY, scal[:, 0:1], bass.AP(ap["cc1_o"].tensor, 0, [[1, P], [1, 1]]), "c_scal", slow=True)
            k.wait(VE, "c_vkld")
            k.do(VE, lambda v: v.tensor_copy(vk_bf[:, :], vk_f[:, :]),
                 inc="c_vkld")
            for h in range(NH):
                k.wait(TE, "c_vkld")
                nev = k.count("c_vktre")
                if nev >= 2:
                    k.wait(TE, "c_vktre", nev - 1)
                k.do(TE, lambda t, h_=h: t.transpose(
                    psT[h_ % 2][0:DH, 0:65], vk_bf[:, h_ * DH:(h_ + 1) * DH],
                    ident_b[0:65, 0:65]), inc="c_vktr")
                k.wait(VE, "c_vktr")
                k.do(VE, lambda v, h_=h: v.tensor_copy(
                    vkT_b[DH * (h_ % 2):DH * (h_ % 2) + DH,
                          (h_ // 2) * 65:(h_ // 2) * 65 + 65],
                    psT[h_ % 2][0:DH, 0:65]), inc="c_vktre")
            k.wait(VE, "c_scal")
            k.do(VE, lambda v: v.tensor_scalar(
                scal[:, 1:2], scal[:, 0:1], ACLIP, None, ALU.max), dr=True)
            k.do(VE, lambda v: v.reciprocal(scal[:, 2:3], scal[:, 1:2]), dr=True)
            k.do(VE, lambda v: v.tensor_scalar(
                scal[:, 3:4], scal[:, 2:3], QMAX, None, ALU.mult), dr=True)
            k.do(VE, lambda v: v.tensor_scalar(
                scal[:, 4:5], scal[:, 1:2], 1.0 / QMAX, None, ALU.mult),
                inc="c_scal", dr=True)

            k.wait(SY, "c_bncst")
            k.wait(SY, "c_bncst1")
            k.wait(SY, "c_p3v")
            for c in range(NC8):
                k.dma(SY, wor_b[:, c * D:(c + 1) * D],
                      ap["wo_bf"][c * P:(c + 1) * P, :], "c_wor")
            def res_A(jt):
                sl = jt % 2
                for h in range(NH):
                    bank = psA[h % 2]
                    nev = k.count("c_rese")
                    if nev >= 2:
                        k.wait(TE, "c_rese", nev - 1)
                    if jt == 0 and h == 0:
                        k.wait(TE, "c_vktre")
                        k.wait(TE, "c_qfte")
                    k.do(TE, lambda t, h_=h, j_=jt, b=bank: t.matmul(
                        b[:, 0:65],
                        big_a[DH * (h_ % 2):DH * (h_ % 2) + DH,
                              (h_ // 2) * T + j_ * P:
                              (h_ // 2) * T + (j_ + 1) * P],
                        vkT_b[DH * (h_ % 2):DH * (h_ % 2) + DH,
                              (h_ // 2) * 65:(h_ // 2) * 65 + 65],
                        start=True, stop=True), inc="c_resp")
                    k.wait(SC, "c_resp")
                    if jt >= 2 and h == 0:
                        k.wait(SC, "c_resd", k.count("c_resd") - 1)
                    k.do(SC, lambda sc, b=bank, h_=h, s_=sl: sc.activation(
                        resst_p[s_][:, h_ * 65:(h_ + 1) * 65], b[:, 0:65],
                        AF.Copy))
                    k.do(SC, lambda sc, b=bank, h_=h, s_=sl: sc.activation(
                        den_p[s_][:, h_:h_ + 1], b[:, 64:65], AF.Copy),
                        inc="c_rese")
                k.wait(VE, "c_rese")
                k.do(VE, lambda v, s_=sl: v.tensor_scalar(
                    den_p[s_][:, 0:NH], den_p[s_][:, 0:NH], EPS_LIN, None,
                    ALU.add), dr=True)
                k.do(VE, lambda v, s_=sl: v.reciprocal(
                    den_p[s_][:, 0:NH], den_p[s_][:, 0:NH]), dr=True)
                if jt >= 2:
                    k.wait(VE, "c_attt", k.count("c_attt") - NC8)
                for h in range(NH):
                    k.do(VE, lambda v, h_=h, s_=sl: v.tensor_scalar(
                        att_p[s_][:, h_ * DH:(h_ + 1) * DH],
                        resst_p[s_][:, h_ * 65:h_ * 65 + DH],
                        den_p[s_][:, h_:h_ + 1], None, ALU.mult),
                        inc=("c_resd" if h == NH - 1 else None),
                        dr=(h == NH - 1))

            def res_B(jt):
                sl = jt % 2
                if dbg and jt == 0:
                    k.wait(SC, "c_resd")
                    k.do(SC, lambda sc: sc.dma_start(
                        out=ap["dbg_att"], in_=att_p[0][:, 0:D]),
                        inc="c_misc", incn=16)
                    k.wait(SC, "c_misc")
                for c in range(NC8):
                    nev = k.count("c_atte")
                    if nev >= 2:
                        k.wait(TE, "c_atte", nev - 1)
                    if c == 0:
                        k.wait(TE, "c_resd")
                    k.do(TE, lambda t, c_=c, s_=sl: t.transpose(
                        psT[c_ % 2][:, :], att_p[s_][:, c_ * P:(c_ + 1) * P],
                        ident_b[:]), inc="c_attt")
                    k.wait(SC, "c_attt")
                    k.do(SC, lambda sc, c_=c, s_=sl: sc.activation(
                        att_fm_p[s_][:, c_ * P:(c_ + 1) * P],
                        psT[c_ % 2][:, :], AF.Copy), inc="c_atte")
                if jt == 0:
                    k.wait(TE, "c_wor")
                for og in range(2):
                    bank = psA[2 + og]
                    nev = k.count("c_woe")
                    if nev >= 2:
                        k.wait(TE, "c_woe", nev - 1)
                    if og == 0:
                        k.wait(TE, "c_atte")
                    for c in range(NC8):
                        k.do(TE, lambda t, b=bank, c_=c, og_=og, s_=sl:
                             t.matmul(
                                 b[:, :], att_fm_p[s_][:, c_ * P:(c_ + 1) * P],
                                 wor_b[:, c_ * D + og_ * 512:
                                       c_ * D + (og_ + 1) * 512],
                                 start=(c_ == 0), stop=(c_ == NC8 - 1)),
                             inc=("c_wop" if c == NC8 - 1 else None))
                    k.wait(VE, "c_wop")
                    if og == 0 and jt >= 2:
                        k.wait(VE, "c_zcp", k.count("c_zcp") - 1)
                    k.do(VE, lambda v, b=bank, og_=og, s_=sl: v.tensor_tensor(
                        ffn_p[s_][:, og_ * 512:(og_ + 1) * 512], b[:, :],
                        bo_b[:, og_ * 512:(og_ + 1) * 512], ALU.add),
                        inc="c_woe")
                k.wait(SC, "c_woe")
                k.wait(SC, "c_lnv")
                if dbg and jt == 0:
                    k.do(SC, lambda sc: sc.dma_start(
                        out=ap["dbg_z"], in_=ffn_p[0][:, 0:D]),
                        inc="c_misc", incn=16)
                    k.wait(SC, "c_misc")
                k.do(SC, lambda sc, s_=sl: sc.activation(
                    sq_f[s_][:], ffn_p[s_][:, 0:D], AF.Square), inc="c_lna")
                k.wait(VE, "c_lna")
                k.do(VE, lambda v, s_=sl: v.reduce_sum(
                    stats[:, 49 + s_:50 + s_], sq_f[s_][:], AX.X), dr=True)
                k.do(VE, lambda v, s_=sl: v.tensor_scalar(
                    stats[:, 49 + s_:50 + s_], stats[:, 49 + s_:50 + s_],
                    1.0 / D, RMS_EPS, ALU.mult, ALU.add), inc="c_lnv",
                    dr=True)
                rsqrt_refined(k, 49 + sl, 53 + sl, 1)
                k.wait(SC, "c_lnv")
                k.do(SC, lambda sc, s_=sl: sc.activation(
                    z_p[s_][:, 0:D], ffn_p[s_][:, 0:D], AF.Copy,
                    scale=stats[:, 53 + s_:54 + s_]), inc="c_zcp")
                k.wait(VE, "c_zcp")
                if jt >= 2:
                    k.wait(VE, "c_aost", k.count("c_aost") - 16)
                k.do(VE, lambda v, s_=sl: v.tensor_tensor(
                    ao_b[s_][:], z_p[s_][:, 0:D], gout_b[:, :], ALU.mult),
                    inc="c_aoz")
                k.wait(SC, "c_aoz")
                k.do(SC, lambda sc, s_=sl, j_=jt: sc.dma_start(
                    out=ap["ao_sp"][j_ * P:(j_ + 1) * P, :], in_=ao_b[s_][:]),
                    inc="c_aost", incn=16)

            for it in range(NT + 1):
                if it < NT:
                    res_A(it)
                if it >= 1:
                    res_B(it - 1)

            for eng in (SY, SC, VE, TE):
                k.wait(eng, "c_aost")
                k.wait(eng, "c_qfte")
                k.wait(eng, "c_xnsp")
            k.wait(SC, "c_qkvp")
            k.wait(VE, "c_resp")
            k.wait(VE, "c_wop")
            for c in range(NC8):
                sl = c % 2
                cx = "c_xnr0" if sl == 0 else "c_xnr1"
                if c >= 2:
                    k.wait(SY, "c_x1c")
                k.dma(SY, ld_b[sl], ap["xn_sp"][c * P:(c + 1) * P, :], cx)
                k.wait(SC, cx)
                k.wait(SC, "c_scal")
                k.wait(SC, "c_x1c")
                k.do(SC, lambda sc, s_=sl: sc.activation(
                    iti[:, 0:T], ld_b[s_], AF.Copy, scale=scal[:, 3:4]),
                    inc="c_lna")
                k.wait(VE, "c_lna")
                k.do(VE, lambda v, c_=c: v.tensor_copy(
                    big_a[:, c_ * T:(c_ + 1) * T], iti[:, 0:T]), inc="c_x1c")
            for ti, (agname, spname, mcol) in enumerate(
                    [("wuag_o", "sp_up", 1), ("wgag_o", "sp_gt", 2)]):
                for oc in range(OCF):
                    g = ti * OCF + oc
                    sl = g % 2
                    cw = "c_wl0" if sl == 0 else "c_wl1"
                    rank = oc // WU_OT
                    within = oc % WU_OT
                    if g >= 2:
                        k.wait(SY, "c_ffp", k.count("c_ffp") - NO5)
                    if g == 0:
                        k.wait(SY, "c_ag")
                    W_ = WU_OT * P
                    k.do(SY, lambda e, s_=sl, r_=rank, w_=within, a_=agname:
                         e.dma_start(
                             out=wl_b[s_][:, :].rearrange(
                                 "p (c q) -> p c q", c=NC8),
                             in_=bass.AP(ap[a_].tensor,
                                         r_ * D * W_ + w_ * P,
                                         [[W_, P], [P * W_, NC8], [1, P]])),
                         inc=cw, incn=16)
                    k.wait(TE, cw)
                    k.wait(TE, "c_x1c")
                    for t5 in range(NO5):
                        bank = psA[k.count("c_ffp") % 2]
                        nev = k.count("c_ffe")
                        if nev >= 2:
                            k.wait(TE, "c_ffe", nev - 1)
                        for c in range(NC8):
                            k.do(TE, lambda t, b=bank, s_=sl, c_=c, t_=t5:
                                 t.matmul(
                                     b[:, :],
                                     wl_b[s_][:, c_ * P:(c_ + 1) * P],
                                     big_a[:, c_ * T + t_ * 512:
                                           c_ * T + (t_ + 1) * 512],
                                     start=(c_ == 0), stop=(c_ == NC8 - 1)),
                                 inc=("c_ffp" if c == NC8 - 1 else None))
                        k.wait(SC, "c_ffp")
                        if t5 == 0:
                            k.wait(SC, "c_ffst" if sl == 0 else "c_ffst1")
                        k.do(SC, lambda sc, b=bank, s_=sl, t_=t5:
                             sc.activation(
                                 A(st_b[s_].tensor, st_b[s_].offset + t_ * 512, [1, 512]),
                                 b[:, :], AF.Copy), inc="c_ffe")
                    k.wait(VE, "c_ffe")
                    k.do(VE, lambda v, s_=sl: v.tensor_reduce(
                        stats[:, 60:61], st_b[s_], AX.X, ALU.max,
                        apply_absolute_value=True), dr=True)
                    k.do(VE, lambda v, m_=mcol: v.tensor_tensor(
                        maxacc[:, m_:m_ + 1], maxacc[:, m_:m_ + 1],
                        stats[:, 60:61], ALU.max), inc="c_ffv", dr=True)
                    k.wait(SC, "c_ffv")
                    cst = "c_ffst" if sl == 0 else "c_ffst1"
                    k.do(SC, lambda sc, s_=sl, o_=oc, sp_=spname:
                         sc.dma_start(out=ap[sp_][o_ * P:(o_ + 1) * P, :],
                                      in_=st_b[s_]), inc=cst, incn=16)

            stage_gmax(k, 1, 0)
            stage_gmax(k, 2, 1)
            k.wait(SC, "c_gmax")
            k.do(SC, lambda sc: sc.dma_start(
                out=ap["cc3_i"][0:1, :], in_=rowst[0][:, :]),
                inc="c_ccst", incn=16)
            k.do(SC, lambda sc: sc.dma_start(
                out=ap["cc3_i"][1:2, :], in_=rowst[1][:, :]),
                inc="c_ccst", incn=16)
            k.wait(GP, "c_ccst")
            k.do(GP, lambda g: g.collective_compute(
                "AllReduce", ALU.max, all_group,
                ins=[ap["cc3_i"]], outs=[ap["cc3_o"]]), inc="c_cc")
            k.wait(SY, "c_cc")
            k.dma(SY, scal[:, 5:6], bass.AP(ap["cc3_o"].tensor, 0, [[1, P], [1, 1]]), "c_scal", slow=True)
            k.dma(SY, scal[:, 6:7], bass.AP(ap["cc3_o"].tensor, P, [[1, P], [1, 1]]), "c_scal", slow=True)
            k.wait(VE, "c_scal")
            k.do(VE, lambda v: v.tensor_scalar(
                scal[:, 5:7], scal[:, 5:7], ACLIP, None, ALU.max), dr=True)
            k.do(VE, lambda v: v.reciprocal(scal[:, 7:9], scal[:, 5:7]), dr=True)
            k.do(VE, lambda v: v.tensor_scalar(
                scal[:, 7:9], scal[:, 7:9], QMAX, None, ALU.mult), dr=True)
            k.do(VE, lambda v: v.tensor_tensor(
                scal[:, 9:10], scal[:, 4:5], scal[:, 6:7], ALU.mult), dr=True)
            k.do(VE, lambda v: v.tensor_scalar(
                scal[:, 9:10], scal[:, 9:10], 1.0 / QMAX, None, ALU.mult),
                inc="c_scal", dr=True)

            for eng in (SY, SC, VE):
                k.wait(eng, "c_ffst")
                k.wait(eng, "c_ffst1")
                k.wait(eng, "c_x1c")
            k.wait(VE, "c_ffp")
            for oc in range(OCF):
                sl = oc % 2
                cl = "c_ld0" if sl == 0 else "c_ld1"
                if oc >= 2:
                    k.wait(SY, "c_p3st" if sl == 0 else "c_p3st1")
                k.dma(SY, ld_b[sl], ap["sp_gt"][oc * P:(oc + 1) * P, :], cl)
                k.wait(SC, cl)
                k.wait(SC, "c_scal")
                k.wait(SC, "c_p3v" if sl == 0 else "c_p3v1")
                k.do(SC, lambda sc, s_=sl: sc.activation(
                    iti_v[s_], ld_b[s_], AF.Copy, scale=scal[:, 8:9]))
                k.do(SC, lambda sc, s_=sl: sc.activation(
                    sig_v[s_], iti_v[s_], AF.Sigmoid,
                    scale=scal[:, 9:10]), inc="c_p3a")
                k.wait(VE, "c_p3a")
                k.do(VE, lambda v, s_=sl: v.tensor_scalar(
                    iti2_v[s_], sig_v[s_], 65536.0, None, ALU.mult), dr=True)
                k.do(VE, lambda v, s_=sl: v.tensor_tensor(
                    st_b[s_], iti_v[s_], iti2_v[s_], ALU.mult))
                k.do(VE, lambda v, s_=sl: v.tensor_reduce(
                    stats[:, 61:62], st_b[s_], AX.X, ALU.max,
                    apply_absolute_value=True), dr=True)
                k.do(VE, lambda v: v.tensor_tensor(
                    maxacc[:, 3:4], maxacc[:, 3:4], stats[:, 61:62],
                    ALU.max), inc=("c_p3v" if sl == 0 else "c_p3v1"),
                    dr=True)
                k.wait(SC, "c_p3v" if sl == 0 else "c_p3v1")
                k.do(SC, lambda sc, s_=sl, o_=oc: sc.dma_start(
                    out=ap["sp_g2"][o_ * P:(o_ + 1) * P, :], in_=st_b[s_]),
                    inc=("c_p3st" if sl == 0 else "c_p3st1"), incn=16)

            k.wait(GP, "c_p3v1")
            stage_gmax(k, 3, 0)
            k.wait(SC, "c_gmax")
            k.do(SC, lambda sc: sc.dma_start(
                out=ap["cc4_i"], in_=rowst[0][:, :]), inc="c_ccst", incn=16)
            k.wait(GP, "c_ccst")
            k.do(GP, lambda g: g.collective_compute(
                "AllReduce", ALU.max, all_group,
                ins=[ap["cc4_i"]], outs=[ap["cc4_o"]]), inc="c_cc")
            k.wait(SY, "c_cc")
            k.dma(SY, scal[:, 10:11], bass.AP(ap["cc4_o"].tensor, 0, [[1, P], [1, 1]]), "c_scal", slow=True)
            k.wait(VE, "c_scal")
            k.do(VE, lambda v: v.tensor_scalar(
                scal[:, 10:11], scal[:, 10:11], ACLIP, None, ALU.max), dr=True)
            k.do(VE, lambda v: v.reciprocal(scal[:, 11:12], scal[:, 10:11]), dr=True)
            k.do(VE, lambda v: v.tensor_scalar(
                scal[:, 11:12], scal[:, 11:12], QMAX, None, ALU.mult),
                inc="c_scal", dr=True)

            k.wait(SY, "c_p3st")
            k.wait(SY, "c_p3st1")
            for oc in range(OCF):
                sl = oc % 2
                cl = "c_ld0" if sl == 0 else "c_ld1"
                cl2 = "c_ld20" if sl == 0 else "c_ld21"
                if oc >= 2:
                    k.wait(SY, "c_p4st" if sl == 0 else "c_p4st1")
                k.dma(SY, ld_b[sl], ap["sp_g2"][oc * P:(oc + 1) * P, :], cl)
                k.dma(SY, ld2_b[sl], ap["sp_up"][oc * P:(oc + 1) * P, :], cl2)
                k.wait(SC, cl)
                k.wait(SC, cl2)
                k.wait(SC, "c_scal")
                k.wait(SC, "c_p4v" if sl == 0 else "c_p4v1")
                k.do(SC, lambda sc, s_=sl: sc.activation(
                    iti_v[s_], ld_b[s_], AF.Copy, scale=scal[:, 11:12]))
                k.do(SC, lambda sc, s_=sl: sc.activation(
                    iti2_v[s_], ld2_b[s_], AF.Copy, scale=scal[:, 7:8]),
                    inc="c_p4a")
                k.wait(VE, "c_p4a")
                k.do(VE, lambda v, s_=sl: v.tensor_tensor(
                    st_b[s_], iti_v[s_], iti2_v[s_], ALU.mult))
                k.do(VE, lambda v, s_=sl: v.tensor_reduce(
                    stats[:, 61:62], st_b[s_], AX.X, ALU.max,
                    apply_absolute_value=True), dr=True)
                k.do(VE, lambda v: v.tensor_tensor(
                    maxacc[:, 4:5], maxacc[:, 4:5], stats[:, 61:62],
                    ALU.max), inc=("c_p4v" if sl == 0 else "c_p4v1"),
                    dr=True)
                k.wait(SC, "c_p4v" if sl == 0 else "c_p4v1")
                k.do(SC, lambda sc, s_=sl, o_=oc: sc.dma_start(
                    out=ap["sp_o"][o_ * P:(o_ + 1) * P, :], in_=st_b[s_]),
                    inc=("c_p4st" if sl == 0 else "c_p4st1"), incn=16)

            k.wait(GP, "c_p4v1")
            stage_gmax(k, 4, 0)
            k.wait(SC, "c_gmax")
            k.do(SC, lambda sc: sc.dma_start(
                out=ap["cc5_i"], in_=rowst[0][:, :]), inc="c_ccst", incn=16)
            k.wait(GP, "c_ccst")
            k.do(GP, lambda g: g.collective_compute(
                "AllReduce", ALU.max, all_group,
                ins=[ap["cc5_i"]], outs=[ap["cc5_o"]]), inc="c_cc")
            k.wait(SY, "c_cc")
            k.dma(SY, scal[:, 12:13], bass.AP(ap["cc5_o"].tensor, 0, [[1, P], [1, 1]]), "c_scal", slow=True)
            k.wait(VE, "c_scal")
            k.do(VE, lambda v: v.tensor_scalar(
                scal[:, 12:13], scal[:, 12:13], ACLIP, None, ALU.max), dr=True)
            k.do(VE, lambda v: v.reciprocal(scal[:, 13:14], scal[:, 12:13]), dr=True)
            k.do(VE, lambda v: v.tensor_scalar(
                scal[:, 13:14], scal[:, 13:14], QMAX, None, ALU.mult), dr=True)
            k.do(VE, lambda v: v.tensor_tensor(
                scal[:, 14:15], scal[:, 4:5], scal[:, 5:6], ALU.mult), dr=True)
            k.do(VE, lambda v: v.tensor_scalar(
                scal[:, 14:15], scal[:, 14:15], 1.0 / QMAX, None, ALU.mult), dr=True)
            k.do(VE, lambda v: v.tensor_tensor(
                scal[:, 15:16], scal[:, 9:10], scal[:, 10:11], ALU.mult), dr=True)
            k.do(VE, lambda v: v.tensor_scalar(
                scal[:, 15:16], scal[:, 15:16], 1.0 / (65536.0 * QMAX), None,
                ALU.mult), dr=True)
            k.do(VE, lambda v: v.tensor_tensor(
                scal[:, 16:17], scal[:, 14:15], scal[:, 15:16], ALU.mult), dr=True)
            k.do(VE, lambda v: v.tensor_tensor(
                scal[:, 16:17], scal[:, 16:17], scal[:, 12:13], ALU.mult), dr=True)
            k.do(VE, lambda v: v.tensor_scalar(
                scal[:, 16:17], scal[:, 16:17], 1.0 / QMAX, None, ALU.mult),
                inc="c_scal", dr=True)

            for eng in (SY, SC, VE, TE):
                k.wait(eng, "c_ffp")
            k.wait(SY, "c_p4st")
            k.wait(SY, "c_p4st1")
            W_ = WD_OT * P
            for r in range(n_cores):
                k.do(SY, lambda e, r_=r: e.dma_start(
                    out=A(wdr_b.tensor, wdr_b.offset + r_ * W_,
                          [D, OCF], [1, W_]),
                    in_=bass.AP(ap["wdag_o"].tensor, r_ * DF * W_,
                                [[W_, P], [P * W_, OCF], [1, W_]])),
                    inc="c_wdr", incn=16)
            for jt in range(NT):
                sl = jt % 2
                co = "c_ol0" if sl == 0 else "c_ol1"
                cxr = "c_xr0" if sl == 0 else "c_xr1"
                car = "c_ar0" if sl == 0 else "c_ar1"
                if jt >= 2:
                    k.wait(SY, "c_yst" if sl == 0 else "c_yst1")
                    k.wait(SY, "c_o1a", k.count("c_o1a") - 1)
                k.do(SY, lambda e, s_=sl, j_=jt: e.dma_start(
                    out=o_raw[s_].rearrange("p (a b) -> p a b", a=OCF),
                    in_=bass.AP(ap["sp_o"].tensor, j_ * P,
                                [[T, P], [P * T, OCF], [1, P]])),
                    inc=co, incn=16)
                k.dma(SY, x_f[sl][:], ap["x"][jt * P:(jt + 1) * P, :], cxr)
                k.dma(SY, ar_b[sl][:], ap["ao_sp"][jt * P:(jt + 1) * P, :],
                      car)
                k.wait(SC, co)
                k.wait(SC, "c_scal")
                if jt >= 2:
                    k.wait(SC, "c_o1v" if sl == 0 else "c_o1v1")
                k.do(SC, lambda sc, s_=sl: sc.activation(
                    iti_p[s_], o_raw[s_], AF.Copy, scale=scal[:, 13:14]),
                    inc="c_o1a")
                k.wait(VE, "c_o1a")
                if jt >= 2:
                    k.wait(VE, "c_dwp", k.count("c_dwp") - 2)
                k.do(VE, lambda v, s_=sl: v.tensor_copy(
                    o1_b[s_], iti_p[s_]),
                    inc=("c_o1v" if sl == 0 else "c_o1v1"))
                k.wait(TE, "c_o1v" if sl == 0 else "c_o1v1")
                k.wait(TE, "c_wdr")
                for og in range(2):
                    bank = psA[og]
                    nev = k.count("c_dwe")
                    if nev >= 2:
                        k.wait(TE, "c_dwe", nev - 1)
                    for fc in range(OCF):
                        k.do(TE, lambda t, b=bank, s_=sl, f_=fc, og_=og:
                             t.matmul(b[:, :],
                                      o1_b[s_][:, f_ * P:(f_ + 1) * P],
                                      wdr_b[:, f_ * D + og_ * 512:
                                            f_ * D + (og_ + 1) * 512],
                                      start=(f_ == 0), stop=(f_ == OCF - 1)),
                             inc=("c_dwp" if fc == OCF - 1 else None))
                    k.wait(SC, "c_dwp")
                    k.wait(SC, "c_scal")
                    k.do(SC, lambda sc, b=bank, og_=og: sc.activation(
                        ffn_f[:, og_ * 512:(og_ + 1) * 512], b[:, :],
                        AF.Copy, scale=scal[:, 16:17]), inc="c_dwe")
                k.wait(VE, "c_dwe")
                k.wait(VE, cxr)
                k.wait(VE, car)
                k.do(VE, lambda v, s_=sl: v.tensor_tensor(
                    y_f[s_][:, :], x_f[s_][:], ar_b[s_][:], ALU.add))
                k.do(VE, lambda v, s_=sl: v.tensor_tensor(
                    y_f[s_][:, :], y_f[s_][:, :], ffn_f[:, 0:D], ALU.add),
                    inc="c_res2")
                k.wait(SC, "c_res2")
                k.do(SC, lambda sc, s_=sl, j_=jt: sc.dma_start(
                    out=yout[j_ * P:(j_ + 1) * P, :], in_=y_f[s_][:, :]),
                    inc=("c_yst" if sl == 0 else "c_yst1"), incn=16)
            k.wait(SY, "c_yst")
            k.wait(SY, "c_yst1")
            if dbg:
                k.dma(SY, ap["dbg_xn"], ap["xn_sp"], "c_misc")
                k.dma(SY, ap["dbg_vki"], ap["vk_i"], "c_misc")
                k.dma(SY, ap["dbg_ao"], ap["ao_sp"], "c_misc")
                k.dma(SY, ap["dbg_up"], ap["sp_up"], "c_misc")
                k.do(SY, lambda e: e.dma_start(out=ap["dbg_scal"],
                                               in_=scal[:, :]),
                     inc="c_misc", incn=16)
                k.wait(SY, "c_misc")

        @block.sync
        def _(eng):
            program(K(SY, eng, sems))

        @block.scalar
        def _(eng):
            program(K(SC, eng, sems))

        @block.vector
        def _(eng):
            program(K(VE, eng, sems))

        @block.tensor
        def _(eng):
            program(K(TE, eng, sems))

        @block.gpsimd
        def _(eng):
            program(K(GP, eng, sems))

    return nc



def _sigma_perm():
    perm = np.empty(D, dtype=np.int64)
    for h in range(NH):
        for i in range(32):
            perm[i * 16 + h] = h * DH + 2 * i
            perm[512 + i * 16 + h] = h * DH + 2 * i + 1
    return perm


def _tau_perm():
    perm = np.empty(D, dtype=np.int64)
    for h in range(NH):
        for j2 in range(DH):
            orig = 2 * j2 if j2 < 32 else 2 * (j2 - 32) + 1
            perm[h * DH + j2] = h * DH + orig
    return perm


def _rope_tables(pos0, T):
    inv_freq = 1.0 / (10000.0 ** (np.arange(0, DH, 2, dtype=np.float64) / DH))
    t = np.arange(pos0, pos0 + T, dtype=np.float64)
    freqs = np.outer(t, inv_freq)
    cos_e = np.repeat(np.cos(freqs), 16, axis=1).astype(ml_dtypes.bfloat16)
    sin_e = np.repeat(np.sin(freqs), 16, axis=1).astype(ml_dtypes.bfloat16)
    return np.ascontiguousarray(cos_e), np.ascontiguousarray(sin_e)


_BUILD_CACHE = {}


def kernel(L0_attn_x, L0_attn_Wqkv, L0_attn_bqkv, L0_attn_gq, L0_attn_gk,
           L0_attn_Wout, L0_attn_bout, L0_attn_gout, L0_ffn_w_gate,
           L0_ffn_w_up, L0_ffn_w_down, _n_cores=8, _T=None, _stiles=4,
           _dbg=False):
    B, L, _ = L0_attn_x.shape
    n_cores = _n_cores
    T = _T if _T is not None else (B * L) // n_cores
    key = (T, n_cores, _stiles, _dbg)
    if key not in _BUILD_CACHE:
        _BUILD_CACHE[key] = build_nc(T, n_cores, _stiles, dbg=_dbg)
    nc = _BUILD_CACHE[key]

    sig = _sigma_perm()
    tau = _tau_perm()
    f32 = np.float32

    wq = np.ascontiguousarray(np.asarray(L0_attn_Wqkv, dtype=f32).T)
    wqp = np.empty_like(wq)
    bq = np.asarray(L0_attn_bqkv, dtype=f32)
    bqp = np.empty_like(bq)
    for g in range(3):
        wqp[:, g * D:(g + 1) * D] = wq[:, g * D:(g + 1) * D][:, sig]
        bqp[g * D:(g + 1) * D] = bq[g * D:(g + 1) * D][sig]
    wo = np.ascontiguousarray(np.asarray(L0_attn_Wout, dtype=f32).T)
    wop = np.ascontiguousarray(wo[tau, :])
    gq = np.asarray(L0_attn_gq, dtype=f32)[sig]
    gk = np.asarray(L0_attn_gk, dtype=f32)[sig]
    gout = np.asarray(L0_attn_gout, dtype=f32)
    bout = np.asarray(L0_attn_bout, dtype=f32)

    def rep(vec):
        return np.ascontiguousarray(np.broadcast_to(vec, (P, vec.shape[0])))

    wu_o = np.asarray(L0_ffn_w_up, dtype=f32)
    wg_o = np.asarray(L0_ffn_w_gate, dtype=f32)
    wd_o = np.asarray(L0_ffn_w_down, dtype=f32)
    ident = np.eye(P, dtype=f32)

    x = np.asarray(L0_attn_x, dtype=f32)
    wu_rows = DF // n_cores
    wd_rows = D // n_cores

    in_maps = []
    for c in range(n_cores):
        b, h = c // 2, c % 2
        if n_cores > 1:
            xs = np.ascontiguousarray(x[b, h * T:(h + 1) * T, :])
            pos0 = h * T
        else:
            xs = np.ascontiguousarray(x.reshape(-1, D)[:T])
            pos0 = 0
        cos_e, sin_e = _rope_tables(pos0, T)
        in_maps.append({
            "x": xs, "rc": cos_e, "rs": sin_e,
            "wqkvT": wqp, "bqkv_rep": rep(bqp),
            "gq_rep": rep(gq), "gk_rep": rep(gk),
            "woutT": wop, "bout_rep": rep(bout), "gout_rep": rep(gout),
            "wu_slice": np.ascontiguousarray(
                wu_o[c * wu_rows:(c + 1) * wu_rows]),
            "wg_slice": np.ascontiguousarray(
                wg_o[c * wu_rows:(c + 1) * wu_rows]),
            "wd_slice": np.ascontiguousarray(
                wd_o[c * wd_rows:(c + 1) * wd_rows]),
            "ident": ident,
        })

    res = run_bass_kernel_spmd(nc, in_maps, core_ids=list(range(n_cores)))
    outs = [res.results[c]["y"] for c in range(n_cores)]
    y = np.empty((B, L, D), dtype=np.float32)
    if n_cores > 1:
        for c in range(n_cores):
            b, h = c // 2, c % 2
            y[b, h * T:(h + 1) * T, :] = outs[c]
    else:
        y.reshape(-1, D)[:T] = outs[0]
    return y


# version 09
# speedup vs baseline: 1.0014x; 1.0014x over previous
import numpy as np
import ml_dtypes
from contextlib import ExitStack

import concourse.bass as bass
import concourse.mybir as mybir
from concourse.bass_utils import run_bass_kernel_spmd

F32 = mybir.dt.float32
BF16 = mybir.dt.bfloat16
I32 = mybir.dt.int32
AF = mybir.ActivationFunctionType
ALU = mybir.AluOpType
AX = mybir.AxisListType

D = 1024
DF = 3072
NH = 16
DH = 64
P = 128
NC8 = D // P
OCF = DF // P
RMS_EPS = 1.1920929e-07
LN_EPS = 1e-5
EPS_LIN = 1e-6
QMAX = 32767.0
ACLIP = QMAX * 1e-12
WQMAX = 7.0
WCLIP = WQMAX * 1e-12

SY, SC, VE, TE, GP = "sync", "scalar", "vector", "tensor", "gpsimd"


def A(t, off, *dims):
    w = t.shape[1]
    return bass.AP(t, off, [[w, P]] + [list(d) for d in dims])


class Ctr:
    __slots__ = ("n",)

    def __init__(self):
        self.n = 0


class K:
    def __init__(self, cur, eng, sems):
        self.cur = cur
        self.eng = eng
        self.sems = sems
        self.ctrs = {}

    def ctr(self, name):
        if name not in self.ctrs:
            self.ctrs[name] = Ctr()
        return self.ctrs[name]

    def do(self, engname, fn, inc=None, incn=1, dr=False):
        if self.cur == engname:
            ins = fn(self.eng)
            if inc is not None:
                ins.then_inc(self.sems[inc], incn)
            if dr:
                self.eng.drain()
        if inc is not None:
            self.ctr(inc).n += incn

    def dma(self, engname, out, in_, inc, slow=False):
        self.do(engname,
                lambda e: e.dma_start(out=out, in_=in_,
                                      allow_slow_non_contiguous=slow),
                inc=inc, incn=16)

    def wait(self, engname, ctrname, val=None):
        if self.cur == engname:
            self.eng.wait_ge(self.sems[ctrname],
                             self.ctr(ctrname).n if val is None else val)

    def count(self, ctrname):
        return self.ctr(ctrname).n


def build_nc(T, n_cores, stiles, dbg=False):
    NT = T // P
    NS = NT // stiles
    NO5 = T // 512
    SD = stiles * D
    assert NT % stiles == 0 and T % 512 == 0
    WU_OT = DF // P // n_cores
    WD_OT = D // P // n_cores
    assert DF % (P * n_cores) == 0 and D % (P * n_cores) == 0
    pair_groups = ([[2 * g, 2 * g + 1] for g in range(n_cores // 2)]
                   if n_cores > 1 else [[0]])
    all_group = [list(range(n_cores))]

    nc = bass.Bass("TRN2", target_bir_lowering=False,
                   detect_race_conditions=False)
    ap = {}

    def din(name, shape, dt=F32):
        ap[name] = nc.dram_tensor(name, shape, dt, kind="ExternalInput").ap()

    def dint(name, shape, dt):
        ap[name] = nc.dram_tensor(name, shape, dt, kind="Internal").ap()

    din("x", [T, D])
    din("rc", [T, 512], BF16)
    din("rs", [T, 512], BF16)
    din("wqkvT", [D, 3 * D])
    din("bqkv_rep", [P, 3 * D])
    din("gq_rep", [P, D])
    din("gk_rep", [P, D])
    din("woutT", [D, D])
    din("bout_rep", [P, D])
    din("gout_rep", [P, D])
    din("wu_slice", [WU_OT * P, D])
    din("wg_slice", [WU_OT * P, D])
    din("wd_slice", [WD_OT * P, DF])
    din("ident", [P, P])

    yout = nc.dram_tensor("y", [T, D], F32, kind="ExternalOutput").ap()

    dint("xn_sp", [D, T], BF16)
    dint("ao_sp", [T, D], BF16)
    dint("wq_bf", [D, 3 * D], BF16)
    dint("wo_bf", [D, D], BF16)
    dint("sp_up", [DF, T], BF16)
    dint("sp_gt", [DF, T], BF16)
    dint("sp_g2", [DF, T], BF16)
    dint("sp_o", [DF, T], BF16)
    for nm, w in [("cc1", 1), ("cc3", 2), ("cc4", 1), ("cc5", 1)]:
        dint(nm + "_i", [w, P], F32)
        dint(nm + "_o", [w, P], F32)
    dint("vk_i", [65, NH * DH], F32)
    dint("vk_o", [65, NH * DH], F32)
    if dbg:
        def dout(name, shape, dt):
            ap[name] = nc.dram_tensor(name, shape, dt,
                                      kind="ExternalOutput").ap()
        dout("dbg_q", [P, stiles * D], BF16)
        dout("dbg_k", [P, stiles * D], BF16)
        dout("dbg_v", [P, stiles * D], BF16)
        dout("dbg_kf", [P, D], BF16)
        dout("dbg_vp", [P, NH * 65], BF16)
        dout("dbg_qf", [P, D], BF16)
        dout("dbg_wk1", [P, D], BF16)
        dout("dbg_wk2", [P, D], BF16)
        dout("dbg_att", [P, D], BF16)
        dout("dbg_z", [P, D], F32)
        dout("dbg_xn", [D, T], BF16)
        dout("dbg_vki", [65, NH * DH], F32)
        dout("dbg_ao", [T, D], BF16)
        dout("dbg_up", [DF, T], BF16)
        dout("dbg_scal", [P, 24], F32)
        dout("dbg_st", [P, 64], F32)
        dout("dbg_xntm", [P, D], BF16)
        dout("dbg_xs", [P, D], F32)
    dint("wuag_i", [D, WU_OT * P], BF16)
    dint("wuag_o", [n_cores * D, WU_OT * P], BF16)
    dint("wgag_i", [D, WU_OT * P], BF16)
    dint("wgag_o", [n_cores * D, WU_OT * P], BF16)
    dint("wdag_i", [DF, WD_OT * P], BF16)
    dint("wdag_o", [n_cores * DF, WD_OT * P], BF16)

    with ExitStack() as stack:
        en = stack.enter_context

        def sb(name, shape, dt):
            return en(nc.sbuf_tensor(name, shape, dt))

        ar1_cols = max(NC8 * T, 3 * DF, OCF * D + 2 * DF)
        ar3_cols = max(6 * T, 5 * SD + 3 * D, 4 * DF, 10 * D + 4224)
        AR1 = sb("AR1", [P, ar1_cols], BF16)
        AR3 = sb("AR3", [P, ar3_cols], BF16)
        AR4 = sb("AR4", [P, SD], BF16)
        itcols = max(T, NC8 * 512)
        iti = sb("iti", [P, itcols], I32)
        iti2 = sb("iti2", [P, itcols], I32)
        ident_b = sb("ident_b", [P, P], BF16)
        gq_b = sb("gq_b", [P, D], BF16)
        gk_b = sb("gk_b", [P, D], BF16)
        gout_b = sb("gout_b", [P, D], BF16)
        bo_b = sb("bo_b", [P, D], BF16)
        sq_f = [sb(f"sq_f{i}", [P, D], F32) for i in range(2)]
        sq_bv = [sq_f[i][:].bitcast(BF16) for i in range(2)]
        stats = sb("stats", [P, 64], F32)
        maxacc = sb("maxacc", [P, 8], F32)
        scal = sb("scal", [P, 24], F32)
        xn_tm = [sb(f"xn_tm{i}", [P, D], BF16) for i in range(2)]
        rc_f = [sb(f"rcbuf{i}", [P, 512], BF16) for i in range(2)]
        rs_f = [sb(f"rsbuf{i}", [P, 512], BF16) for i in range(2)]
        wk1_b = sb("wk1_b", [P, D], BF16)
        wk2_b = sb("wk2_b", [P, D], BF16)
        wk3_b = sb("wk3_b", [P, D], BF16)
        kf_b = [sb(f"kf_b{i}", [P, D], BF16) for i in range(2)]
        vp_b = [sb(f"vp_b{i}", [P, NH * 65], BF16) for i in range(2)]
        vkT_b = sb("vkT_b", [P, 8 * 65], BF16)

        wl_b = [sb(f"wl_b{i}", [P, NC8 * P], BF16) for i in range(2)]
        ffn_f = sb("ffn_f", [P, D], F32)
        vk_f = ffn_f[0:65, :]
        wps_f = sb("wps_f", [P, 96], F32)
        wpi_f = sb("wpi_f", [P, 96], F32)
        wag_b = sb("wag_b", [P, P], BF16)
        onesr_f = sb("onesr_f", [1, P], F32)
        rowst = [sb(f"rowst{i}", [1, P], F32) for i in range(2)]
        colst = sb("colst", [1, 2], F32)

        q_tm = AR3[:, 0:SD]
        k_tm = AR3[:, SD:2 * SD]
        v_tm = AR3[:, 2 * SD:3 * SD]
        ld_b = [AR3[:, 0:T], AR3[:, T:2 * T]]
        x_s = AR3[:, 3 * SD:5 * SD].bitcast(F32)
        bq_b = AR3[:, 5 * SD:5 * SD + 3 * D]
        x_f = [AR3[:, 0:2 * D].bitcast(F32),
               AR3[:, 2 * D:4 * D].bitcast(F32)]
        o_raw = [AR3[:, 4 * D:4 * D + DF],
                 AR3[:, 4 * D + DF:4 * D + 2 * DF]]
        iti_p = [iti[:, 0:DF], iti2[:, 0:DF]]
        att_p = [AR3[:, 0:D], AR3[:, D:2 * D]]
        att_fm_p = [AR3[:, 2 * D:3 * D], AR3[:, 3 * D:4 * D]]
        z_p = [AR3[:, 4 * D:5 * D], AR3[:, 5 * D:6 * D]]
        ffn_p = [AR3[:, 6 * D:8 * D].bitcast(F32),
                 AR3[:, 8 * D:10 * D].bitcast(F32)]
        resst_p = [AR3[:, 10 * D:10 * D + 2080].bitcast(F32),
                   AR3[:, 10 * D + 2080:10 * D + 4160].bitcast(F32)]
        den_p = [AR3[:, 10 * D + 4160:10 * D + 4192].bitcast(F32),
                 AR3[:, 10 * D + 4192:10 * D + 4224].bitcast(F32)]
        vk_bf = AR3[0:65, 0:1024]
        ld2_b = [AR3[:, 2 * T:3 * T], AR3[:, 3 * T:4 * T]]
        st_b = [AR3[:, 4 * T:5 * T], AR3[:, 5 * T:6 * T]]
        sig_f = AR3[:, 2 * T:4 * T].bitcast(F32)
        iti_v = [iti[:, 0:T], AR1[:, 0:2 * T].bitcast(I32)]
        iti2_v = [iti2[:, 0:T], AR1[:, 2 * T:4 * T].bitcast(I32)]
        sig_v = [sig_f[:, 0:T], AR1[:, 4 * T:6 * T].bitcast(F32)]
        wp_f = [AR3[:, 0:2 * DF].bitcast(F32),
                AR3[:, 2 * DF:4 * DF].bitcast(F32)]
        tmp_f = AR1[:, 0:2 * DF].bitcast(F32)
        wq_b = AR1[:, 2 * DF:3 * DF]
        wq_b2 = AR1[:, 3 * DF:4 * DF]
        big_a = AR1
        wdr_b = AR1[:, 0:OCF * D]
        o1_b = [AR1[:, OCF * D:OCF * D + DF],
                AR1[:, OCF * D + DF:OCF * D + 2 * DF]]
        wqr_b = [iti[:, 0:itcols // 2].bitcast(BF16),
                 iti[:, itcols // 2:itcols].bitcast(BF16)]
        wor_b = iti2.bitcast(BF16)
        ao_b = [wk1_b, wk2_b]
        ar_b = kf_b
        y_f = sq_f

        psA = [en(nc.psum_tensor(f"psA{i}", [P, 512], F32)) for i in range(4)]
        psT = [en(nc.psum_tensor(f"psT{i}", [P, P], BF16)) for i in range(2)]
        psV = [en(nc.psum_tensor(f"psV{i}", [65, 8 * DH], F32)) for i in range(2)]

        sem_names = [
            "c_prol", "c_prolv", "c_bnc0", "c_bnc1", "c_bncv", "c_bncst",
            "c_wp0", "c_wp1", "c_wpv", "c_wpt", "c_wpte", "c_agst", "c_ag",
            "c_x0", "c_x1", "c_lnv", "c_lna", "c_trp", "c_trpe", "c_xnsp",
            "c_qw0", "c_qw1", "c_qkvp", "c_qkve",
            "c_rt0", "c_rt1", "c_tokv", "c_toka", "c_vkmm", "c_qft", "c_qfte",
            "c_vkev", "c_ccst", "c_cc", "c_vkld", "c_vktr", "c_vktre",
            "c_resp", "c_rese", "c_attt", "c_atte", "c_wop", "c_woe", "c_wor",
            "c_aost",
            "c_xnr0", "c_xnr1", "c_x1c",
            "c_wl0", "c_wl1", "c_ffp", "c_ffe", "c_ffv", "c_ffst",
            "c_ld0", "c_ld1", "c_p3a", "c_p3v", "c_p3st",
            "c_ld20", "c_ld21", "c_p4a", "c_p4v", "c_p4st",
            "c_o1a", "c_o1v", "c_o1st", "c_ol0", "c_ol1", "c_wdr",
            "c_dwp", "c_dwe", "c_xr0", "c_xr1", "c_ar0", "c_ar1",
            "c_res2", "c_yst", "c_misc", "c_scal", "c_gmax", "c_ffst1", "c_p3st1", "c_p4st1", "c_o1st1", "c_yst1", "c_bncst1", "c_p3v1", "c_p4v1", "c_o1v1",
            "c_resd", "c_zcp", "c_aoz",
        ]
        sems = {n: en(nc.semaphore(n)) for n in sem_names}
        block = en(nc.Block())


        def stage_gmax(k, mcol, row):
            k.wait(GP, "c_lnv")
            k.wait(GP, "c_ffv")
            k.wait(GP, "c_p3v")
            k.wait(GP, "c_p4v")
            k.wait(GP, "c_p3v1")
            k.wait(GP, "c_p4v1")
            k.do(GP, lambda g, m_=mcol, r_=row: g.tensor_reduce(
                colst[0:1, r_:r_ + 1], maxacc[:, m_:m_ + 1], AX.C, ALU.max),
                inc="c_gmax")
            k.wait(VE, "c_gmax")
            k.do(VE, lambda v, r_=row: v.tensor_scalar(
                rowst[r_][:, :], onesr_f[:, :], colst[0:1, r_:r_ + 1],
                None, ALU.mult), inc="c_gmax")

        def rsqrt_refined(k, src, dst, n):
            k.do(VE, lambda v: v.reciprocal(stats[:, dst:dst + n],
                                            stats[:, src:src + n]),
                 inc="c_lnv", dr=True)
            k.wait(SC, "c_lnv")
            k.do(SC, lambda s: s.activation(stats[:, dst:dst + n],
                                            stats[:, dst:dst + n], AF.Sqrt),
                 inc="c_lna")
            k.wait(VE, "c_lna")
            k.do(VE, lambda v: v.tensor_tensor(
                stats[:, 56:56 + n], stats[:, dst:dst + n],
                stats[:, dst:dst + n], ALU.mult), dr=True)
            k.do(VE, lambda v: v.tensor_tensor(
                stats[:, 56:56 + n], stats[:, 56:56 + n],
                stats[:, src:src + n], ALU.mult), dr=True)
            k.do(VE, lambda v: v.tensor_scalar(
                stats[:, 56:56 + n], stats[:, 56:56 + n], -0.5, 1.5,
                ALU.mult, ALU.add), dr=True)
            k.do(VE, lambda v: v.tensor_tensor(
                stats[:, dst:dst + n], stats[:, dst:dst + n],
                stats[:, 56:56 + n], ALU.mult), inc="c_lnv", dr=True)

        def program(k: K):
            for (src, dst, cols) in [(ap["ident"], ident_b, P),
                                     (ap["gq_rep"], gq_b, D),
                                     (ap["gk_rep"], gk_b, D),
                                     (ap["gout_rep"], gout_b, D),
                                     (ap["bqkv_rep"], bq_b, 3 * D),
                                     (ap["bout_rep"], bo_b, D)]:
                k.wait(SY, "c_prolv")
                k.dma(SY, tmp_f[:, 0:cols], src, "c_prol")
                k.wait(VE, "c_prol")
                k.do(VE, lambda v, d_=dst, c_=cols: v.tensor_copy(
                    d_[:, 0:c_], tmp_f[:, 0:c_]), inc="c_prolv")
            k.do(VE, lambda v: v.memset(maxacc[:], 0.0))
            k.do(VE, lambda v: v.memset(onesr_f[:], 1.0), inc="c_prolv")

            for (orig, agi, I, nOT) in [
                    (ap["wu_slice"], ap["wuag_i"], D, WU_OT),
                    (ap["wg_slice"], ap["wgag_i"], D, WU_OT),
                    (ap["wd_slice"], ap["wdag_i"], DF, WD_OT)]:
                nblk = I // 32
                nich = I // P
                for ot in range(nOT):
                    sl = ot % 2
                    cw = "c_wp0" if sl == 0 else "c_wp1"
                    k.wait(SY, "c_wpv")
                    k.dma(SY, wp_f[sl][:, 0:I], orig[ot * P:(ot + 1) * P, :],
                          cw)
                    k.wait(VE, cw)
                    k.wait(VE, "c_prolv")
                    k.wait(VE, "c_agst")
                    k.do(VE, lambda v, s_=sl, I_=I, nb=nblk: v.tensor_reduce(
                        wps_f[:, 0:nb],
                        wp_f[s_][:, 0:I_].rearrange("p (a b) -> p a b", b=32),
                        AX.X, ALU.max, apply_absolute_value=True), dr=True)
                    k.do(VE, lambda v, nb=nblk: v.tensor_scalar(
                        wps_f[:, 0:nb], wps_f[:, 0:nb], WCLIP, 1.0 / WQMAX,
                        ALU.max, ALU.mult), dr=True)
                    k.do(VE, lambda v, nb=nblk: v.reciprocal(
                        wpi_f[:, 0:nb], wps_f[:, 0:nb]), dr=True)
                    for b in range(nblk):
                        k.do(VE, lambda v, s_=sl, b_=b: v.tensor_scalar(
                            tmp_f[:, 32 * b_:32 * (b_ + 1)],
                            wp_f[s_][:, 32 * b_:32 * (b_ + 1)],
                            wpi_f[:, b_:b_ + 1], None, ALU.mult),
                            dr=(b == nblk - 1))
                    k.do(VE, lambda v, I_=I: v.tensor_copy(
                        iti2[:, 0:I_], tmp_f[:, 0:I_]), dr=True)
                    for b in range(nblk):
                        k.do(VE, lambda v, b_=b: v.tensor_scalar(
                            wq_b[:, 32 * b_:32 * (b_ + 1)],
                            iti2[:, 32 * b_:32 * (b_ + 1)],
                            wps_f[:, b_:b_ + 1], None, ALU.mult),
                            dr=(b == nblk - 1))
                    k.do(VE, lambda v: v.nop(), inc="c_wpv")
                    for c in range(nich):
                        k.wait(TE, "c_wpv")
                        nev = k.count("c_wpte")
                        if nev >= 2:
                            k.wait(TE, "c_wpte", nev - 1)
                        k.do(TE, lambda t, c_=c: t.transpose(
                            psT[c_ % 2][:, :], wq_b[:, c_ * P:(c_ + 1) * P],
                            ident_b[:]), inc="c_wpt")
                        k.wait(VE, "c_wpt")
                        k.do(VE, lambda v, c_=c: v.tensor_copy(
                            wag_b[:, :], psT[c_ % 2][:, :]), inc="c_wpte")
                        k.wait(SC, "c_wpte")
                        k.do(SC, lambda s, c_=c, o_=ot, a_=agi: s.dma_start(
                            out=a_[c_ * P:(c_ + 1) * P, o_ * P:(o_ + 1) * P],
                            in_=wag_b[:, :]), inc="c_agst", incn=16)
                        k.wait(VE, "c_agst")
            k.wait(GP, "c_agst")
            for nm in ("wu", "wg", "wd"):
                k.do(GP, lambda g, n_=nm: g.collective_compute(
                    "AllGather", ALU.bypass, all_group,
                    ins=[ap[n_ + "ag_i"]], outs=[ap[n_ + "ag_o"]]),
                    inc="c_ag")

            for (src, dst, cols) in [(ap["wqkvT"], ap["wq_bf"], 3 * D),
                                     (ap["woutT"], ap["wo_bf"], D)]:
                for r in range(NC8):
                    sl = r % 2
                    cb = "c_bnc0" if sl == 0 else "c_bnc1"
                    k.wait(SY, "c_bncst")
                    k.wait(SY, "c_wpv")
                    k.dma(SY, wp_f[sl][:, 0:cols], src[r * P:(r + 1) * P, :],
                          cb)
                    k.wait(VE, cb)
                    k.wait(VE, "c_wpt")
                    k.wait(VE, "c_bncst")
                    k.do(VE, lambda v, s_=sl, c_=cols: v.tensor_copy(
                        wq_b[:, 0:c_], wp_f[s_][:, 0:c_]), inc="c_bncv")
                    k.wait(SC, "c_bncv")
                    k.do(SC, lambda s, r_=r, c_=cols, d_=dst: s.dma_start(
                        out=d_[r_ * P:(r_ + 1) * P, :], in_=wq_b[:, 0:c_]),
                        inc="c_bncst", incn=16)

            for eng in (SY, SC, VE, TE):
                k.wait(eng, "c_bncst")
                k.wait(eng, "c_bncst1")
                k.wait(eng, "c_agst")
            for s in range(NS):
                for j in range(stiles):
                    jt = s * stiles + j
                    sl = jt % 2
                    cx = "c_x0" if sl == 0 else "c_x1"
                    if s > 0 and j == 0:
                        k.wait(SY, "c_lnv")
                    k.dma(SY, x_s[:, j * D:(j + 1) * D],
                          ap["x"][jt * P:(jt + 1) * P, :], cx)
                    k.wait(VE, cx)
                    k.do(VE, lambda v, j_=j: v.reduce_sum(
                        stats[:, j_:j_ + 1], x_s[:, j_ * D:(j_ + 1) * D],
                        AX.X))
                    k.wait(SC, cx)
                    k.wait(SC, "c_lnv")
                    k.do(SC, lambda sc, s_=sl, j_=j: sc.activation(
                        sq_f[s_][:], x_s[:, j_ * D:(j_ + 1) * D], AF.Square),
                        inc="c_lna")
                    k.wait(VE, "c_lna")
                    k.do(VE, lambda v, s_=sl, j_=j: v.reduce_sum(
                        stats[:, 8 + j_:9 + j_], sq_f[s_][:], AX.X),
                        inc="c_lnv")
                n = stiles
                k.do(VE, lambda v: v.tensor_scalar(
                    stats[:, 16:16 + n], stats[:, 0:n], 1.0 / D, None,
                    ALU.mult), dr=True)
                k.do(VE, lambda v: v.tensor_scalar(
                    stats[:, 24:24 + n], stats[:, 8:8 + n], 1.0 / D, None,
                    ALU.mult), dr=True)
                k.do(VE, lambda v: v.tensor_tensor(
                    stats[:, 48:48 + n], stats[:, 16:16 + n],
                    stats[:, 16:16 + n], ALU.mult), dr=True)
                k.do(VE, lambda v: v.tensor_sub(
                    stats[:, 24:24 + n], stats[:, 24:24 + n],
                    stats[:, 48:48 + n]), dr=True)
                k.do(VE, lambda v: v.tensor_scalar(
                    stats[:, 24:24 + n], stats[:, 24:24 + n], LN_EPS, None,
                    ALU.add), inc="c_lnv", dr=True)
                rsqrt_refined(k, 24, 32, n)
                k.do(VE, lambda v: v.tensor_tensor(
                    stats[:, 40:40 + n], stats[:, 16:16 + n],
                    stats[:, 32:32 + n], ALU.mult), dr=True)
                k.do(VE, lambda v: v.tensor_scalar(
                    stats[:, 40:40 + n], stats[:, 40:40 + n], -1.0, None,
                    ALU.mult), inc="c_lnv", dr=True)

                if dbg and s == 0:
                    k.wait(SC, "c_lnv")
                    k.do(SC, lambda sc: sc.dma_start(
                        out=ap["dbg_st"], in_=stats[:, :]),
                        inc="c_misc", incn=16)
                    k.do(SC, lambda sc: sc.dma_start(
                        out=ap["dbg_xs"], in_=x_s[:, 0:D]),
                        inc="c_misc", incn=16)
                    k.wait(SC, "c_misc")
                for j in range(stiles):
                    jt = s * stiles + j
                    sl = jt % 2
                    k.wait(VE, "c_lnv")
                    if jt >= 2:
                        k.wait(VE, "c_trpe")
                    k.do(VE, lambda v, s_=sl, j_=j: v.tensor_scalar(
                        xn_tm[s_][:], x_s[:, j_ * D:(j_ + 1) * D],
                        stats[:, 32 + j_:33 + j_],
                        stats[:, 40 + j_:41 + j_],
                        ALU.mult, ALU.add), inc="c_lna")
                    if dbg and jt == 0:
                        k.wait(SC, "c_lna")
                        k.do(SC, lambda sc: sc.dma_start(
                            out=ap["dbg_xntm"], in_=xn_tm[0][:, :]),
                            inc="c_misc", incn=16)
                        k.wait(SC, "c_misc")
                    k.do(VE, lambda v, s_=sl: v.tensor_reduce(
                        stats[:, 48:49], xn_tm[s_][:], AX.X, ALU.max,
                        apply_absolute_value=True), dr=True)
                    k.do(VE, lambda v: v.tensor_tensor(
                        maxacc[:, 0:1], maxacc[:, 0:1], stats[:, 48:49],
                        ALU.max), inc="c_lnv", dr=True)
                    for c in range(NC8):
                        nev = k.count("c_trpe")
                        if nev >= 2:
                            k.wait(TE, "c_trpe", nev - 1)
                        k.wait(TE, "c_lna")
                        k.do(TE, lambda t, s_=sl, c_=c: t.transpose(
                            psT[c_ % 2][:, :],
                            xn_tm[s_][:, c_ * P:(c_ + 1) * P], ident_b[:]),
                            inc="c_trp")
                        k.wait(VE, "c_trp")
                        k.do(VE, lambda v, c_=c, j_=j: v.tensor_copy(
                            AR4[:, (c_ * stiles + j_) * P:
                                (c_ * stiles + j_ + 1) * P],
                            psT[c_ % 2][:, :]), inc="c_trpe")
                k.wait(SC, "c_trpe")
                for c in range(NC8):
                    k.do(SC, lambda sc, c_=c, s_=s: sc.dma_start(
                        out=ap["xn_sp"][c_ * P:(c_ + 1) * P,
                                        s_ * stiles * P:(s_ + 1) * stiles * P],
                        in_=AR4[:, c_ * stiles * P:(c_ + 1) * stiles * P]),
                        inc="c_xnsp", incn=16)

                for og in range(6):
                    sl = og % 2
                    cqw = "c_qw0" if sl == 0 else "c_qw1"
                    if s * 6 + og >= 2:
                        k.wait(SY, "c_qkvp", k.count("c_qkvp") - stiles)
                    for c in range(NC8):
                        k.dma(SY, wqr_b[sl][:, c * 512:(c + 1) * 512],
                              ap["wq_bf"][c * P:(c + 1) * P,
                                          og * 512:(og + 1) * 512], cqw)
                    k.wait(TE, cqw)
                    for j in range(stiles):
                        g = og * stiles + j
                        bank = psA[g % 2]
                        nev = k.count("c_qkve")
                        if nev >= 2:
                            k.wait(TE, "c_qkve", nev - 1)
                        for c in range(NC8):
                            k.do(TE, lambda t, b=bank, s_=sl, c_=c, j_=j:
                                 t.matmul(
                                     b[:, :],
                                     AR4[:, (c_ * stiles + j_) * P:
                                         (c_ * stiles + j_ + 1) * P],
                                     wqr_b[s_][:, c_ * 512:(c_ + 1) * 512],
                                     start=(c_ == 0), stop=(c_ == NC8 - 1)),
                                 inc=("c_qkvp" if c == NC8 - 1 else None))
                        dst = (q_tm, q_tm, k_tm, k_tm, v_tm, v_tm)[og]
                        off = (og % 2) * 512
                        k.wait(VE, "c_qkvp")
                        k.do(VE, lambda v, b=bank, d=dst, j_=j, o_=off,
                             og_=og: v.tensor_tensor(
                                 d[:, j_ * D + o_:j_ * D + o_ + 512], b[:, :],
                                 bq_b[:, og_ * 512:(og_ + 1) * 512], ALU.add),
                             inc="c_qkve")

                if dbg and s == 0:
                    k.wait(SC, "c_qkve")
                    for nm, src in [("dbg_q", q_tm), ("dbg_k", k_tm),
                                    ("dbg_v", v_tm)]:
                        k.do(SC, lambda sc, n_=nm, s2=src: sc.dma_start(
                            out=ap[n_], in_=s2), inc="c_misc", incn=16)
                    k.wait(SC, "c_misc")

                for j in range(stiles):
                    qs = q_tm[:, j * D:(j + 1) * D]
                    ks = k_tm[:, j * D:(j + 1) * D]
                    k.wait(SC, "c_qkve")
                    k.wait(SC, "c_lnv")
                    k.do(SC, lambda sc, q_=qs: sc.activation(
                        sq_f[0][:], q_, AF.Square))
                    k.do(SC, lambda sc, k_=ks: sc.activation(
                        sq_f[1][:], k_, AF.Square), inc="c_lna")
                    k.wait(VE, "c_lna")
                    k.do(VE, lambda v, j_=j: v.reduce_sum(
                        stats[:, 16 + 2 * j_:17 + 2 * j_], sq_f[0][:], AX.X))
                    k.do(VE, lambda v, j_=j: v.reduce_sum(
                        stats[:, 17 + 2 * j_:18 + 2 * j_], sq_f[1][:], AX.X),
                        inc="c_lnv", dr=True)
                k.do(VE, lambda v: v.tensor_scalar(
                    stats[:, 16:16 + 2 * stiles], stats[:, 16:16 + 2 * stiles],
                    1.0 / D, RMS_EPS, ALU.mult, ALU.add), inc="c_lnv",
                    dr=True)
                rsqrt_refined(k, 16, 24, 2 * stiles)

                for j in range(stiles):
                    jt = s * stiles + j
                    sl = jt % 2
                    crt = "c_rt0" if sl == 0 else "c_rt1"
                    if jt >= 2:
                        k.wait(SY, "c_toka", k.count("c_toka") - 1)
                    k.dma(SY, rc_f[sl][:], ap["rc"][jt * P:(jt + 1) * P, :],
                          crt)
                    k.dma(SY, rs_f[sl][:], ap["rs"][jt * P:(jt + 1) * P, :],
                          crt)
                    qs = q_tm[:, j * D:(j + 1) * D]
                    ks = k_tm[:, j * D:(j + 1) * D]
                    vs = v_tm[:, j * D:(j + 1) * D]
                    for (src_tm, gg, dst_hm, ridx) in [
                            (qs, gq_b, wk3_b, 24 + 2 * j),
                            (ks, gk_b, kf_b[sl], 25 + 2 * j)]:
                        k.wait(VE, crt)
                        k.wait(VE, "c_lna")
                        if dst_hm is wk3_b:
                            k.wait(SC, "c_qft")
                        else:
                            k.wait(SC, "c_vkmm")
                        k.do(VE, lambda v, s_=src_tm, g_=gg: v.tensor_tensor(
                            wk1_b[:], s_, g_[:, :], ALU.mult), dr=True)
                        k.do(VE, lambda v, s_=sl: v.tensor_tensor(
                            wk2_b[:, 0:512], wk1_b[:, 0:512], rc_f[s_][:],
                            ALU.mult))
                        k.do(VE, lambda v, s_=sl: v.tensor_tensor(
                            sq_bv[s_][:, 0:512], wk1_b[:, 512:1024],
                            rs_f[s_][:], ALU.mult), dr=True)
                        k.do(VE, lambda v, s_=sl: v.tensor_sub(
                            wk2_b[:, 0:512], wk2_b[:, 0:512],
                            sq_bv[s_][:, 0:512]))
                        k.do(VE, lambda v, s_=sl: v.tensor_tensor(
                            sq_bv[s_][:, 512:1024], wk1_b[:, 0:512],
                            rs_f[s_][:], ALU.mult), dr=True)
                        k.do(VE, lambda v, s_=sl: v.tensor_tensor(
                            wk2_b[:, 512:1024], wk1_b[:, 512:1024],
                            rc_f[s_][:], ALU.mult), dr=True)
                        k.do(VE, lambda v, s_=sl: v.tensor_add(
                            wk2_b[:, 512:1024], wk2_b[:, 512:1024],
                            sq_bv[s_][:, 512:1024]), inc="c_lnv", dr=True)
                        k.wait(SC, "c_lnv")
                        if dbg and jt == 0 and dst_hm is not wk3_b:
                            k.do(SC, lambda sc: sc.dma_start(
                                out=ap["dbg_wk1"], in_=wk1_b[:, :]),
                                inc="c_misc", incn=16)
                            k.do(SC, lambda sc: sc.dma_start(
                                out=ap["dbg_wk2"], in_=wk2_b[:, :]),
                                inc="c_misc", incn=16)
                            k.wait(SC, "c_misc")
                        k.do(SC, lambda sc, d_=dst_hm, r_=ridx: sc.activation(
                            A(d_, 0, [DH, NH], [1, 32]),
                            A(wk2_b, 0, [1, NH], [NH, 32]),
                            AF.Relu, scale=stats[:, r_:r_ + 1]))
                        k.do(SC, lambda sc, d_=dst_hm, r_=ridx: sc.activation(
                            A(d_, 32, [DH, NH], [1, 32]),
                            A(wk2_b, 512, [1, NH], [NH, 32]),
                            AF.Relu, scale=stats[:, r_:r_ + 1]), inc="c_lna")
                    k.do(VE, lambda v: v.nop(), inc="c_toka")
                    k.wait(VE, "c_qkve")
                    k.wait(VE, "c_vkmm")
                    k.do(VE, lambda v, s_=sl: v.memset(vp_b[s_][:], 1.0),
                         dr=True)
                    k.do(VE, lambda v, s_=sl, v_=vs: v.tensor_copy(
                        A(vp_b[s_], 0, [65, NH], [1, 32]),
                        A(v_.tensor, v_.offset, [1, NH], [NH, 32])))
                    k.do(VE, lambda v, s_=sl, v_=vs: v.tensor_copy(
                        A(vp_b[s_], 32, [65, NH], [1, 32]),
                        A(v_.tensor, v_.offset + 512, [1, NH], [NH, 32])), inc="c_tokv")
                    if dbg and jt == 0:
                        k.wait(SC, "c_tokv")
                        k.wait(SC, "c_lna")
                        k.do(SC, lambda sc, s_=sl: sc.dma_start(
                            out=ap["dbg_kf"], in_=kf_b[s_][:, :]),
                            inc="c_misc", incn=16)
                        k.do(SC, lambda sc, s_=sl: sc.dma_start(
                            out=ap["dbg_vp"], in_=vp_b[s_][:, :]),
                            inc="c_misc", incn=16)
                        k.do(SC, lambda sc: sc.dma_start(
                            out=ap["dbg_qf"], in_=wk3_b[:, :]),
                            inc="c_misc", incn=16)
                        k.wait(SC, "c_misc")
                    k.wait(TE, "c_tokv")
                    k.wait(TE, "c_lna")
                    for h in range(NH):
                        k.do(TE, lambda t, s_=sl, h_=h,
                             f=(jt == 0 and h % 8 == 0),
                             l=(jt == NT - 1 and h % 8 == 7): t.matmul(
                                 psV[h_ // 8][:, (h_ % 8) * DH:
                                              (h_ % 8 + 1) * DH],
                                 vp_b[s_][:, h_ * 65:(h_ + 1) * 65],
                                 kf_b[s_][:, h_ * DH:(h_ + 1) * DH],
                                 start=f, stop=l),
                             inc=("c_vkmm" if h == NH - 1 else None))
                    for c in range(NC8):
                        nev = k.count("c_qfte")
                        if nev >= 2:
                            k.wait(TE, "c_qfte", nev - 1)
                        k.do(TE, lambda t, c_=c: t.transpose(
                            psT[c_ % 2][:, :], wk3_b[:, c_ * P:(c_ + 1) * P],
                            ident_b[:]), inc="c_qft")
                        k.wait(VE, "c_qft")
                        k.do(VE, lambda v, c_=c, j_=jt: v.tensor_copy(
                            big_a[:, c_ * T + j_ * P:c_ * T + (j_ + 1) * P],
                            psT[c_ % 2][:, :]), inc="c_qfte")


            stage_gmax(k, 0, 0)
            k.wait(SC, "c_gmax")
            k.do(SC, lambda sc: sc.dma_start(
                out=ap["cc1_i"], in_=rowst[0][:, :]), inc="c_ccst", incn=16)
            k.wait(GP, "c_ag")
            k.wait(GP, "c_ccst")
            k.do(GP, lambda g: g.collective_compute(
                "AllReduce", ALU.max, all_group,
                ins=[ap["cc1_i"]], outs=[ap["cc1_o"]]), inc="c_cc")
            k.wait(SC, "c_vkmm")
            k.do(SC, lambda sc: sc.activation(vk_f[:, 0:8 * DH], psV[0][:, :],
                                              AF.Copy))
            k.do(SC, lambda sc: sc.activation(vk_f[:, 8 * DH:NH * DH],
                                              psV[1][:, :], AF.Copy),
                 inc="c_vkev")
            k.wait(SC, "c_vkev")
            k.do(SC, lambda sc: sc.dma_start(out=ap["vk_i"], in_=vk_f[:, :]),
                 inc="c_ccst", incn=16)
            k.wait(GP, "c_ccst")
            k.do(GP, lambda g: g.collective_compute(
                "AllReduce", ALU.add, pair_groups,
                ins=[ap["vk_i"]], outs=[ap["vk_o"]]), inc="c_cc")
            k.wait(SY, "c_cc")
            k.dma(SY, vk_f[:, :], ap["vk_o"], "c_vkld")
            k.dma(SY, scal[:, 0:1], bass.AP(ap["cc1_o"].tensor, 0, [[1, P], [1, 1]]), "c_scal", slow=True)
            k.wait(VE, "c_vkld")
            k.do(VE, lambda v: v.tensor_copy(vk_bf[:, :], vk_f[:, :]),
                 inc="c_vkld")
            for h in range(NH):
                k.wait(TE, "c_vkld")
                nev = k.count("c_vktre")
                if nev >= 2:
                    k.wait(TE, "c_vktre", nev - 1)
                k.do(TE, lambda t, h_=h: t.transpose(
                    psT[h_ % 2][0:DH, 0:65], vk_bf[:, h_ * DH:(h_ + 1) * DH],
                    ident_b[0:65, 0:65]), inc="c_vktr")
                k.wait(VE, "c_vktr")
                k.do(VE, lambda v, h_=h: v.tensor_copy(
                    vkT_b[DH * (h_ % 2):DH * (h_ % 2) + DH,
                          (h_ // 2) * 65:(h_ // 2) * 65 + 65],
                    psT[h_ % 2][0:DH, 0:65]), inc="c_vktre")
            k.wait(VE, "c_scal")
            k.do(VE, lambda v: v.tensor_scalar(
                scal[:, 1:2], scal[:, 0:1], ACLIP, None, ALU.max), dr=True)
            k.do(VE, lambda v: v.reciprocal(scal[:, 2:3], scal[:, 1:2]), dr=True)
            k.do(VE, lambda v: v.tensor_scalar(
                scal[:, 3:4], scal[:, 2:3], QMAX, None, ALU.mult), dr=True)
            k.do(VE, lambda v: v.tensor_scalar(
                scal[:, 4:5], scal[:, 1:2], 1.0 / QMAX, None, ALU.mult),
                inc="c_scal", dr=True)

            k.wait(SY, "c_bncst")
            k.wait(SY, "c_bncst1")
            k.wait(SY, "c_p3v")
            for c in range(NC8):
                k.dma(SY, wor_b[:, c * D:(c + 1) * D],
                      ap["wo_bf"][c * P:(c + 1) * P, :], "c_wor")
            def res_A(jt):
                sl = jt % 2
                for h in range(NH):
                    bank = psA[h % 2]
                    nev = k.count("c_rese")
                    if nev >= 2:
                        k.wait(TE, "c_rese", nev - 1)
                    if jt == 0 and h == 0:
                        k.wait(TE, "c_vktre")
                        k.wait(TE, "c_qfte")
                    k.do(TE, lambda t, h_=h, j_=jt, b=bank: t.matmul(
                        b[:, 0:65],
                        big_a[DH * (h_ % 2):DH * (h_ % 2) + DH,
                              (h_ // 2) * T + j_ * P:
                              (h_ // 2) * T + (j_ + 1) * P],
                        vkT_b[DH * (h_ % 2):DH * (h_ % 2) + DH,
                              (h_ // 2) * 65:(h_ // 2) * 65 + 65],
                        start=True, stop=True), inc="c_resp")
                    k.wait(SC, "c_resp")
                    if jt >= 2 and h == 0:
                        k.wait(SC, "c_resd", k.count("c_resd") - 1)
                    k.do(SC, lambda sc, b=bank, h_=h, s_=sl: sc.activation(
                        resst_p[s_][:, h_ * 65:(h_ + 1) * 65], b[:, 0:65],
                        AF.Copy))
                    k.do(SC, lambda sc, b=bank, h_=h, s_=sl: sc.activation(
                        den_p[s_][:, h_:h_ + 1], b[:, 64:65], AF.Copy),
                        inc="c_rese")
                k.wait(VE, "c_rese")
                k.do(VE, lambda v, s_=sl: v.tensor_scalar(
                    den_p[s_][:, 0:NH], den_p[s_][:, 0:NH], EPS_LIN, None,
                    ALU.add), dr=True)
                k.do(VE, lambda v, s_=sl: v.reciprocal(
                    den_p[s_][:, 0:NH], den_p[s_][:, 0:NH]), dr=True)
                if jt >= 2:
                    k.wait(VE, "c_attt")
                for h in range(NH):
                    k.do(VE, lambda v, h_=h, s_=sl: v.tensor_scalar(
                        att_p[s_][:, h_ * DH:(h_ + 1) * DH],
                        resst_p[s_][:, h_ * 65:h_ * 65 + DH],
                        den_p[s_][:, h_:h_ + 1], None, ALU.mult),
                        inc=("c_resd" if h == NH - 1 else None),
                        dr=(h == NH - 1))

            def res_B(jt):
                sl = jt % 2
                if dbg and jt == 0:
                    k.wait(SC, "c_resd")
                    k.do(SC, lambda sc: sc.dma_start(
                        out=ap["dbg_att"], in_=att_p[0][:, 0:D]),
                        inc="c_misc", incn=16)
                    k.wait(SC, "c_misc")
                for c in range(NC8):
                    nev = k.count("c_atte")
                    if nev >= 2:
                        k.wait(TE, "c_atte", nev - 1)
                    if c == 0:
                        nrd = k.count("c_resd")
                        k.wait(TE, "c_resd", nrd - 1 if jt < NT - 1 else nrd)
                    k.do(TE, lambda t, c_=c, s_=sl: t.transpose(
                        psT[c_ % 2][:, :], att_p[s_][:, c_ * P:(c_ + 1) * P],
                        ident_b[:]), inc="c_attt")
                    k.wait(SC, "c_attt")
                    k.do(SC, lambda sc, c_=c, s_=sl: sc.activation(
                        att_fm_p[s_][:, c_ * P:(c_ + 1) * P],
                        psT[c_ % 2][:, :], AF.Copy), inc="c_atte")
                if jt == 0:
                    k.wait(TE, "c_wor")
                for og in range(2):
                    bank = psA[2 + og]
                    nev = k.count("c_woe")
                    if nev >= 2:
                        k.wait(TE, "c_woe", nev - 1)
                    if og == 0:
                        k.wait(TE, "c_atte")
                    for c in range(NC8):
                        k.do(TE, lambda t, b=bank, c_=c, og_=og, s_=sl:
                             t.matmul(
                                 b[:, :], att_fm_p[s_][:, c_ * P:(c_ + 1) * P],
                                 wor_b[:, c_ * D + og_ * 512:
                                       c_ * D + (og_ + 1) * 512],
                                 start=(c_ == 0), stop=(c_ == NC8 - 1)),
                             inc=("c_wop" if c == NC8 - 1 else None))
                    k.wait(VE, "c_wop")
                    k.do(VE, lambda v, b=bank, og_=og, s_=sl: v.tensor_tensor(
                        ffn_p[s_][:, og_ * 512:(og_ + 1) * 512], b[:, :],
                        bo_b[:, og_ * 512:(og_ + 1) * 512], ALU.add),
                        inc="c_woe", dr=(og == 1))
                if dbg and jt == 0:
                    k.wait(SC, "c_woe")
                    k.do(SC, lambda sc: sc.dma_start(
                        out=ap["dbg_z"], in_=ffn_p[0][:, 0:D]),
                        inc="c_misc", incn=16)
                    k.wait(SC, "c_misc")
                k.do(VE, lambda v, s_=sl: v.tensor_tensor(
                    sq_f[s_][:], ffn_p[s_][:, 0:D], ffn_p[s_][:, 0:D],
                    ALU.mult))
                k.do(VE, lambda v, s_=sl: v.tensor_tensor(
                    z_p[s_][:, 0:D], ffn_p[s_][:, 0:D], gout_b[:, :],
                    ALU.mult), dr=True)
                k.do(VE, lambda v, s_=sl: v.reduce_sum(
                    stats[:, 49 + s_:50 + s_], sq_f[s_][:], AX.X), dr=True)
                k.do(VE, lambda v, s_=sl: v.tensor_scalar(
                    stats[:, 49 + s_:50 + s_], stats[:, 49 + s_:50 + s_],
                    1.0 / D, RMS_EPS, ALU.mult, ALU.add), inc="c_lnv",
                    dr=True)
                rsqrt_refined(k, 49 + sl, 53 + sl, 1)
                if jt >= 2:
                    k.wait(VE, "c_aost", k.count("c_aost") - 16)
                k.do(VE, lambda v, s_=sl: v.tensor_scalar(
                    ao_b[s_][:], z_p[s_][:, 0:D],
                    stats[:, 53 + s_:54 + s_], None, ALU.mult),
                    inc="c_aoz")
                k.wait(SC, "c_aoz")
                k.do(SC, lambda sc, s_=sl, j_=jt: sc.dma_start(
                    out=ap["ao_sp"][j_ * P:(j_ + 1) * P, :], in_=ao_b[s_][:]),
                    inc="c_aost", incn=16)

            for it in range(NT + 1):
                if it < NT:
                    res_A(it)
                if it >= 1:
                    res_B(it - 1)

            for eng in (SY, SC, VE, TE):
                k.wait(eng, "c_aost")
                k.wait(eng, "c_qfte")
                k.wait(eng, "c_xnsp")
            k.wait(SC, "c_qkvp")
            k.wait(VE, "c_resp")
            k.wait(VE, "c_wop")
            for c in range(NC8):
                sl = c % 2
                cx = "c_xnr0" if sl == 0 else "c_xnr1"
                if c >= 2:
                    k.wait(SY, "c_x1c")
                k.dma(SY, ld_b[sl], ap["xn_sp"][c * P:(c + 1) * P, :], cx)
                k.wait(SC, cx)
                k.wait(SC, "c_scal")
                k.wait(SC, "c_x1c")
                k.do(SC, lambda sc, s_=sl: sc.activation(
                    iti[:, 0:T], ld_b[s_], AF.Copy, scale=scal[:, 3:4]),
                    inc="c_lna")
                k.wait(VE, "c_lna")
                k.do(VE, lambda v, c_=c: v.tensor_copy(
                    big_a[:, c_ * T:(c_ + 1) * T], iti[:, 0:T]), inc="c_x1c")
            for ti, (agname, spname, mcol) in enumerate(
                    [("wuag_o", "sp_up", 1), ("wgag_o", "sp_gt", 2)]):
                for oc in range(OCF):
                    g = ti * OCF + oc
                    sl = g % 2
                    cw = "c_wl0" if sl == 0 else "c_wl1"
                    rank = oc // WU_OT
                    within = oc % WU_OT
                    if g >= 2:
                        k.wait(SY, "c_ffp", k.count("c_ffp") - NO5)
                    if g == 0:
                        k.wait(SY, "c_ag")
                    W_ = WU_OT * P
                    k.do(SY, lambda e, s_=sl, r_=rank, w_=within, a_=agname:
                         e.dma_start(
                             out=wl_b[s_][:, :].rearrange(
                                 "p (c q) -> p c q", c=NC8),
                             in_=bass.AP(ap[a_].tensor,
                                         r_ * D * W_ + w_ * P,
                                         [[W_, P], [P * W_, NC8], [1, P]])),
                         inc=cw, incn=16)
                    k.wait(TE, cw)
                    k.wait(TE, "c_x1c")
                    for t5 in range(NO5):
                        bank = psA[k.count("c_ffp") % 2]
                        nev = k.count("c_ffe")
                        if nev >= 2:
                            k.wait(TE, "c_ffe", nev - 1)
                        for c in range(NC8):
                            k.do(TE, lambda t, b=bank, s_=sl, c_=c, t_=t5:
                                 t.matmul(
                                     b[:, :],
                                     wl_b[s_][:, c_ * P:(c_ + 1) * P],
                                     big_a[:, c_ * T + t_ * 512:
                                           c_ * T + (t_ + 1) * 512],
                                     start=(c_ == 0), stop=(c_ == NC8 - 1)),
                                 inc=("c_ffp" if c == NC8 - 1 else None))
                        k.wait(SC, "c_ffp")
                        if t5 == 0:
                            k.wait(SC, "c_ffst" if sl == 0 else "c_ffst1")
                        k.do(SC, lambda sc, b=bank, s_=sl, t_=t5:
                             sc.activation(
                                 A(st_b[s_].tensor, st_b[s_].offset + t_ * 512, [1, 512]),
                                 b[:, :], AF.Copy), inc="c_ffe")
                    k.wait(VE, "c_ffe")
                    k.do(VE, lambda v, s_=sl: v.tensor_reduce(
                        stats[:, 60:61], st_b[s_], AX.X, ALU.max,
                        apply_absolute_value=True), dr=True)
                    k.do(VE, lambda v, m_=mcol: v.tensor_tensor(
                        maxacc[:, m_:m_ + 1], maxacc[:, m_:m_ + 1],
                        stats[:, 60:61], ALU.max), inc="c_ffv", dr=True)
                    k.wait(SC, "c_ffv")
                    cst = "c_ffst" if sl == 0 else "c_ffst1"
                    k.do(SC, lambda sc, s_=sl, o_=oc, sp_=spname:
                         sc.dma_start(out=ap[sp_][o_ * P:(o_ + 1) * P, :],
                                      in_=st_b[s_]), inc=cst, incn=16)

            stage_gmax(k, 1, 0)
            stage_gmax(k, 2, 1)
            k.wait(SC, "c_gmax")
            k.do(SC, lambda sc: sc.dma_start(
                out=ap["cc3_i"][0:1, :], in_=rowst[0][:, :]),
                inc="c_ccst", incn=16)
            k.do(SC, lambda sc: sc.dma_start(
                out=ap["cc3_i"][1:2, :], in_=rowst[1][:, :]),
                inc="c_ccst", incn=16)
            k.wait(GP, "c_ccst")
            k.do(GP, lambda g: g.collective_compute(
                "AllReduce", ALU.max, all_group,
                ins=[ap["cc3_i"]], outs=[ap["cc3_o"]]), inc="c_cc")
            k.wait(SY, "c_cc")
            k.dma(SY, scal[:, 5:6], bass.AP(ap["cc3_o"].tensor, 0, [[1, P], [1, 1]]), "c_scal", slow=True)
            k.dma(SY, scal[:, 6:7], bass.AP(ap["cc3_o"].tensor, P, [[1, P], [1, 1]]), "c_scal", slow=True)
            k.wait(VE, "c_scal")
            k.do(VE, lambda v: v.tensor_scalar(
                scal[:, 5:7], scal[:, 5:7], ACLIP, None, ALU.max), dr=True)
            k.do(VE, lambda v: v.reciprocal(scal[:, 7:9], scal[:, 5:7]), dr=True)
            k.do(VE, lambda v: v.tensor_scalar(
                scal[:, 7:9], scal[:, 7:9], QMAX, None, ALU.mult), dr=True)
            k.do(VE, lambda v: v.tensor_tensor(
                scal[:, 9:10], scal[:, 4:5], scal[:, 6:7], ALU.mult), dr=True)
            k.do(VE, lambda v: v.tensor_scalar(
                scal[:, 9:10], scal[:, 9:10], 1.0 / QMAX, None, ALU.mult),
                inc="c_scal", dr=True)

            for eng in (SY, SC, VE):
                k.wait(eng, "c_ffst")
                k.wait(eng, "c_ffst1")
                k.wait(eng, "c_x1c")
            k.wait(VE, "c_ffp")
            for oc in range(OCF):
                sl = oc % 2
                cl = "c_ld0" if sl == 0 else "c_ld1"
                if oc >= 2:
                    k.wait(SY, "c_p3st" if sl == 0 else "c_p3st1")
                k.dma(SY, ld_b[sl], ap["sp_gt"][oc * P:(oc + 1) * P, :], cl)
                k.wait(SC, cl)
                k.wait(SC, "c_scal")
                k.wait(SC, "c_p3v" if sl == 0 else "c_p3v1")
                k.do(SC, lambda sc, s_=sl: sc.activation(
                    iti_v[s_], ld_b[s_], AF.Copy, scale=scal[:, 8:9]))
                k.do(SC, lambda sc, s_=sl: sc.activation(
                    sig_v[s_], iti_v[s_], AF.Sigmoid,
                    scale=scal[:, 9:10]), inc="c_p3a")
                k.wait(VE, "c_p3a")
                k.do(VE, lambda v, s_=sl: v.tensor_scalar(
                    iti2_v[s_], sig_v[s_], 65536.0, None, ALU.mult), dr=True)
                k.do(VE, lambda v, s_=sl: v.tensor_tensor(
                    st_b[s_], iti_v[s_], iti2_v[s_], ALU.mult))
                k.do(VE, lambda v, s_=sl: v.tensor_reduce(
                    stats[:, 61:62], st_b[s_], AX.X, ALU.max,
                    apply_absolute_value=True), dr=True)
                k.do(VE, lambda v: v.tensor_tensor(
                    maxacc[:, 3:4], maxacc[:, 3:4], stats[:, 61:62],
                    ALU.max), inc=("c_p3v" if sl == 0 else "c_p3v1"),
                    dr=True)
                k.wait(SC, "c_p3v" if sl == 0 else "c_p3v1")
                k.do(SC, lambda sc, s_=sl, o_=oc: sc.dma_start(
                    out=ap["sp_g2"][o_ * P:(o_ + 1) * P, :], in_=st_b[s_]),
                    inc=("c_p3st" if sl == 0 else "c_p3st1"), incn=16)

            k.wait(GP, "c_p3v1")
            stage_gmax(k, 3, 0)
            k.wait(SC, "c_gmax")
            k.do(SC, lambda sc: sc.dma_start(
                out=ap["cc4_i"], in_=rowst[0][:, :]), inc="c_ccst", incn=16)
            k.wait(GP, "c_ccst")
            k.do(GP, lambda g: g.collective_compute(
                "AllReduce", ALU.max, all_group,
                ins=[ap["cc4_i"]], outs=[ap["cc4_o"]]), inc="c_cc")
            k.wait(SY, "c_cc")
            k.dma(SY, scal[:, 10:11], bass.AP(ap["cc4_o"].tensor, 0, [[1, P], [1, 1]]), "c_scal", slow=True)
            k.wait(VE, "c_scal")
            k.do(VE, lambda v: v.tensor_scalar(
                scal[:, 10:11], scal[:, 10:11], ACLIP, None, ALU.max), dr=True)
            k.do(VE, lambda v: v.reciprocal(scal[:, 11:12], scal[:, 10:11]), dr=True)
            k.do(VE, lambda v: v.tensor_scalar(
                scal[:, 11:12], scal[:, 11:12], QMAX, None, ALU.mult),
                inc="c_scal", dr=True)

            k.wait(SY, "c_p3st")
            k.wait(SY, "c_p3st1")
            for oc in range(OCF):
                sl = oc % 2
                cl = "c_ld0" if sl == 0 else "c_ld1"
                cl2 = "c_ld20" if sl == 0 else "c_ld21"
                if oc >= 2:
                    k.wait(SY, "c_p4st" if sl == 0 else "c_p4st1")
                k.dma(SY, ld_b[sl], ap["sp_g2"][oc * P:(oc + 1) * P, :], cl)
                k.dma(SY, ld2_b[sl], ap["sp_up"][oc * P:(oc + 1) * P, :], cl2)
                k.wait(SC, cl)
                k.wait(SC, cl2)
                k.wait(SC, "c_scal")
                k.wait(SC, "c_p4v" if sl == 0 else "c_p4v1")
                k.do(SC, lambda sc, s_=sl: sc.activation(
                    iti_v[s_], ld_b[s_], AF.Copy, scale=scal[:, 11:12]))
                k.do(SC, lambda sc, s_=sl: sc.activation(
                    iti2_v[s_], ld2_b[s_], AF.Copy, scale=scal[:, 7:8]),
                    inc="c_p4a")
                k.wait(VE, "c_p4a")
                k.do(VE, lambda v, s_=sl: v.tensor_tensor(
                    st_b[s_], iti_v[s_], iti2_v[s_], ALU.mult))
                k.do(VE, lambda v, s_=sl: v.tensor_reduce(
                    stats[:, 61:62], st_b[s_], AX.X, ALU.max,
                    apply_absolute_value=True), dr=True)
                k.do(VE, lambda v: v.tensor_tensor(
                    maxacc[:, 4:5], maxacc[:, 4:5], stats[:, 61:62],
                    ALU.max), inc=("c_p4v" if sl == 0 else "c_p4v1"),
                    dr=True)
                k.wait(SC, "c_p4v" if sl == 0 else "c_p4v1")
                k.do(SC, lambda sc, s_=sl, o_=oc: sc.dma_start(
                    out=ap["sp_o"][o_ * P:(o_ + 1) * P, :], in_=st_b[s_]),
                    inc=("c_p4st" if sl == 0 else "c_p4st1"), incn=16)

            k.wait(GP, "c_p4v1")
            stage_gmax(k, 4, 0)
            k.wait(SC, "c_gmax")
            k.do(SC, lambda sc: sc.dma_start(
                out=ap["cc5_i"], in_=rowst[0][:, :]), inc="c_ccst", incn=16)
            k.wait(GP, "c_ccst")
            k.do(GP, lambda g: g.collective_compute(
                "AllReduce", ALU.max, all_group,
                ins=[ap["cc5_i"]], outs=[ap["cc5_o"]]), inc="c_cc")
            k.wait(SY, "c_cc")
            k.dma(SY, scal[:, 12:13], bass.AP(ap["cc5_o"].tensor, 0, [[1, P], [1, 1]]), "c_scal", slow=True)
            k.wait(VE, "c_scal")
            k.do(VE, lambda v: v.tensor_scalar(
                scal[:, 12:13], scal[:, 12:13], ACLIP, None, ALU.max), dr=True)
            k.do(VE, lambda v: v.reciprocal(scal[:, 13:14], scal[:, 12:13]), dr=True)
            k.do(VE, lambda v: v.tensor_scalar(
                scal[:, 13:14], scal[:, 13:14], QMAX, None, ALU.mult), dr=True)
            k.do(VE, lambda v: v.tensor_tensor(
                scal[:, 14:15], scal[:, 4:5], scal[:, 5:6], ALU.mult), dr=True)
            k.do(VE, lambda v: v.tensor_scalar(
                scal[:, 14:15], scal[:, 14:15], 1.0 / QMAX, None, ALU.mult), dr=True)
            k.do(VE, lambda v: v.tensor_tensor(
                scal[:, 15:16], scal[:, 9:10], scal[:, 10:11], ALU.mult), dr=True)
            k.do(VE, lambda v: v.tensor_scalar(
                scal[:, 15:16], scal[:, 15:16], 1.0 / (65536.0 * QMAX), None,
                ALU.mult), dr=True)
            k.do(VE, lambda v: v.tensor_tensor(
                scal[:, 16:17], scal[:, 14:15], scal[:, 15:16], ALU.mult), dr=True)
            k.do(VE, lambda v: v.tensor_tensor(
                scal[:, 16:17], scal[:, 16:17], scal[:, 12:13], ALU.mult), dr=True)
            k.do(VE, lambda v: v.tensor_scalar(
                scal[:, 16:17], scal[:, 16:17], 1.0 / QMAX, None, ALU.mult),
                inc="c_scal", dr=True)

            for eng in (SY, SC, VE, TE):
                k.wait(eng, "c_ffp")
            k.wait(SY, "c_p4st")
            k.wait(SY, "c_p4st1")
            W_ = WD_OT * P
            for r in range(n_cores):
                k.do(SY, lambda e, r_=r: e.dma_start(
                    out=A(wdr_b.tensor, wdr_b.offset + r_ * W_,
                          [D, OCF], [1, W_]),
                    in_=bass.AP(ap["wdag_o"].tensor, r_ * DF * W_,
                                [[W_, P], [P * W_, OCF], [1, W_]])),
                    inc="c_wdr", incn=16)
            for jt in range(NT):
                sl = jt % 2
                co = "c_ol0" if sl == 0 else "c_ol1"
                cxr = "c_xr0" if sl == 0 else "c_xr1"
                car = "c_ar0" if sl == 0 else "c_ar1"
                if jt >= 2:
                    k.wait(SY, "c_yst" if sl == 0 else "c_yst1")
                    k.wait(SY, "c_o1a", k.count("c_o1a") - 1)
                OQ = OCF // 2
                for q, eng in enumerate((SY, SC)):
                    k.do(eng, lambda e, s_=sl, j_=jt, q_=q: e.dma_start(
                        out=o_raw[s_][:, q_ * OQ * P:(q_ + 1) * OQ * P]
                        .rearrange("p (a b) -> p a b", a=OQ),
                        in_=bass.AP(ap["sp_o"].tensor,
                                    j_ * P + q_ * OQ * P * T,
                                    [[T, P], [P * T, OQ], [1, P]])),
                        inc=co, incn=16)
                k.dma(SY, x_f[sl][:], ap["x"][jt * P:(jt + 1) * P, :], cxr)
                k.dma(SY, ar_b[sl][:], ap["ao_sp"][jt * P:(jt + 1) * P, :],
                      car)
                k.wait(SC, co)
                k.wait(SC, "c_scal")
                if jt >= 2:
                    k.wait(SC, "c_o1v" if sl == 0 else "c_o1v1")
                k.do(SC, lambda sc, s_=sl: sc.activation(
                    iti_p[s_], o_raw[s_], AF.Copy, scale=scal[:, 13:14]),
                    inc="c_o1a")
                k.wait(VE, "c_o1a")
                if jt >= 2:
                    k.wait(VE, "c_dwp", k.count("c_dwp") - 2)
                k.do(VE, lambda v, s_=sl: v.tensor_copy(
                    o1_b[s_], iti_p[s_]),
                    inc=("c_o1v" if sl == 0 else "c_o1v1"))
                k.wait(TE, "c_o1v" if sl == 0 else "c_o1v1")
                k.wait(TE, "c_wdr")
                for og in range(2):
                    bank = psA[og]
                    nev = k.count("c_dwe")
                    if nev >= 2:
                        k.wait(TE, "c_dwe", nev - 1)
                    for fc in range(OCF):
                        k.do(TE, lambda t, b=bank, s_=sl, f_=fc, og_=og:
                             t.matmul(b[:, :],
                                      o1_b[s_][:, f_ * P:(f_ + 1) * P],
                                      wdr_b[:, f_ * D + og_ * 512:
                                            f_ * D + (og_ + 1) * 512],
                                      start=(f_ == 0), stop=(f_ == OCF - 1)),
                             inc=("c_dwp" if fc == OCF - 1 else None))
                    k.wait(SC, "c_dwp")
                    k.wait(SC, "c_scal")
                    k.do(SC, lambda sc, b=bank, og_=og: sc.activation(
                        ffn_f[:, og_ * 512:(og_ + 1) * 512], b[:, :],
                        AF.Copy, scale=scal[:, 16:17]), inc="c_dwe")
                k.wait(VE, "c_dwe")
                k.wait(VE, cxr)
                k.wait(VE, car)
                k.do(VE, lambda v, s_=sl: v.tensor_tensor(
                    y_f[s_][:, :], x_f[s_][:], ar_b[s_][:], ALU.add))
                k.do(VE, lambda v, s_=sl: v.tensor_tensor(
                    y_f[s_][:, :], y_f[s_][:, :], ffn_f[:, 0:D], ALU.add),
                    inc="c_res2")
                k.wait(SC, "c_res2")
                k.do(SC, lambda sc, s_=sl, j_=jt: sc.dma_start(
                    out=yout[j_ * P:(j_ + 1) * P, :], in_=y_f[s_][:, :]),
                    inc=("c_yst" if sl == 0 else "c_yst1"), incn=16)
            k.wait(SY, "c_yst")
            k.wait(SY, "c_yst1")
            if dbg:
                k.dma(SY, ap["dbg_xn"], ap["xn_sp"], "c_misc")
                k.dma(SY, ap["dbg_vki"], ap["vk_i"], "c_misc")
                k.dma(SY, ap["dbg_ao"], ap["ao_sp"], "c_misc")
                k.dma(SY, ap["dbg_up"], ap["sp_up"], "c_misc")
                k.do(SY, lambda e: e.dma_start(out=ap["dbg_scal"],
                                               in_=scal[:, :]),
                     inc="c_misc", incn=16)
                k.wait(SY, "c_misc")

        @block.sync
        def _(eng):
            program(K(SY, eng, sems))

        @block.scalar
        def _(eng):
            program(K(SC, eng, sems))

        @block.vector
        def _(eng):
            program(K(VE, eng, sems))

        @block.tensor
        def _(eng):
            program(K(TE, eng, sems))

        @block.gpsimd
        def _(eng):
            program(K(GP, eng, sems))

    return nc



def _sigma_perm():
    perm = np.empty(D, dtype=np.int64)
    for h in range(NH):
        for i in range(32):
            perm[i * 16 + h] = h * DH + 2 * i
            perm[512 + i * 16 + h] = h * DH + 2 * i + 1
    return perm


def _tau_perm():
    perm = np.empty(D, dtype=np.int64)
    for h in range(NH):
        for j2 in range(DH):
            orig = 2 * j2 if j2 < 32 else 2 * (j2 - 32) + 1
            perm[h * DH + j2] = h * DH + orig
    return perm


def _rope_tables(pos0, T):
    inv_freq = 1.0 / (10000.0 ** (np.arange(0, DH, 2, dtype=np.float64) / DH))
    t = np.arange(pos0, pos0 + T, dtype=np.float64)
    freqs = np.outer(t, inv_freq)
    cos_e = np.repeat(np.cos(freqs), 16, axis=1).astype(ml_dtypes.bfloat16)
    sin_e = np.repeat(np.sin(freqs), 16, axis=1).astype(ml_dtypes.bfloat16)
    return np.ascontiguousarray(cos_e), np.ascontiguousarray(sin_e)


_BUILD_CACHE = {}


def kernel(L0_attn_x, L0_attn_Wqkv, L0_attn_bqkv, L0_attn_gq, L0_attn_gk,
           L0_attn_Wout, L0_attn_bout, L0_attn_gout, L0_ffn_w_gate,
           L0_ffn_w_up, L0_ffn_w_down, _n_cores=8, _T=None, _stiles=4,
           _dbg=False):
    B, L, _ = L0_attn_x.shape
    n_cores = _n_cores
    T = _T if _T is not None else (B * L) // n_cores
    key = (T, n_cores, _stiles, _dbg)
    if key not in _BUILD_CACHE:
        _BUILD_CACHE[key] = build_nc(T, n_cores, _stiles, dbg=_dbg)
    nc = _BUILD_CACHE[key]

    sig = _sigma_perm()
    tau = _tau_perm()
    f32 = np.float32

    wq = np.ascontiguousarray(np.asarray(L0_attn_Wqkv, dtype=f32).T)
    wqp = np.empty_like(wq)
    bq = np.asarray(L0_attn_bqkv, dtype=f32)
    bqp = np.empty_like(bq)
    for g in range(3):
        wqp[:, g * D:(g + 1) * D] = wq[:, g * D:(g + 1) * D][:, sig]
        bqp[g * D:(g + 1) * D] = bq[g * D:(g + 1) * D][sig]
    wo = np.ascontiguousarray(np.asarray(L0_attn_Wout, dtype=f32).T)
    wop = np.ascontiguousarray(wo[tau, :])
    gq = np.asarray(L0_attn_gq, dtype=f32)[sig]
    gk = np.asarray(L0_attn_gk, dtype=f32)[sig]
    gout = np.asarray(L0_attn_gout, dtype=f32)
    bout = np.asarray(L0_attn_bout, dtype=f32)

    def rep(vec):
        return np.ascontiguousarray(np.broadcast_to(vec, (P, vec.shape[0])))

    wu_o = np.asarray(L0_ffn_w_up, dtype=f32)
    wg_o = np.asarray(L0_ffn_w_gate, dtype=f32)
    wd_o = np.asarray(L0_ffn_w_down, dtype=f32)
    ident = np.eye(P, dtype=f32)

    x = np.asarray(L0_attn_x, dtype=f32)
    wu_rows = DF // n_cores
    wd_rows = D // n_cores

    in_maps = []
    for c in range(n_cores):
        b, h = c // 2, c % 2
        if n_cores > 1:
            xs = np.ascontiguousarray(x[b, h * T:(h + 1) * T, :])
            pos0 = h * T
        else:
            xs = np.ascontiguousarray(x.reshape(-1, D)[:T])
            pos0 = 0
        cos_e, sin_e = _rope_tables(pos0, T)
        in_maps.append({
            "x": xs, "rc": cos_e, "rs": sin_e,
            "wqkvT": wqp, "bqkv_rep": rep(bqp),
            "gq_rep": rep(gq), "gk_rep": rep(gk),
            "woutT": wop, "bout_rep": rep(bout), "gout_rep": rep(gout),
            "wu_slice": np.ascontiguousarray(
                wu_o[c * wu_rows:(c + 1) * wu_rows]),
            "wg_slice": np.ascontiguousarray(
                wg_o[c * wu_rows:(c + 1) * wu_rows]),
            "wd_slice": np.ascontiguousarray(
                wd_o[c * wd_rows:(c + 1) * wd_rows]),
            "ident": ident,
        })

    res = run_bass_kernel_spmd(nc, in_maps, core_ids=list(range(n_cores)))
    outs = [res.results[c]["y"] for c in range(n_cores)]
    y = np.empty((B, L, D), dtype=np.float32)
    if n_cores > 1:
        for c in range(n_cores):
            b, h = c // 2, c % 2
            y[b, h * T:(h + 1) * T, :] = outs[c]
    else:
        y.reshape(-1, D)[:T] = outs[0]
    return y


# version 10
# speedup vs baseline: 1.0050x; 1.0036x over previous
import numpy as np
import ml_dtypes
from contextlib import ExitStack

import concourse.bass as bass
import concourse.mybir as mybir
from concourse.bass_utils import run_bass_kernel_spmd

F32 = mybir.dt.float32
BF16 = mybir.dt.bfloat16
I32 = mybir.dt.int32
AF = mybir.ActivationFunctionType
ALU = mybir.AluOpType
AX = mybir.AxisListType

D = 1024
DF = 3072
NH = 16
DH = 64
P = 128
NC8 = D // P
OCF = DF // P
RMS_EPS = 1.1920929e-07
LN_EPS = 1e-5
EPS_LIN = 1e-6
QMAX = 32767.0
ACLIP = QMAX * 1e-12
WQMAX = 7.0
WCLIP = WQMAX * 1e-12

SY, SC, VE, TE, GP = "sync", "scalar", "vector", "tensor", "gpsimd"


def A(t, off, *dims):
    w = t.shape[1]
    return bass.AP(t, off, [[w, P]] + [list(d) for d in dims])


class Ctr:
    __slots__ = ("n",)

    def __init__(self):
        self.n = 0


class K:
    def __init__(self, cur, eng, sems):
        self.cur = cur
        self.eng = eng
        self.sems = sems
        self.ctrs = {}

    def ctr(self, name):
        if name not in self.ctrs:
            self.ctrs[name] = Ctr()
        return self.ctrs[name]

    def do(self, engname, fn, inc=None, incn=1, dr=False):
        if self.cur == engname:
            ins = fn(self.eng)
            if inc is not None:
                ins.then_inc(self.sems[inc], incn)
            if dr:
                self.eng.drain()
        if inc is not None:
            self.ctr(inc).n += incn

    def dma(self, engname, out, in_, inc, slow=False):
        self.do(engname,
                lambda e: e.dma_start(out=out, in_=in_,
                                      allow_slow_non_contiguous=slow),
                inc=inc, incn=16)

    def wait(self, engname, ctrname, val=None):
        if self.cur == engname:
            self.eng.wait_ge(self.sems[ctrname],
                             self.ctr(ctrname).n if val is None else val)

    def count(self, ctrname):
        return self.ctr(ctrname).n


def build_nc(T, n_cores, stiles, dbg=False):
    NT = T // P
    NS = NT // stiles
    NO5 = T // 512
    SD = stiles * D
    assert NT % stiles == 0 and T % 512 == 0
    WU_OT = DF // P // n_cores
    WD_OT = D // P // n_cores
    assert DF % (P * n_cores) == 0 and D % (P * n_cores) == 0
    pair_groups = ([[2 * g, 2 * g + 1] for g in range(n_cores // 2)]
                   if n_cores > 1 else [[0]])
    all_group = [list(range(n_cores))]

    nc = bass.Bass("TRN2", target_bir_lowering=False,
                   detect_race_conditions=False)
    ap = {}

    def din(name, shape, dt=F32):
        ap[name] = nc.dram_tensor(name, shape, dt, kind="ExternalInput").ap()

    def dint(name, shape, dt):
        ap[name] = nc.dram_tensor(name, shape, dt, kind="Internal").ap()

    din("x", [T, D])
    din("rc", [T, 512], BF16)
    din("rs", [T, 512], BF16)
    din("wqkvT", [D, 3 * D])
    din("bqkv_rep", [P, 3 * D])
    din("gq_rep", [P, D])
    din("gk_rep", [P, D])
    din("woutT", [D, D])
    din("bout_rep", [P, D])
    din("gout_rep", [P, D])
    din("wu_slice", [WU_OT * P, D])
    din("wg_slice", [WU_OT * P, D])
    din("wd_slice", [WD_OT * P, DF])
    din("ident", [P, P])

    yout = nc.dram_tensor("y", [T, D], F32, kind="ExternalOutput").ap()

    dint("xn_sp", [D, T], BF16)
    dint("ao_sp", [T, D], F32)
    dint("wq_bf", [D, 3 * D], BF16)
    dint("wo_bf", [D, D], BF16)
    dint("sp_up", [DF, T], BF16)
    dint("sp_gt", [DF, T], BF16)
    dint("sp_g2", [DF, T], BF16)
    dint("sp_o", [DF, T], BF16)
    for nm, w in [("cc1", 1), ("cc3", 2), ("cc4", 1), ("cc5", 1)]:
        dint(nm + "_i", [w, P], F32)
        dint(nm + "_o", [w, P], F32)
    dint("vk_i", [65, NH * DH], F32)
    dint("vk_o", [65, NH * DH], F32)
    if dbg:
        def dout(name, shape, dt):
            ap[name] = nc.dram_tensor(name, shape, dt,
                                      kind="ExternalOutput").ap()
        dout("dbg_q", [P, stiles * D], BF16)
        dout("dbg_k", [P, stiles * D], BF16)
        dout("dbg_v", [P, stiles * D], BF16)
        dout("dbg_kf", [P, D], BF16)
        dout("dbg_vp", [P, NH * 65], BF16)
        dout("dbg_qf", [P, D], BF16)
        dout("dbg_wk1", [P, D], BF16)
        dout("dbg_wk2", [P, D], BF16)
        dout("dbg_att", [P, D], BF16)
        dout("dbg_z", [P, D], F32)
        dout("dbg_xn", [D, T], BF16)
        dout("dbg_vki", [65, NH * DH], F32)
        dout("dbg_ao", [T, D], BF16)
        dout("dbg_up", [DF, T], BF16)
        dout("dbg_scal", [P, 24], F32)
        dout("dbg_st", [P, 64], F32)
        dout("dbg_xntm", [P, D], BF16)
        dout("dbg_xs", [P, D], F32)
    dint("wuag_i", [D, WU_OT * P], BF16)
    dint("wuag_o", [n_cores * D, WU_OT * P], BF16)
    dint("wgag_i", [D, WU_OT * P], BF16)
    dint("wgag_o", [n_cores * D, WU_OT * P], BF16)
    dint("wdag_i", [DF, WD_OT * P], BF16)
    dint("wdag_o", [n_cores * DF, WD_OT * P], BF16)

    with ExitStack() as stack:
        en = stack.enter_context

        def sb(name, shape, dt):
            return en(nc.sbuf_tensor(name, shape, dt))

        ar1_cols = max(NC8 * T, 3 * DF, OCF * D + 2 * DF)
        ar3_cols = max(6 * T, 5 * SD + 3 * D, 4 * DF, 10 * D + 8320)
        AR1 = sb("AR1", [P, ar1_cols], BF16)
        AR3 = sb("AR3", [P, ar3_cols], BF16)
        AR4 = sb("AR4", [P, SD], BF16)
        itcols = max(T, NC8 * 512)
        iti = sb("iti", [P, itcols], I32)
        iti2 = sb("iti2", [P, itcols], I32)
        ident_b = sb("ident_b", [P, P], BF16)
        gq_b = sb("gq_b", [P, D], BF16)
        gk_b = sb("gk_b", [P, D], BF16)
        gout_b = sb("gout_b", [P, D], BF16)
        bo_b = sb("bo_b", [P, D], BF16)
        sq_f = [sb(f"sq_f{i}", [P, D], F32) for i in range(2)]
        sq_bv = [sq_f[i][:].bitcast(BF16) for i in range(2)]
        stats = sb("stats", [P, 64], F32)
        maxacc = sb("maxacc", [P, 8], F32)
        scal = sb("scal", [P, 24], F32)
        xn_tm = [sb(f"xn_tm{i}", [P, D], BF16) for i in range(2)]
        rc_f = [sb(f"rcbuf{i}", [P, 512], BF16) for i in range(2)]
        rs_f = [sb(f"rsbuf{i}", [P, 512], BF16) for i in range(2)]
        wk1_b = sb("wk1_b", [P, D], BF16)
        wk2_b = sb("wk2_b", [P, D], BF16)
        wk3_b = sb("wk3_b", [P, D], BF16)
        kf_b = [sb(f"kf_b{i}", [P, D], BF16) for i in range(2)]
        vp_b = [sb(f"vp_b{i}", [P, NH * 65], BF16) for i in range(2)]
        vkT_b = sb("vkT_b", [P, 8 * 65], BF16)

        wl_b = [sb(f"wl_b{i}", [P, NC8 * P], BF16) for i in range(2)]
        ffn_f = sb("ffn_f", [P, D], F32)
        vk_f = ffn_f[0:65, :]
        wps_f = sb("wps_f", [P, 96], F32)
        wpi_f = sb("wpi_f", [P, 96], F32)
        wag_b = sb("wag_b", [P, P], BF16)
        onesr_f = sb("onesr_f", [1, P], F32)
        rowst = [sb(f"rowst{i}", [1, P], F32) for i in range(2)]
        colst = sb("colst", [1, 2], F32)

        q_tm = AR3[:, 0:SD]
        k_tm = AR3[:, SD:2 * SD]
        v_tm = AR3[:, 2 * SD:3 * SD]
        ld_b = [AR3[:, 0:T], AR3[:, T:2 * T]]
        x_s = AR3[:, 3 * SD:5 * SD].bitcast(F32)
        bq_b = AR3[:, 5 * SD:5 * SD + 3 * D]
        x_f = [AR3[:, 0:2 * D].bitcast(F32),
               AR3[:, 2 * D:4 * D].bitcast(F32)]
        o_raw = [AR3[:, 4 * D:4 * D + DF],
                 AR3[:, 4 * D + DF:4 * D + 2 * DF]]
        iti_p = [iti[:, 0:DF], iti2[:, 0:DF]]
        att_p = [AR3[:, 0:D], AR3[:, D:2 * D]]
        att_fm_p = [AR3[:, 2 * D:3 * D], AR3[:, 3 * D:4 * D]]
        z_p = [AR3[:, 4 * D:5 * D], AR3[:, 5 * D:6 * D]]
        ffn_p = [AR3[:, 6 * D:8 * D].bitcast(F32),
                 AR3[:, 8 * D:10 * D].bitcast(F32)]
        resst_p = [AR3[:, 10 * D:10 * D + 2080].bitcast(F32),
                   AR3[:, 10 * D + 2080:10 * D + 4160].bitcast(F32)]
        den_p = [AR3[:, 10 * D + 4160:10 * D + 4192].bitcast(F32),
                 AR3[:, 10 * D + 4192:10 * D + 4224].bitcast(F32)]
        x_r = [AR3[:, 10 * D + 4224:10 * D + 6272].bitcast(F32),
               AR3[:, 10 * D + 6272:10 * D + 8320].bitcast(F32)]
        vk_bf = AR3[0:65, 0:1024]
        ld2_b = [AR3[:, 2 * T:3 * T], AR3[:, 3 * T:4 * T]]
        st_b = [AR3[:, 4 * T:5 * T], AR3[:, 5 * T:6 * T]]
        sig_f = AR3[:, 2 * T:4 * T].bitcast(F32)
        iti_v = [iti[:, 0:T], AR1[:, 0:2 * T].bitcast(I32)]
        iti2_v = [iti2[:, 0:T], AR1[:, 2 * T:4 * T].bitcast(I32)]
        sig_v = [sig_f[:, 0:T], AR1[:, 4 * T:6 * T].bitcast(F32)]
        wp_f = [AR3[:, 0:2 * DF].bitcast(F32),
                AR3[:, 2 * DF:4 * DF].bitcast(F32)]
        tmp_f = AR1[:, 0:2 * DF].bitcast(F32)
        wq_b = AR1[:, 2 * DF:3 * DF]
        wq_b2 = AR1[:, 3 * DF:4 * DF]
        big_a = AR1
        wdr_b = AR1[:, 0:OCF * D]
        o1_b = [AR1[:, OCF * D:OCF * D + DF],
                AR1[:, OCF * D + DF:OCF * D + 2 * DF]]
        wqr_b = [iti[:, 0:itcols // 2].bitcast(BF16),
                 iti[:, itcols // 2:itcols].bitcast(BF16)]
        wor_b = iti2.bitcast(BF16)
        ao_b = [wk1_b, wk2_b]
        ar_b = kf_b
        y_f = sq_f

        psA = [en(nc.psum_tensor(f"psA{i}", [P, 512], F32)) for i in range(4)]
        psT = [en(nc.psum_tensor(f"psT{i}", [P, P], BF16)) for i in range(2)]
        psV = [en(nc.psum_tensor(f"psV{i}", [65, 8 * DH], F32)) for i in range(2)]

        sem_names = [
            "c_prol", "c_prolv", "c_bnc0", "c_bnc1", "c_bncv", "c_bncst",
            "c_wp0", "c_wp1", "c_wpv", "c_wpt", "c_wpte", "c_agst", "c_ag",
            "c_x0", "c_x1", "c_lnv", "c_lna", "c_trp", "c_trpe", "c_xnsp",
            "c_qw0", "c_qw1", "c_qkvp", "c_qkve",
            "c_rt0", "c_rt1", "c_tokv", "c_toka", "c_vkmm", "c_qft", "c_qfte",
            "c_vkev", "c_ccst", "c_cc", "c_vkld", "c_vktr", "c_vktre",
            "c_resp", "c_rese", "c_attt", "c_atte", "c_wop", "c_woe", "c_wor",
            "c_aost",
            "c_xnr0", "c_xnr1", "c_x1c",
            "c_wl0", "c_wl1", "c_ffp", "c_ffe", "c_ffv", "c_ffst",
            "c_ld0", "c_ld1", "c_p3a", "c_p3v", "c_p3st",
            "c_ld20", "c_ld21", "c_p4a", "c_p4v", "c_p4st",
            "c_o1a", "c_o1v", "c_o1st", "c_ol0", "c_ol1", "c_wdr",
            "c_dwp", "c_dwe", "c_xr0", "c_xr1", "c_ar0", "c_ar1",
            "c_res2", "c_yst", "c_misc", "c_scal", "c_gmax", "c_ffst1", "c_p3st1", "c_p4st1", "c_o1st1", "c_yst1", "c_bncst1", "c_p3v1", "c_p4v1", "c_o1v1",
            "c_resd", "c_zcp", "c_aoz",
        ]
        sems = {n: en(nc.semaphore(n)) for n in sem_names}
        block = en(nc.Block())


        def stage_gmax(k, mcol, row):
            k.wait(GP, "c_lnv")
            k.wait(GP, "c_ffv")
            k.wait(GP, "c_p3v")
            k.wait(GP, "c_p4v")
            k.wait(GP, "c_p3v1")
            k.wait(GP, "c_p4v1")
            k.do(GP, lambda g, m_=mcol, r_=row: g.tensor_reduce(
                colst[0:1, r_:r_ + 1], maxacc[:, m_:m_ + 1], AX.C, ALU.max),
                inc="c_gmax")
            k.wait(VE, "c_gmax")
            k.do(VE, lambda v, r_=row: v.tensor_scalar(
                rowst[r_][:, :], onesr_f[:, :], colst[0:1, r_:r_ + 1],
                None, ALU.mult), inc="c_gmax")

        def rsqrt_refined(k, src, dst, n):
            k.do(VE, lambda v: v.reciprocal(stats[:, dst:dst + n],
                                            stats[:, src:src + n]),
                 inc="c_lnv", dr=True)
            k.wait(SC, "c_lnv")
            k.do(SC, lambda s: s.activation(stats[:, dst:dst + n],
                                            stats[:, dst:dst + n], AF.Sqrt),
                 inc="c_lna")
            k.wait(VE, "c_lna")
            k.do(VE, lambda v: v.tensor_tensor(
                stats[:, 56:56 + n], stats[:, dst:dst + n],
                stats[:, dst:dst + n], ALU.mult), dr=True)
            k.do(VE, lambda v: v.tensor_tensor(
                stats[:, 56:56 + n], stats[:, 56:56 + n],
                stats[:, src:src + n], ALU.mult), dr=True)
            k.do(VE, lambda v: v.tensor_scalar(
                stats[:, 56:56 + n], stats[:, 56:56 + n], -0.5, 1.5,
                ALU.mult, ALU.add), dr=True)
            k.do(VE, lambda v: v.tensor_tensor(
                stats[:, dst:dst + n], stats[:, dst:dst + n],
                stats[:, 56:56 + n], ALU.mult), inc="c_lnv", dr=True)

        def program(k: K):
            for (src, dst, cols) in [(ap["ident"], ident_b, P),
                                     (ap["gq_rep"], gq_b, D),
                                     (ap["gk_rep"], gk_b, D),
                                     (ap["gout_rep"], gout_b, D),
                                     (ap["bqkv_rep"], bq_b, 3 * D),
                                     (ap["bout_rep"], bo_b, D)]:
                k.wait(SY, "c_prolv")
                k.dma(SY, tmp_f[:, 0:cols], src, "c_prol")
                k.wait(VE, "c_prol")
                k.do(VE, lambda v, d_=dst, c_=cols: v.tensor_copy(
                    d_[:, 0:c_], tmp_f[:, 0:c_]), inc="c_prolv")
            k.do(VE, lambda v: v.memset(maxacc[:], 0.0))
            k.do(VE, lambda v: v.memset(onesr_f[:], 1.0), inc="c_prolv")

            for (orig, agi, I, nOT) in [
                    (ap["wu_slice"], ap["wuag_i"], D, WU_OT),
                    (ap["wg_slice"], ap["wgag_i"], D, WU_OT),
                    (ap["wd_slice"], ap["wdag_i"], DF, WD_OT)]:
                nblk = I // 32
                nich = I // P
                for ot in range(nOT):
                    sl = ot % 2
                    cw = "c_wp0" if sl == 0 else "c_wp1"
                    k.wait(SY, "c_wpv")
                    k.dma(SY, wp_f[sl][:, 0:I], orig[ot * P:(ot + 1) * P, :],
                          cw)
                    k.wait(VE, cw)
                    k.wait(VE, "c_prolv")
                    k.wait(VE, "c_agst")
                    k.do(VE, lambda v, s_=sl, I_=I, nb=nblk: v.tensor_reduce(
                        wps_f[:, 0:nb],
                        wp_f[s_][:, 0:I_].rearrange("p (a b) -> p a b", b=32),
                        AX.X, ALU.max, apply_absolute_value=True), dr=True)
                    k.do(VE, lambda v, nb=nblk: v.tensor_scalar(
                        wps_f[:, 0:nb], wps_f[:, 0:nb], WCLIP, 1.0 / WQMAX,
                        ALU.max, ALU.mult), dr=True)
                    k.do(VE, lambda v, nb=nblk: v.reciprocal(
                        wpi_f[:, 0:nb], wps_f[:, 0:nb]), dr=True)
                    for b in range(nblk):
                        k.do(VE, lambda v, s_=sl, b_=b: v.tensor_scalar(
                            tmp_f[:, 32 * b_:32 * (b_ + 1)],
                            wp_f[s_][:, 32 * b_:32 * (b_ + 1)],
                            wpi_f[:, b_:b_ + 1], None, ALU.mult),
                            dr=(b == nblk - 1))
                    k.do(VE, lambda v, I_=I: v.tensor_copy(
                        iti2[:, 0:I_], tmp_f[:, 0:I_]), dr=True)
                    for b in range(nblk):
                        k.do(VE, lambda v, b_=b: v.tensor_scalar(
                            wq_b[:, 32 * b_:32 * (b_ + 1)],
                            iti2[:, 32 * b_:32 * (b_ + 1)],
                            wps_f[:, b_:b_ + 1], None, ALU.mult),
                            dr=(b == nblk - 1))
                    k.do(VE, lambda v: v.nop(), inc="c_wpv")
                    for c in range(nich):
                        k.wait(TE, "c_wpv")
                        nev = k.count("c_wpte")
                        if nev >= 2:
                            k.wait(TE, "c_wpte", nev - 1)
                        k.do(TE, lambda t, c_=c: t.transpose(
                            psT[c_ % 2][:, :], wq_b[:, c_ * P:(c_ + 1) * P],
                            ident_b[:]), inc="c_wpt")
                        k.wait(VE, "c_wpt")
                        k.do(VE, lambda v, c_=c: v.tensor_copy(
                            wag_b[:, :], psT[c_ % 2][:, :]), inc="c_wpte")
                        k.wait(SC, "c_wpte")
                        k.do(SC, lambda s, c_=c, o_=ot, a_=agi: s.dma_start(
                            out=a_[c_ * P:(c_ + 1) * P, o_ * P:(o_ + 1) * P],
                            in_=wag_b[:, :]), inc="c_agst", incn=16)
                        k.wait(VE, "c_agst")
            k.wait(GP, "c_agst")
            for nm in ("wu", "wg", "wd"):
                k.do(GP, lambda g, n_=nm: g.collective_compute(
                    "AllGather", ALU.bypass, all_group,
                    ins=[ap[n_ + "ag_i"]], outs=[ap[n_ + "ag_o"]]),
                    inc="c_ag")

            for (src, dst, cols) in [(ap["wqkvT"], ap["wq_bf"], 3 * D),
                                     (ap["woutT"], ap["wo_bf"], D)]:
                for r in range(NC8):
                    sl = r % 2
                    cb = "c_bnc0" if sl == 0 else "c_bnc1"
                    k.wait(SY, "c_bncst")
                    k.wait(SY, "c_wpv")
                    k.dma(SY, wp_f[sl][:, 0:cols], src[r * P:(r + 1) * P, :],
                          cb)
                    k.wait(VE, cb)
                    k.wait(VE, "c_wpt")
                    k.wait(VE, "c_bncst")
                    k.do(VE, lambda v, s_=sl, c_=cols: v.tensor_copy(
                        wq_b[:, 0:c_], wp_f[s_][:, 0:c_]), inc="c_bncv")
                    k.wait(SC, "c_bncv")
                    k.do(SC, lambda s, r_=r, c_=cols, d_=dst: s.dma_start(
                        out=d_[r_ * P:(r_ + 1) * P, :], in_=wq_b[:, 0:c_]),
                        inc="c_bncst", incn=16)

            for eng in (SY, SC, VE, TE):
                k.wait(eng, "c_bncst")
                k.wait(eng, "c_bncst1")
                k.wait(eng, "c_agst")
            for s in range(NS):
                for j in range(stiles):
                    jt = s * stiles + j
                    sl = jt % 2
                    cx = "c_x0" if sl == 0 else "c_x1"
                    if s > 0 and j == 0:
                        k.wait(SY, "c_lnv")
                    k.dma(SY, x_s[:, j * D:(j + 1) * D],
                          ap["x"][jt * P:(jt + 1) * P, :], cx)
                    k.wait(VE, cx)
                    k.do(VE, lambda v, j_=j: v.reduce_sum(
                        stats[:, j_:j_ + 1], x_s[:, j_ * D:(j_ + 1) * D],
                        AX.X))
                    k.wait(SC, cx)
                    k.wait(SC, "c_lnv")
                    k.do(SC, lambda sc, s_=sl, j_=j: sc.activation(
                        sq_f[s_][:], x_s[:, j_ * D:(j_ + 1) * D], AF.Square),
                        inc="c_lna")
                    k.wait(VE, "c_lna")
                    k.do(VE, lambda v, s_=sl, j_=j: v.reduce_sum(
                        stats[:, 8 + j_:9 + j_], sq_f[s_][:], AX.X),
                        inc="c_lnv")
                n = stiles
                k.do(VE, lambda v: v.tensor_scalar(
                    stats[:, 16:16 + n], stats[:, 0:n], 1.0 / D, None,
                    ALU.mult), dr=True)
                k.do(VE, lambda v: v.tensor_scalar(
                    stats[:, 24:24 + n], stats[:, 8:8 + n], 1.0 / D, None,
                    ALU.mult), dr=True)
                k.do(VE, lambda v: v.tensor_tensor(
                    stats[:, 48:48 + n], stats[:, 16:16 + n],
                    stats[:, 16:16 + n], ALU.mult), dr=True)
                k.do(VE, lambda v: v.tensor_sub(
                    stats[:, 24:24 + n], stats[:, 24:24 + n],
                    stats[:, 48:48 + n]), dr=True)
                k.do(VE, lambda v: v.tensor_scalar(
                    stats[:, 24:24 + n], stats[:, 24:24 + n], LN_EPS, None,
                    ALU.add), inc="c_lnv", dr=True)
                rsqrt_refined(k, 24, 32, n)
                k.do(VE, lambda v: v.tensor_tensor(
                    stats[:, 40:40 + n], stats[:, 16:16 + n],
                    stats[:, 32:32 + n], ALU.mult), dr=True)
                k.do(VE, lambda v: v.tensor_scalar(
                    stats[:, 40:40 + n], stats[:, 40:40 + n], -1.0, None,
                    ALU.mult), inc="c_lnv", dr=True)

                if dbg and s == 0:
                    k.wait(SC, "c_lnv")
                    k.do(SC, lambda sc: sc.dma_start(
                        out=ap["dbg_st"], in_=stats[:, :]),
                        inc="c_misc", incn=16)
                    k.do(SC, lambda sc: sc.dma_start(
                        out=ap["dbg_xs"], in_=x_s[:, 0:D]),
                        inc="c_misc", incn=16)
                    k.wait(SC, "c_misc")
                for j in range(stiles):
                    jt = s * stiles + j
                    sl = jt % 2
                    k.wait(VE, "c_lnv")
                    if jt >= 2:
                        k.wait(VE, "c_trpe")
                    k.do(VE, lambda v, s_=sl, j_=j: v.tensor_scalar(
                        xn_tm[s_][:], x_s[:, j_ * D:(j_ + 1) * D],
                        stats[:, 32 + j_:33 + j_],
                        stats[:, 40 + j_:41 + j_],
                        ALU.mult, ALU.add), inc="c_lna")
                    if dbg and jt == 0:
                        k.wait(SC, "c_lna")
                        k.do(SC, lambda sc: sc.dma_start(
                            out=ap["dbg_xntm"], in_=xn_tm[0][:, :]),
                            inc="c_misc", incn=16)
                        k.wait(SC, "c_misc")
                    k.do(VE, lambda v, s_=sl: v.tensor_reduce(
                        stats[:, 48:49], xn_tm[s_][:], AX.X, ALU.max,
                        apply_absolute_value=True), dr=True)
                    k.do(VE, lambda v: v.tensor_tensor(
                        maxacc[:, 0:1], maxacc[:, 0:1], stats[:, 48:49],
                        ALU.max), inc="c_lnv", dr=True)
                    for c in range(NC8):
                        nev = k.count("c_trpe")
                        if nev >= 2:
                            k.wait(TE, "c_trpe", nev - 1)
                        k.wait(TE, "c_lna")
                        k.do(TE, lambda t, s_=sl, c_=c: t.transpose(
                            psT[c_ % 2][:, :],
                            xn_tm[s_][:, c_ * P:(c_ + 1) * P], ident_b[:]),
                            inc="c_trp")
                        k.wait(VE, "c_trp")
                        k.do(VE, lambda v, c_=c, j_=j: v.tensor_copy(
                            AR4[:, (c_ * stiles + j_) * P:
                                (c_ * stiles + j_ + 1) * P],
                            psT[c_ % 2][:, :]), inc="c_trpe")
                k.wait(SC, "c_trpe")
                for c in range(NC8):
                    k.do(SC, lambda sc, c_=c, s_=s: sc.dma_start(
                        out=ap["xn_sp"][c_ * P:(c_ + 1) * P,
                                        s_ * stiles * P:(s_ + 1) * stiles * P],
                        in_=AR4[:, c_ * stiles * P:(c_ + 1) * stiles * P]),
                        inc="c_xnsp", incn=16)

                for og in range(6):
                    sl = og % 2
                    cqw = "c_qw0" if sl == 0 else "c_qw1"
                    if s * 6 + og >= 2:
                        k.wait(SY, "c_qkvp", k.count("c_qkvp") - stiles)
                    for c in range(NC8):
                        k.dma(SY, wqr_b[sl][:, c * 512:(c + 1) * 512],
                              ap["wq_bf"][c * P:(c + 1) * P,
                                          og * 512:(og + 1) * 512], cqw)
                    k.wait(TE, cqw)
                    for j in range(stiles):
                        g = og * stiles + j
                        bank = psA[g % 2]
                        nev = k.count("c_qkve")
                        if nev >= 2:
                            k.wait(TE, "c_qkve", nev - 1)
                        for c in range(NC8):
                            k.do(TE, lambda t, b=bank, s_=sl, c_=c, j_=j:
                                 t.matmul(
                                     b[:, :],
                                     AR4[:, (c_ * stiles + j_) * P:
                                         (c_ * stiles + j_ + 1) * P],
                                     wqr_b[s_][:, c_ * 512:(c_ + 1) * 512],
                                     start=(c_ == 0), stop=(c_ == NC8 - 1)),
                                 inc=("c_qkvp" if c == NC8 - 1 else None))
                        dst = (q_tm, q_tm, k_tm, k_tm, v_tm, v_tm)[og]
                        off = (og % 2) * 512
                        k.wait(VE, "c_qkvp")
                        k.do(VE, lambda v, b=bank, d=dst, j_=j, o_=off,
                             og_=og: v.tensor_tensor(
                                 d[:, j_ * D + o_:j_ * D + o_ + 512], b[:, :],
                                 bq_b[:, og_ * 512:(og_ + 1) * 512], ALU.add),
                             inc="c_qkve")

                if dbg and s == 0:
                    k.wait(SC, "c_qkve")
                    for nm, src in [("dbg_q", q_tm), ("dbg_k", k_tm),
                                    ("dbg_v", v_tm)]:
                        k.do(SC, lambda sc, n_=nm, s2=src: sc.dma_start(
                            out=ap[n_], in_=s2), inc="c_misc", incn=16)
                    k.wait(SC, "c_misc")

                for j in range(stiles):
                    qs = q_tm[:, j * D:(j + 1) * D]
                    ks = k_tm[:, j * D:(j + 1) * D]
                    k.wait(SC, "c_qkve")
                    k.wait(SC, "c_lnv")
                    k.do(SC, lambda sc, q_=qs: sc.activation(
                        sq_f[0][:], q_, AF.Square))
                    k.do(SC, lambda sc, k_=ks: sc.activation(
                        sq_f[1][:], k_, AF.Square), inc="c_lna")
                    k.wait(VE, "c_lna")
                    k.do(VE, lambda v, j_=j: v.reduce_sum(
                        stats[:, 16 + 2 * j_:17 + 2 * j_], sq_f[0][:], AX.X))
                    k.do(VE, lambda v, j_=j: v.reduce_sum(
                        stats[:, 17 + 2 * j_:18 + 2 * j_], sq_f[1][:], AX.X),
                        inc="c_lnv", dr=True)
                k.do(VE, lambda v: v.tensor_scalar(
                    stats[:, 16:16 + 2 * stiles], stats[:, 16:16 + 2 * stiles],
                    1.0 / D, RMS_EPS, ALU.mult, ALU.add), inc="c_lnv",
                    dr=True)
                rsqrt_refined(k, 16, 24, 2 * stiles)

                for j in range(stiles):
                    jt = s * stiles + j
                    sl = jt % 2
                    crt = "c_rt0" if sl == 0 else "c_rt1"
                    if jt >= 2:
                        k.wait(SY, "c_toka", k.count("c_toka") - 1)
                    k.dma(SY, rc_f[sl][:], ap["rc"][jt * P:(jt + 1) * P, :],
                          crt)
                    k.dma(SY, rs_f[sl][:], ap["rs"][jt * P:(jt + 1) * P, :],
                          crt)
                    qs = q_tm[:, j * D:(j + 1) * D]
                    ks = k_tm[:, j * D:(j + 1) * D]
                    vs = v_tm[:, j * D:(j + 1) * D]
                    for (src_tm, gg, dst_hm, ridx) in [
                            (qs, gq_b, wk3_b, 24 + 2 * j),
                            (ks, gk_b, kf_b[sl], 25 + 2 * j)]:
                        k.wait(VE, crt)
                        k.wait(VE, "c_lna")
                        if dst_hm is wk3_b:
                            k.wait(SC, "c_qft")
                        else:
                            k.wait(SC, "c_vkmm")
                        k.do(VE, lambda v, s_=src_tm, g_=gg: v.tensor_tensor(
                            wk1_b[:], s_, g_[:, :], ALU.mult), dr=True)
                        k.do(VE, lambda v, s_=sl: v.tensor_tensor(
                            wk2_b[:, 0:512], wk1_b[:, 0:512], rc_f[s_][:],
                            ALU.mult))
                        k.do(VE, lambda v, s_=sl: v.tensor_tensor(
                            sq_bv[s_][:, 0:512], wk1_b[:, 512:1024],
                            rs_f[s_][:], ALU.mult), dr=True)
                        k.do(VE, lambda v, s_=sl: v.tensor_sub(
                            wk2_b[:, 0:512], wk2_b[:, 0:512],
                            sq_bv[s_][:, 0:512]))
                        k.do(VE, lambda v, s_=sl: v.tensor_tensor(
                            sq_bv[s_][:, 512:1024], wk1_b[:, 0:512],
                            rs_f[s_][:], ALU.mult), dr=True)
                        k.do(VE, lambda v, s_=sl: v.tensor_tensor(
                            wk2_b[:, 512:1024], wk1_b[:, 512:1024],
                            rc_f[s_][:], ALU.mult), dr=True)
                        k.do(VE, lambda v, s_=sl: v.tensor_add(
                            wk2_b[:, 512:1024], wk2_b[:, 512:1024],
                            sq_bv[s_][:, 512:1024]), inc="c_lnv", dr=True)
                        k.wait(SC, "c_lnv")
                        if dbg and jt == 0 and dst_hm is not wk3_b:
                            k.do(SC, lambda sc: sc.dma_start(
                                out=ap["dbg_wk1"], in_=wk1_b[:, :]),
                                inc="c_misc", incn=16)
                            k.do(SC, lambda sc: sc.dma_start(
                                out=ap["dbg_wk2"], in_=wk2_b[:, :]),
                                inc="c_misc", incn=16)
                            k.wait(SC, "c_misc")
                        k.do(SC, lambda sc, d_=dst_hm, r_=ridx: sc.activation(
                            A(d_, 0, [DH, NH], [1, 32]),
                            A(wk2_b, 0, [1, NH], [NH, 32]),
                            AF.Relu, scale=stats[:, r_:r_ + 1]))
                        k.do(SC, lambda sc, d_=dst_hm, r_=ridx: sc.activation(
                            A(d_, 32, [DH, NH], [1, 32]),
                            A(wk2_b, 512, [1, NH], [NH, 32]),
                            AF.Relu, scale=stats[:, r_:r_ + 1]), inc="c_lna")
                    k.do(VE, lambda v: v.nop(), inc="c_toka")
                    k.wait(VE, "c_qkve")
                    k.wait(VE, "c_vkmm")
                    k.do(VE, lambda v, s_=sl: v.memset(vp_b[s_][:], 1.0),
                         dr=True)
                    k.do(VE, lambda v, s_=sl, v_=vs: v.tensor_copy(
                        A(vp_b[s_], 0, [65, NH], [1, 32]),
                        A(v_.tensor, v_.offset, [1, NH], [NH, 32])))
                    k.do(VE, lambda v, s_=sl, v_=vs: v.tensor_copy(
                        A(vp_b[s_], 32, [65, NH], [1, 32]),
                        A(v_.tensor, v_.offset + 512, [1, NH], [NH, 32])), inc="c_tokv")
                    if dbg and jt == 0:
                        k.wait(SC, "c_tokv")
                        k.wait(SC, "c_lna")
                        k.do(SC, lambda sc, s_=sl: sc.dma_start(
                            out=ap["dbg_kf"], in_=kf_b[s_][:, :]),
                            inc="c_misc", incn=16)
                        k.do(SC, lambda sc, s_=sl: sc.dma_start(
                            out=ap["dbg_vp"], in_=vp_b[s_][:, :]),
                            inc="c_misc", incn=16)
                        k.do(SC, lambda sc: sc.dma_start(
                            out=ap["dbg_qf"], in_=wk3_b[:, :]),
                            inc="c_misc", incn=16)
                        k.wait(SC, "c_misc")
                    k.wait(TE, "c_tokv")
                    k.wait(TE, "c_lna")
                    for h in range(NH):
                        k.do(TE, lambda t, s_=sl, h_=h,
                             f=(jt == 0 and h % 8 == 0),
                             l=(jt == NT - 1 and h % 8 == 7): t.matmul(
                                 psV[h_ // 8][:, (h_ % 8) * DH:
                                              (h_ % 8 + 1) * DH],
                                 vp_b[s_][:, h_ * 65:(h_ + 1) * 65],
                                 kf_b[s_][:, h_ * DH:(h_ + 1) * DH],
                                 start=f, stop=l),
                             inc=("c_vkmm" if h == NH - 1 else None))
                    for c in range(NC8):
                        nev = k.count("c_qfte")
                        if nev >= 2:
                            k.wait(TE, "c_qfte", nev - 1)
                        k.do(TE, lambda t, c_=c: t.transpose(
                            psT[c_ % 2][:, :], wk3_b[:, c_ * P:(c_ + 1) * P],
                            ident_b[:]), inc="c_qft")
                        k.wait(VE, "c_qft")
                        k.do(VE, lambda v, c_=c, j_=jt: v.tensor_copy(
                            big_a[:, c_ * T + j_ * P:c_ * T + (j_ + 1) * P],
                            psT[c_ % 2][:, :]), inc="c_qfte")


            stage_gmax(k, 0, 0)
            k.wait(SC, "c_gmax")
            k.do(SC, lambda sc: sc.dma_start(
                out=ap["cc1_i"], in_=rowst[0][:, :]), inc="c_ccst", incn=16)
            k.wait(GP, "c_ag")
            k.wait(GP, "c_ccst")
            k.do(GP, lambda g: g.collective_compute(
                "AllReduce", ALU.max, all_group,
                ins=[ap["cc1_i"]], outs=[ap["cc1_o"]]), inc="c_cc")
            k.wait(SC, "c_vkmm")
            k.do(SC, lambda sc: sc.activation(vk_f[:, 0:8 * DH], psV[0][:, :],
                                              AF.Copy))
            k.do(SC, lambda sc: sc.activation(vk_f[:, 8 * DH:NH * DH],
                                              psV[1][:, :], AF.Copy),
                 inc="c_vkev")
            k.wait(SC, "c_vkev")
            k.do(SC, lambda sc: sc.dma_start(out=ap["vk_i"], in_=vk_f[:, :]),
                 inc="c_ccst", incn=16)
            k.wait(GP, "c_ccst")
            k.do(GP, lambda g: g.collective_compute(
                "AllReduce", ALU.add, pair_groups,
                ins=[ap["vk_i"]], outs=[ap["vk_o"]]), inc="c_cc")
            k.wait(SY, "c_cc")
            k.dma(SY, vk_f[:, :], ap["vk_o"], "c_vkld")
            k.dma(SY, scal[:, 0:1], bass.AP(ap["cc1_o"].tensor, 0, [[1, P], [1, 1]]), "c_scal", slow=True)
            k.wait(VE, "c_vkld")
            k.do(VE, lambda v: v.tensor_copy(vk_bf[:, :], vk_f[:, :]),
                 inc="c_vkld")
            for h in range(NH):
                k.wait(TE, "c_vkld")
                nev = k.count("c_vktre")
                if nev >= 2:
                    k.wait(TE, "c_vktre", nev - 1)
                k.do(TE, lambda t, h_=h: t.transpose(
                    psT[h_ % 2][0:DH, 0:65], vk_bf[:, h_ * DH:(h_ + 1) * DH],
                    ident_b[0:65, 0:65]), inc="c_vktr")
                k.wait(VE, "c_vktr")
                k.do(VE, lambda v, h_=h: v.tensor_copy(
                    vkT_b[DH * (h_ % 2):DH * (h_ % 2) + DH,
                          (h_ // 2) * 65:(h_ // 2) * 65 + 65],
                    psT[h_ % 2][0:DH, 0:65]), inc="c_vktre")
            k.wait(VE, "c_scal")
            k.do(VE, lambda v: v.tensor_scalar(
                scal[:, 1:2], scal[:, 0:1], ACLIP, None, ALU.max), dr=True)
            k.do(VE, lambda v: v.reciprocal(scal[:, 2:3], scal[:, 1:2]), dr=True)
            k.do(VE, lambda v: v.tensor_scalar(
                scal[:, 3:4], scal[:, 2:3], QMAX, None, ALU.mult), dr=True)
            k.do(VE, lambda v: v.tensor_scalar(
                scal[:, 4:5], scal[:, 1:2], 1.0 / QMAX, None, ALU.mult),
                inc="c_scal", dr=True)

            k.wait(SY, "c_bncst")
            k.wait(SY, "c_bncst1")
            k.wait(SY, "c_p3v")
            for c in range(NC8):
                k.dma(SY, wor_b[:, c * D:(c + 1) * D],
                      ap["wo_bf"][c * P:(c + 1) * P, :], "c_wor")
            def res_A(jt):
                sl = jt % 2
                for h in range(NH):
                    bank = psA[h % 2]
                    nev = k.count("c_rese")
                    if nev >= 2:
                        k.wait(TE, "c_rese", nev - 1)
                    if jt == 0 and h == 0:
                        k.wait(TE, "c_vktre")
                        k.wait(TE, "c_qfte")
                    k.do(TE, lambda t, h_=h, j_=jt, b=bank: t.matmul(
                        b[:, 0:65],
                        big_a[DH * (h_ % 2):DH * (h_ % 2) + DH,
                              (h_ // 2) * T + j_ * P:
                              (h_ // 2) * T + (j_ + 1) * P],
                        vkT_b[DH * (h_ % 2):DH * (h_ % 2) + DH,
                              (h_ // 2) * 65:(h_ // 2) * 65 + 65],
                        start=True, stop=True), inc="c_resp")
                    k.wait(SC, "c_resp")
                    if jt >= 2 and h == 0:
                        k.wait(SC, "c_resd", k.count("c_resd") - 1)
                    k.do(SC, lambda sc, b=bank, h_=h, s_=sl: sc.activation(
                        resst_p[s_][:, h_ * 65:(h_ + 1) * 65], b[:, 0:65],
                        AF.Copy))
                    k.do(SC, lambda sc, b=bank, h_=h, s_=sl: sc.activation(
                        den_p[s_][:, h_:h_ + 1], b[:, 64:65], AF.Copy),
                        inc="c_rese")
                k.wait(VE, "c_rese")
                k.do(VE, lambda v, s_=sl: v.tensor_scalar(
                    den_p[s_][:, 0:NH], den_p[s_][:, 0:NH], EPS_LIN, None,
                    ALU.add), dr=True)
                k.do(VE, lambda v, s_=sl: v.reciprocal(
                    den_p[s_][:, 0:NH], den_p[s_][:, 0:NH]), dr=True)
                if jt >= 2:
                    k.wait(VE, "c_attt")
                for h in range(NH):
                    k.do(VE, lambda v, h_=h, s_=sl: v.tensor_scalar(
                        att_p[s_][:, h_ * DH:(h_ + 1) * DH],
                        resst_p[s_][:, h_ * 65:h_ * 65 + DH],
                        den_p[s_][:, h_:h_ + 1], None, ALU.mult),
                        inc=("c_resd" if h == NH - 1 else None),
                        dr=(h == NH - 1))

            def res_B(jt):
                sl = jt % 2
                cxr = "c_xr0" if sl == 0 else "c_xr1"
                if jt >= 2:
                    k.wait(SY, "c_aoz", k.count("c_aoz") - 1)
                k.dma(SY, x_r[sl][:], ap["x"][jt * P:(jt + 1) * P, :], cxr)
                if dbg and jt == 0:
                    k.wait(SC, "c_resd")
                    k.do(SC, lambda sc: sc.dma_start(
                        out=ap["dbg_att"], in_=att_p[0][:, 0:D]),
                        inc="c_misc", incn=16)
                    k.wait(SC, "c_misc")
                for c in range(NC8):
                    nev = k.count("c_atte")
                    if nev >= 2:
                        k.wait(TE, "c_atte", nev - 1)
                    if c == 0:
                        nrd = k.count("c_resd")
                        k.wait(TE, "c_resd", nrd - 1 if jt < NT - 1 else nrd)
                    k.do(TE, lambda t, c_=c, s_=sl: t.transpose(
                        psT[c_ % 2][:, :], att_p[s_][:, c_ * P:(c_ + 1) * P],
                        ident_b[:]), inc="c_attt")
                    k.wait(SC, "c_attt")
                    k.do(SC, lambda sc, c_=c, s_=sl: sc.activation(
                        att_fm_p[s_][:, c_ * P:(c_ + 1) * P],
                        psT[c_ % 2][:, :], AF.Copy), inc="c_atte")
                if jt == 0:
                    k.wait(TE, "c_wor")
                for og in range(2):
                    bank = psA[2 + og]
                    nev = k.count("c_woe")
                    if nev >= 2:
                        k.wait(TE, "c_woe", nev - 1)
                    if og == 0:
                        k.wait(TE, "c_atte")
                    for c in range(NC8):
                        k.do(TE, lambda t, b=bank, c_=c, og_=og, s_=sl:
                             t.matmul(
                                 b[:, :], att_fm_p[s_][:, c_ * P:(c_ + 1) * P],
                                 wor_b[:, c_ * D + og_ * 512:
                                       c_ * D + (og_ + 1) * 512],
                                 start=(c_ == 0), stop=(c_ == NC8 - 1)),
                             inc=("c_wop" if c == NC8 - 1 else None))
                    k.wait(VE, "c_wop")
                    if og == 0 and jt >= 2:
                        k.wait(VE, "c_aost", k.count("c_aost") - 16)
                    k.do(VE, lambda v, b=bank, og_=og, s_=sl: v.tensor_tensor(
                        ffn_p[s_][:, og_ * 512:(og_ + 1) * 512], b[:, :],
                        bo_b[:, og_ * 512:(og_ + 1) * 512], ALU.add),
                        inc="c_woe", dr=(og == 1))
                if dbg and jt == 0:
                    k.wait(SC, "c_woe")
                    k.do(SC, lambda sc: sc.dma_start(
                        out=ap["dbg_z"], in_=ffn_p[0][:, 0:D]),
                        inc="c_misc", incn=16)
                    k.wait(SC, "c_misc")
                k.do(VE, lambda v, s_=sl: v.tensor_tensor(
                    sq_f[s_][:], ffn_p[s_][:, 0:D], ffn_p[s_][:, 0:D],
                    ALU.mult))
                k.do(VE, lambda v, s_=sl: v.tensor_tensor(
                    z_p[s_][:, 0:D], ffn_p[s_][:, 0:D], gout_b[:, :],
                    ALU.mult), dr=True)
                k.do(VE, lambda v, s_=sl: v.reduce_sum(
                    stats[:, 49 + s_:50 + s_], sq_f[s_][:], AX.X), dr=True)
                k.do(VE, lambda v, s_=sl: v.tensor_scalar(
                    stats[:, 49 + s_:50 + s_], stats[:, 49 + s_:50 + s_],
                    1.0 / D, RMS_EPS, ALU.mult, ALU.add), inc="c_lnv",
                    dr=True)
                rsqrt_refined(k, 49 + sl, 53 + sl, 1)
                k.do(VE, lambda v, s_=sl: v.tensor_scalar(
                    ao_b[s_][:], z_p[s_][:, 0:D],
                    stats[:, 53 + s_:54 + s_], None, ALU.mult), dr=True)
                k.wait(VE, cxr)
                k.do(VE, lambda v, s_=sl: v.tensor_tensor(
                    ffn_p[s_][:, 0:D], x_r[s_][:], ao_b[s_][:], ALU.add),
                    inc="c_aoz")
                k.wait(SC, "c_aoz")
                k.do(SC, lambda sc, s_=sl, j_=jt: sc.dma_start(
                    out=ap["ao_sp"][j_ * P:(j_ + 1) * P, :],
                    in_=ffn_p[s_][:, 0:D]), inc="c_aost", incn=16)

            for it in range(NT + 1):
                if it < NT:
                    res_A(it)
                if it >= 1:
                    res_B(it - 1)

            for eng in (SY, SC, VE, TE):
                k.wait(eng, "c_aost")
                k.wait(eng, "c_qfte")
                k.wait(eng, "c_xnsp")
            k.wait(SC, "c_qkvp")
            k.wait(VE, "c_resp")
            k.wait(VE, "c_wop")
            for c in range(NC8):
                sl = c % 2
                cx = "c_xnr0" if sl == 0 else "c_xnr1"
                if c >= 2:
                    k.wait(SY, "c_x1c")
                k.dma(SY, ld_b[sl], ap["xn_sp"][c * P:(c + 1) * P, :], cx)
                k.wait(SC, cx)
                k.wait(SC, "c_scal")
                k.wait(SC, "c_x1c")
                k.do(SC, lambda sc, s_=sl: sc.activation(
                    iti[:, 0:T], ld_b[s_], AF.Copy, scale=scal[:, 3:4]),
                    inc="c_lna")
                k.wait(VE, "c_lna")
                k.do(VE, lambda v, c_=c: v.tensor_copy(
                    big_a[:, c_ * T:(c_ + 1) * T], iti[:, 0:T]), inc="c_x1c")
            for ti, (agname, spname, mcol) in enumerate(
                    [("wuag_o", "sp_up", 1), ("wgag_o", "sp_gt", 2)]):
                for oc in range(OCF):
                    g = ti * OCF + oc
                    sl = g % 2
                    cw = "c_wl0" if sl == 0 else "c_wl1"
                    rank = oc // WU_OT
                    within = oc % WU_OT
                    if g >= 2:
                        k.wait(SY, "c_ffp", k.count("c_ffp") - NO5)
                    if g == 0:
                        k.wait(SY, "c_ag")
                    W_ = WU_OT * P
                    k.do(SY, lambda e, s_=sl, r_=rank, w_=within, a_=agname:
                         e.dma_start(
                             out=wl_b[s_][:, :].rearrange(
                                 "p (c q) -> p c q", c=NC8),
                             in_=bass.AP(ap[a_].tensor,
                                         r_ * D * W_ + w_ * P,
                                         [[W_, P], [P * W_, NC8], [1, P]])),
                         inc=cw, incn=16)
                    k.wait(TE, cw)
                    k.wait(TE, "c_x1c")
                    for t5 in range(NO5):
                        bank = psA[k.count("c_ffp") % 2]
                        nev = k.count("c_ffe")
                        if nev >= 2:
                            k.wait(TE, "c_ffe", nev - 1)
                        for c in range(NC8):
                            k.do(TE, lambda t, b=bank, s_=sl, c_=c, t_=t5:
                                 t.matmul(
                                     b[:, :],
                                     wl_b[s_][:, c_ * P:(c_ + 1) * P],
                                     big_a[:, c_ * T + t_ * 512:
                                           c_ * T + (t_ + 1) * 512],
                                     start=(c_ == 0), stop=(c_ == NC8 - 1)),
                                 inc=("c_ffp" if c == NC8 - 1 else None))
                        k.wait(SC, "c_ffp")
                        if t5 == 0:
                            k.wait(SC, "c_ffst" if sl == 0 else "c_ffst1")
                        k.do(SC, lambda sc, b=bank, s_=sl, t_=t5:
                             sc.activation(
                                 A(st_b[s_].tensor, st_b[s_].offset + t_ * 512, [1, 512]),
                                 b[:, :], AF.Copy), inc="c_ffe")
                    k.wait(VE, "c_ffe")
                    k.do(VE, lambda v, s_=sl: v.tensor_reduce(
                        stats[:, 60:61], st_b[s_], AX.X, ALU.max,
                        apply_absolute_value=True), dr=True)
                    k.do(VE, lambda v, m_=mcol: v.tensor_tensor(
                        maxacc[:, m_:m_ + 1], maxacc[:, m_:m_ + 1],
                        stats[:, 60:61], ALU.max), inc="c_ffv", dr=True)
                    k.wait(SC, "c_ffv")
                    cst = "c_ffst" if sl == 0 else "c_ffst1"
                    k.do(SC, lambda sc, s_=sl, o_=oc, sp_=spname:
                         sc.dma_start(out=ap[sp_][o_ * P:(o_ + 1) * P, :],
                                      in_=st_b[s_]), inc=cst, incn=16)

            stage_gmax(k, 1, 0)
            stage_gmax(k, 2, 1)
            k.wait(SC, "c_gmax")
            k.do(SC, lambda sc: sc.dma_start(
                out=ap["cc3_i"][0:1, :], in_=rowst[0][:, :]),
                inc="c_ccst", incn=16)
            k.do(SC, lambda sc: sc.dma_start(
                out=ap["cc3_i"][1:2, :], in_=rowst[1][:, :]),
                inc="c_ccst", incn=16)
            k.wait(GP, "c_ccst")
            k.do(GP, lambda g: g.collective_compute(
                "AllReduce", ALU.max, all_group,
                ins=[ap["cc3_i"]], outs=[ap["cc3_o"]]), inc="c_cc")
            k.wait(SY, "c_cc")
            k.dma(SY, scal[:, 5:6], bass.AP(ap["cc3_o"].tensor, 0, [[1, P], [1, 1]]), "c_scal", slow=True)
            k.dma(SY, scal[:, 6:7], bass.AP(ap["cc3_o"].tensor, P, [[1, P], [1, 1]]), "c_scal", slow=True)
            k.wait(VE, "c_scal")
            k.do(VE, lambda v: v.tensor_scalar(
                scal[:, 5:7], scal[:, 5:7], ACLIP, None, ALU.max), dr=True)
            k.do(VE, lambda v: v.reciprocal(scal[:, 7:9], scal[:, 5:7]), dr=True)
            k.do(VE, lambda v: v.tensor_scalar(
                scal[:, 7:9], scal[:, 7:9], QMAX, None, ALU.mult), dr=True)
            k.do(VE, lambda v: v.tensor_tensor(
                scal[:, 9:10], scal[:, 4:5], scal[:, 6:7], ALU.mult), dr=True)
            k.do(VE, lambda v: v.tensor_scalar(
                scal[:, 9:10], scal[:, 9:10], 1.0 / QMAX, None, ALU.mult),
                inc="c_scal", dr=True)

            for eng in (SY, SC, VE):
                k.wait(eng, "c_ffst")
                k.wait(eng, "c_ffst1")
                k.wait(eng, "c_x1c")
            k.wait(VE, "c_ffp")
            for oc in range(OCF):
                sl = oc % 2
                cl = "c_ld0" if sl == 0 else "c_ld1"
                if oc >= 2:
                    k.wait(SY, "c_p3st" if sl == 0 else "c_p3st1")
                k.dma(SY, ld_b[sl], ap["sp_gt"][oc * P:(oc + 1) * P, :], cl)
                k.wait(SC, cl)
                k.wait(SC, "c_scal")
                k.wait(SC, "c_p3v" if sl == 0 else "c_p3v1")
                k.do(SC, lambda sc, s_=sl: sc.activation(
                    iti_v[s_], ld_b[s_], AF.Copy, scale=scal[:, 8:9]))
                k.do(SC, lambda sc, s_=sl: sc.activation(
                    sig_v[s_], iti_v[s_], AF.Sigmoid,
                    scale=scal[:, 9:10]), inc="c_p3a")
                k.wait(VE, "c_p3a")
                k.do(VE, lambda v, s_=sl: v.tensor_scalar(
                    iti2_v[s_], sig_v[s_], 65536.0, None, ALU.mult), dr=True)
                k.do(VE, lambda v, s_=sl: v.tensor_tensor(
                    st_b[s_], iti_v[s_], iti2_v[s_], ALU.mult))
                k.do(VE, lambda v, s_=sl: v.tensor_reduce(
                    stats[:, 61:62], st_b[s_], AX.X, ALU.max,
                    apply_absolute_value=True), dr=True)
                k.do(VE, lambda v: v.tensor_tensor(
                    maxacc[:, 3:4], maxacc[:, 3:4], stats[:, 61:62],
                    ALU.max), inc=("c_p3v" if sl == 0 else "c_p3v1"),
                    dr=True)
                k.wait(SC, "c_p3v" if sl == 0 else "c_p3v1")
                k.do(SC, lambda sc, s_=sl, o_=oc: sc.dma_start(
                    out=ap["sp_g2"][o_ * P:(o_ + 1) * P, :], in_=st_b[s_]),
                    inc=("c_p3st" if sl == 0 else "c_p3st1"), incn=16)

            k.wait(GP, "c_p3v1")
            stage_gmax(k, 3, 0)
            k.wait(SC, "c_gmax")
            k.do(SC, lambda sc: sc.dma_start(
                out=ap["cc4_i"], in_=rowst[0][:, :]), inc="c_ccst", incn=16)
            k.wait(GP, "c_ccst")
            k.do(GP, lambda g: g.collective_compute(
                "AllReduce", ALU.max, all_group,
                ins=[ap["cc4_i"]], outs=[ap["cc4_o"]]), inc="c_cc")
            k.wait(SY, "c_cc")
            k.dma(SY, scal[:, 10:11], bass.AP(ap["cc4_o"].tensor, 0, [[1, P], [1, 1]]), "c_scal", slow=True)
            k.wait(VE, "c_scal")
            k.do(VE, lambda v: v.tensor_scalar(
                scal[:, 10:11], scal[:, 10:11], ACLIP, None, ALU.max), dr=True)
            k.do(VE, lambda v: v.reciprocal(scal[:, 11:12], scal[:, 10:11]), dr=True)
            k.do(VE, lambda v: v.tensor_scalar(
                scal[:, 11:12], scal[:, 11:12], QMAX, None, ALU.mult),
                inc="c_scal", dr=True)

            k.wait(SY, "c_p3st")
            k.wait(SY, "c_p3st1")
            for oc in range(OCF):
                sl = oc % 2
                cl = "c_ld0" if sl == 0 else "c_ld1"
                cl2 = "c_ld20" if sl == 0 else "c_ld21"
                if oc >= 2:
                    k.wait(SY, "c_p4st" if sl == 0 else "c_p4st1")
                k.dma(SY, ld_b[sl], ap["sp_g2"][oc * P:(oc + 1) * P, :], cl)
                k.dma(SY, ld2_b[sl], ap["sp_up"][oc * P:(oc + 1) * P, :], cl2)
                k.wait(SC, cl)
                k.wait(SC, cl2)
                k.wait(SC, "c_scal")
                k.wait(SC, "c_p4v" if sl == 0 else "c_p4v1")
                k.do(SC, lambda sc, s_=sl: sc.activation(
                    iti_v[s_], ld_b[s_], AF.Copy, scale=scal[:, 11:12]))
                k.do(SC, lambda sc, s_=sl: sc.activation(
                    iti2_v[s_], ld2_b[s_], AF.Copy, scale=scal[:, 7:8]),
                    inc="c_p4a")
                k.wait(VE, "c_p4a")
                k.do(VE, lambda v, s_=sl: v.tensor_tensor(
                    st_b[s_], iti_v[s_], iti2_v[s_], ALU.mult))
                k.do(VE, lambda v, s_=sl: v.tensor_reduce(
                    stats[:, 61:62], st_b[s_], AX.X, ALU.max,
                    apply_absolute_value=True), dr=True)
                k.do(VE, lambda v: v.tensor_tensor(
                    maxacc[:, 4:5], maxacc[:, 4:5], stats[:, 61:62],
                    ALU.max), inc=("c_p4v" if sl == 0 else "c_p4v1"),
                    dr=True)
                k.wait(SC, "c_p4v" if sl == 0 else "c_p4v1")
                k.do(SC, lambda sc, s_=sl, o_=oc: sc.dma_start(
                    out=ap["sp_o"][o_ * P:(o_ + 1) * P, :], in_=st_b[s_]),
                    inc=("c_p4st" if sl == 0 else "c_p4st1"), incn=16)

            k.wait(GP, "c_p4v1")
            stage_gmax(k, 4, 0)
            k.wait(SC, "c_gmax")
            k.do(SC, lambda sc: sc.dma_start(
                out=ap["cc5_i"], in_=rowst[0][:, :]), inc="c_ccst", incn=16)
            k.wait(GP, "c_ccst")
            k.do(GP, lambda g: g.collective_compute(
                "AllReduce", ALU.max, all_group,
                ins=[ap["cc5_i"]], outs=[ap["cc5_o"]]), inc="c_cc")
            k.wait(SY, "c_cc")
            k.dma(SY, scal[:, 12:13], bass.AP(ap["cc5_o"].tensor, 0, [[1, P], [1, 1]]), "c_scal", slow=True)
            k.wait(VE, "c_scal")
            k.do(VE, lambda v: v.tensor_scalar(
                scal[:, 12:13], scal[:, 12:13], ACLIP, None, ALU.max), dr=True)
            k.do(VE, lambda v: v.reciprocal(scal[:, 13:14], scal[:, 12:13]), dr=True)
            k.do(VE, lambda v: v.tensor_scalar(
                scal[:, 13:14], scal[:, 13:14], QMAX, None, ALU.mult), dr=True)
            k.do(VE, lambda v: v.tensor_tensor(
                scal[:, 14:15], scal[:, 4:5], scal[:, 5:6], ALU.mult), dr=True)
            k.do(VE, lambda v: v.tensor_scalar(
                scal[:, 14:15], scal[:, 14:15], 1.0 / QMAX, None, ALU.mult), dr=True)
            k.do(VE, lambda v: v.tensor_tensor(
                scal[:, 15:16], scal[:, 9:10], scal[:, 10:11], ALU.mult), dr=True)
            k.do(VE, lambda v: v.tensor_scalar(
                scal[:, 15:16], scal[:, 15:16], 1.0 / (65536.0 * QMAX), None,
                ALU.mult), dr=True)
            k.do(VE, lambda v: v.tensor_tensor(
                scal[:, 16:17], scal[:, 14:15], scal[:, 15:16], ALU.mult), dr=True)
            k.do(VE, lambda v: v.tensor_tensor(
                scal[:, 16:17], scal[:, 16:17], scal[:, 12:13], ALU.mult), dr=True)
            k.do(VE, lambda v: v.tensor_scalar(
                scal[:, 16:17], scal[:, 16:17], 1.0 / QMAX, None, ALU.mult),
                inc="c_scal", dr=True)

            for eng in (SY, SC, VE, TE):
                k.wait(eng, "c_ffp")
            k.wait(SY, "c_p4st")
            k.wait(SY, "c_p4st1")
            W_ = WD_OT * P
            for r in range(n_cores):
                k.do(SY, lambda e, r_=r: e.dma_start(
                    out=A(wdr_b.tensor, wdr_b.offset + r_ * W_,
                          [D, OCF], [1, W_]),
                    in_=bass.AP(ap["wdag_o"].tensor, r_ * DF * W_,
                                [[W_, P], [P * W_, OCF], [1, W_]])),
                    inc="c_wdr", incn=16)
            for jt in range(NT):
                sl = jt % 2
                co = "c_ol0" if sl == 0 else "c_ol1"
                cxr = "c_xr0" if sl == 0 else "c_xr1"
                car = "c_ar0" if sl == 0 else "c_ar1"
                if jt >= 2:
                    k.wait(SY, "c_yst" if sl == 0 else "c_yst1")
                    k.wait(SY, "c_o1a", k.count("c_o1a") - 1)
                for (eng, q0, qn) in ((SY, 0, 2 * OCF // 3),
                                      (SC, 2 * OCF // 3, OCF // 3)):
                    k.do(eng, lambda e, s_=sl, j_=jt, a_=q0, n_=qn:
                         e.dma_start(
                             out=o_raw[s_][:, a_ * P:(a_ + n_) * P]
                             .rearrange("p (a b) -> p a b", a=n_),
                             in_=bass.AP(ap["sp_o"].tensor,
                                         j_ * P + a_ * P * T,
                                         [[T, P], [P * T, n_], [1, P]])),
                         inc=co, incn=16)
                k.dma(SY, x_f[sl][:], ap["ao_sp"][jt * P:(jt + 1) * P, :],
                      cxr)
                k.wait(SC, co)
                k.wait(SC, "c_scal")
                if jt >= 2:
                    k.wait(SC, "c_o1v" if sl == 0 else "c_o1v1")
                k.do(SC, lambda sc, s_=sl: sc.activation(
                    iti_p[s_], o_raw[s_], AF.Copy, scale=scal[:, 13:14]),
                    inc="c_o1a")
                k.wait(VE, "c_o1a")
                if jt >= 2:
                    k.wait(VE, "c_dwp", k.count("c_dwp") - 2)
                k.do(VE, lambda v, s_=sl: v.tensor_copy(
                    o1_b[s_], iti_p[s_]),
                    inc=("c_o1v" if sl == 0 else "c_o1v1"))
                k.wait(TE, "c_o1v" if sl == 0 else "c_o1v1")
                k.wait(TE, "c_wdr")
                for og in range(2):
                    bank = psA[og]
                    nev = k.count("c_dwe")
                    if nev >= 2:
                        k.wait(TE, "c_dwe", nev - 1)
                    for fc in range(OCF):
                        k.do(TE, lambda t, b=bank, s_=sl, f_=fc, og_=og:
                             t.matmul(b[:, :],
                                      o1_b[s_][:, f_ * P:(f_ + 1) * P],
                                      wdr_b[:, f_ * D + og_ * 512:
                                            f_ * D + (og_ + 1) * 512],
                                      start=(f_ == 0), stop=(f_ == OCF - 1)),
                             inc=("c_dwp" if fc == OCF - 1 else None))
                    k.wait(SC, "c_dwp")
                    k.wait(SC, "c_scal")
                    k.do(SC, lambda sc, b=bank, og_=og: sc.activation(
                        ffn_f[:, og_ * 512:(og_ + 1) * 512], b[:, :],
                        AF.Copy, scale=scal[:, 16:17]), inc="c_dwe")
                k.wait(VE, "c_dwe")
                k.wait(VE, cxr)
                k.do(VE, lambda v, s_=sl: v.tensor_tensor(
                    y_f[s_][:, :], x_f[s_][:], ffn_f[:, 0:D], ALU.add),
                    inc="c_res2")
                k.wait(SC, "c_res2")
                k.do(SC, lambda sc, s_=sl, j_=jt: sc.dma_start(
                    out=yout[j_ * P:(j_ + 1) * P, :], in_=y_f[s_][:, :]),
                    inc=("c_yst" if sl == 0 else "c_yst1"), incn=16)
            k.wait(SY, "c_yst")
            k.wait(SY, "c_yst1")
            if dbg:
                k.dma(SY, ap["dbg_xn"], ap["xn_sp"], "c_misc")
                k.dma(SY, ap["dbg_vki"], ap["vk_i"], "c_misc")
                k.dma(SY, ap["dbg_ao"], ap["ao_sp"], "c_misc")
                k.dma(SY, ap["dbg_up"], ap["sp_up"], "c_misc")
                k.do(SY, lambda e: e.dma_start(out=ap["dbg_scal"],
                                               in_=scal[:, :]),
                     inc="c_misc", incn=16)
                k.wait(SY, "c_misc")

        @block.sync
        def _(eng):
            program(K(SY, eng, sems))

        @block.scalar
        def _(eng):
            program(K(SC, eng, sems))

        @block.vector
        def _(eng):
            program(K(VE, eng, sems))

        @block.tensor
        def _(eng):
            program(K(TE, eng, sems))

        @block.gpsimd
        def _(eng):
            program(K(GP, eng, sems))

    return nc



def _sigma_perm():
    perm = np.empty(D, dtype=np.int64)
    for h in range(NH):
        for i in range(32):
            perm[i * 16 + h] = h * DH + 2 * i
            perm[512 + i * 16 + h] = h * DH + 2 * i + 1
    return perm


def _tau_perm():
    perm = np.empty(D, dtype=np.int64)
    for h in range(NH):
        for j2 in range(DH):
            orig = 2 * j2 if j2 < 32 else 2 * (j2 - 32) + 1
            perm[h * DH + j2] = h * DH + orig
    return perm


def _rope_tables(pos0, T):
    inv_freq = 1.0 / (10000.0 ** (np.arange(0, DH, 2, dtype=np.float64) / DH))
    t = np.arange(pos0, pos0 + T, dtype=np.float64)
    freqs = np.outer(t, inv_freq)
    cos_e = np.repeat(np.cos(freqs), 16, axis=1).astype(ml_dtypes.bfloat16)
    sin_e = np.repeat(np.sin(freqs), 16, axis=1).astype(ml_dtypes.bfloat16)
    return np.ascontiguousarray(cos_e), np.ascontiguousarray(sin_e)


_BUILD_CACHE = {}


def kernel(L0_attn_x, L0_attn_Wqkv, L0_attn_bqkv, L0_attn_gq, L0_attn_gk,
           L0_attn_Wout, L0_attn_bout, L0_attn_gout, L0_ffn_w_gate,
           L0_ffn_w_up, L0_ffn_w_down, _n_cores=8, _T=None, _stiles=4,
           _dbg=False):
    B, L, _ = L0_attn_x.shape
    n_cores = _n_cores
    T = _T if _T is not None else (B * L) // n_cores
    key = (T, n_cores, _stiles, _dbg)
    if key not in _BUILD_CACHE:
        _BUILD_CACHE[key] = build_nc(T, n_cores, _stiles, dbg=_dbg)
    nc = _BUILD_CACHE[key]

    sig = _sigma_perm()
    tau = _tau_perm()
    f32 = np.float32

    wq = np.ascontiguousarray(np.asarray(L0_attn_Wqkv, dtype=f32).T)
    wqp = np.empty_like(wq)
    bq = np.asarray(L0_attn_bqkv, dtype=f32)
    bqp = np.empty_like(bq)
    for g in range(3):
        wqp[:, g * D:(g + 1) * D] = wq[:, g * D:(g + 1) * D][:, sig]
        bqp[g * D:(g + 1) * D] = bq[g * D:(g + 1) * D][sig]
    wo = np.ascontiguousarray(np.asarray(L0_attn_Wout, dtype=f32).T)
    wop = np.ascontiguousarray(wo[tau, :])
    gq = np.asarray(L0_attn_gq, dtype=f32)[sig]
    gk = np.asarray(L0_attn_gk, dtype=f32)[sig]
    gout = np.asarray(L0_attn_gout, dtype=f32)
    bout = np.asarray(L0_attn_bout, dtype=f32)

    def rep(vec):
        return np.ascontiguousarray(np.broadcast_to(vec, (P, vec.shape[0])))

    wu_o = np.asarray(L0_ffn_w_up, dtype=f32)
    wg_o = np.asarray(L0_ffn_w_gate, dtype=f32)
    wd_o = np.asarray(L0_ffn_w_down, dtype=f32)
    ident = np.eye(P, dtype=f32)

    x = np.asarray(L0_attn_x, dtype=f32)
    wu_rows = DF // n_cores
    wd_rows = D // n_cores

    in_maps = []
    for c in range(n_cores):
        b, h = c // 2, c % 2
        if n_cores > 1:
            xs = np.ascontiguousarray(x[b, h * T:(h + 1) * T, :])
            pos0 = h * T
        else:
            xs = np.ascontiguousarray(x.reshape(-1, D)[:T])
            pos0 = 0
        cos_e, sin_e = _rope_tables(pos0, T)
        in_maps.append({
            "x": xs, "rc": cos_e, "rs": sin_e,
            "wqkvT": wqp, "bqkv_rep": rep(bqp),
            "gq_rep": rep(gq), "gk_rep": rep(gk),
            "woutT": wop, "bout_rep": rep(bout), "gout_rep": rep(gout),
            "wu_slice": np.ascontiguousarray(
                wu_o[c * wu_rows:(c + 1) * wu_rows]),
            "wg_slice": np.ascontiguousarray(
                wg_o[c * wu_rows:(c + 1) * wu_rows]),
            "wd_slice": np.ascontiguousarray(
                wd_o[c * wd_rows:(c + 1) * wd_rows]),
            "ident": ident,
        })

    res = run_bass_kernel_spmd(nc, in_maps, core_ids=list(range(n_cores)))
    outs = [res.results[c]["y"] for c in range(n_cores)]
    y = np.empty((B, L, D), dtype=np.float32)
    if n_cores > 1:
        for c in range(n_cores):
            b, h = c // 2, c % 2
            y[b, h * T:(h + 1) * T, :] = outs[c]
    else:
        y.reshape(-1, D)[:T] = outs[0]
    return y


# version 12 (final)
# speedup vs baseline: 1.0554x; 1.0501x over previous
import numpy as np
import ml_dtypes
from contextlib import ExitStack

import concourse.bass as bass
import concourse.mybir as mybir
from concourse.bass_utils import run_bass_kernel_spmd

F32 = mybir.dt.float32
BF16 = mybir.dt.bfloat16
I32 = mybir.dt.int32
AF = mybir.ActivationFunctionType
ALU = mybir.AluOpType
AX = mybir.AxisListType

D = 1024
DF = 3072
NH = 16
DH = 64
P = 128
NC8 = D // P
OCF = DF // P
RMS_EPS = 1.1920929e-07
LN_EPS = 1e-5
EPS_LIN = 1e-6
QMAX = 32767.0
ACLIP = QMAX * 1e-12
WQMAX = 7.0
WCLIP = WQMAX * 1e-12

SY, SC, VE, TE, GP = "sync", "scalar", "vector", "tensor", "gpsimd"


def A(t, off, *dims):
    w = t.shape[1]
    return bass.AP(t, off, [[w, P]] + [list(d) for d in dims])


class Ctr:
    __slots__ = ("n",)

    def __init__(self):
        self.n = 0


class K:
    def __init__(self, cur, eng, sems):
        self.cur = cur
        self.eng = eng
        self.sems = sems
        self.ctrs = {}

    def ctr(self, name):
        if name not in self.ctrs:
            self.ctrs[name] = Ctr()
        return self.ctrs[name]

    def do(self, engname, fn, inc=None, incn=1, dr=False):
        if self.cur == engname:
            ins = fn(self.eng)
            if inc is not None:
                ins.then_inc(self.sems[inc], incn)
            if dr:
                self.eng.drain()
        if inc is not None:
            self.ctr(inc).n += incn

    def dma(self, engname, out, in_, inc, slow=False):
        self.do(engname,
                lambda e: e.dma_start(out=out, in_=in_,
                                      allow_slow_non_contiguous=slow),
                inc=inc, incn=16)

    def wait(self, engname, ctrname, val=None):
        if self.cur == engname:
            self.eng.wait_ge(self.sems[ctrname],
                             self.ctr(ctrname).n if val is None else val)

    def count(self, ctrname):
        return self.ctr(ctrname).n


def build_nc(T, n_cores, stiles, dbg=False):
    NT = T // P
    NS = NT // stiles
    NO5 = T // 512
    SD = stiles * D
    assert NT % stiles == 0 and T % 512 == 0
    WU_OT = DF // P // n_cores
    WD_OT = D // P // n_cores
    assert DF % (P * n_cores) == 0 and D % (P * n_cores) == 0
    pair_groups = ([[2 * g, 2 * g + 1] for g in range(n_cores // 2)]
                   if n_cores > 1 else [[0]])
    all_group = [list(range(n_cores))]

    nc = bass.Bass("TRN2", target_bir_lowering=False,
                   detect_race_conditions=False)
    ap = {}

    def din(name, shape, dt=F32):
        ap[name] = nc.dram_tensor(name, shape, dt, kind="ExternalInput").ap()

    def dint(name, shape, dt):
        ap[name] = nc.dram_tensor(name, shape, dt, kind="Internal").ap()

    din("x", [T, D])
    din("rc", [T, 512], BF16)
    din("rs", [T, 512], BF16)
    din("wqkvT", [D, 3 * D])
    din("bqkv_rep", [P, 3 * D])
    din("gq_rep", [P, D])
    din("gk_rep", [P, D])
    din("woutT", [D, D])
    din("bout_rep", [P, D])
    din("gout_rep", [P, D])
    din("wu_slice", [WU_OT * P, D])
    din("wg_slice", [WU_OT * P, D])
    din("wd_slice", [WD_OT * P, DF])
    din("ident", [P, P])

    yout = nc.dram_tensor("y", [T, D], F32, kind="ExternalOutput").ap()

    dint("xn_sp", [D, T], BF16)
    dint("ao_sp", [T, D], F32)
    dint("wq_bf", [D, 3 * D], BF16)
    dint("wo_bf", [D, D], BF16)
    dint("sp_up", [DF, T], BF16)
    dint("sp_gt", [DF, T], BF16)
    dint("sp_g2", [DF, T], BF16)
    dint("sp_o", [DF, T], BF16)
    for nm, w in [("cc1", 1), ("cc3", 2), ("cc4", 1), ("cc5", 1)]:
        dint(nm + "_i", [w, P], F32)
        dint(nm + "_o", [w, P], F32)
    dint("vk_i", [65, NH * DH], F32)
    dint("vk_o", [65, NH * DH], F32)
    if dbg:
        def dout(name, shape, dt):
            ap[name] = nc.dram_tensor(name, shape, dt,
                                      kind="ExternalOutput").ap()
        dout("dbg_q", [P, stiles * D], BF16)
        dout("dbg_k", [P, stiles * D], BF16)
        dout("dbg_v", [P, stiles * D], BF16)
        dout("dbg_kf", [P, D], BF16)
        dout("dbg_vp", [P, NH * 65], BF16)
        dout("dbg_qf", [P, D], BF16)
        dout("dbg_wk1", [P, D], BF16)
        dout("dbg_wk2", [P, D], BF16)
        dout("dbg_att", [P, D], BF16)
        dout("dbg_z", [P, D], F32)
        dout("dbg_xn", [D, T], BF16)
        dout("dbg_vki", [65, NH * DH], F32)
        dout("dbg_ao", [T, D], BF16)
        dout("dbg_up", [DF, T], BF16)
        dout("dbg_scal", [P, 24], F32)
        dout("dbg_st", [P, 64], F32)
        dout("dbg_xntm", [P, D], BF16)
        dout("dbg_xs", [P, D], F32)
    dint("wuag_i", [D, WU_OT * P], BF16)
    dint("wuag_o", [n_cores * D, WU_OT * P], BF16)
    dint("wgag_i", [D, WU_OT * P], BF16)
    dint("wgag_o", [n_cores * D, WU_OT * P], BF16)
    dint("wdag_i", [DF, WD_OT * P], BF16)
    dint("wdag_o", [n_cores * DF, WD_OT * P], BF16)

    with ExitStack() as stack:
        en = stack.enter_context

        def sb(name, shape, dt):
            return en(nc.sbuf_tensor(name, shape, dt))

        ar1_cols = max(NC8 * T, 3 * DF, OCF * D + 2 * DF)
        ar3_cols = max(6 * T, 5 * SD + 3 * D, 4 * DF, 10 * D + 8320)
        AR1 = sb("AR1", [P, ar1_cols], BF16)
        AR3 = sb("AR3", [P, ar3_cols], BF16)
        AR4 = sb("AR4", [P, SD], BF16)
        itcols = max(T, NC8 * 512)
        iti = sb("iti", [P, itcols], I32)
        iti2 = sb("iti2", [P, itcols], I32)
        ident_b = sb("ident_b", [P, P], BF16)
        gq_b = sb("gq_b", [P, D], BF16)
        gk_b = sb("gk_b", [P, D], BF16)
        gout_b = sb("gout_b", [P, D], BF16)
        bo_b = sb("bo_b", [P, D], BF16)
        sq_f = [sb(f"sq_f{i}", [P, D], F32) for i in range(2)]
        sq_bv = [sq_f[i][:].bitcast(BF16) for i in range(2)]
        stats = sb("stats", [P, 64], F32)
        maxacc = sb("maxacc", [P, 8], F32)
        scal = sb("scal", [P, 24], F32)
        xn_tm = [sb(f"xn_tm{i}", [P, D], BF16) for i in range(2)]
        rc_f = [sb(f"rcbuf{i}", [P, 512], BF16) for i in range(2)]
        rs_f = [sb(f"rsbuf{i}", [P, 512], BF16) for i in range(2)]
        wk1_b = sb("wk1_b", [P, D], BF16)
        wk2_b = sb("wk2_b", [P, D], BF16)
        wk3_b = sb("wk3_b", [P, D], BF16)
        kf_b = [sb(f"kf_b{i}", [P, D], BF16) for i in range(2)]
        vp_b = [sb(f"vp_b{i}", [P, NH * 65], BF16) for i in range(2)]
        vkT_b = sb("vkT_b", [P, 8 * 65], BF16)

        wl_b = [sb(f"wl_b{i}", [P, NC8 * P], BF16) for i in range(2)]
        ffn_f = sb("ffn_f", [P, D], F32)
        vk_f = ffn_f[0:65, :]
        wps_f = sb("wps_f", [P, 96], F32)
        wpi_f = sb("wpi_f", [P, 96], F32)
        wag_b = sb("wag_b", [P, P], BF16)
        onesr_f = sb("onesr_f", [1, P], F32)
        rowst = [sb(f"rowst{i}", [1, P], F32) for i in range(2)]
        colst = sb("colst", [1, 2], F32)

        q_tm = AR3[:, 0:SD]
        k_tm = AR3[:, SD:2 * SD]
        v_tm = AR3[:, 2 * SD:3 * SD]
        ld_b = [AR3[:, 0:T], AR3[:, T:2 * T]]
        x_s = AR3[:, 3 * SD:5 * SD].bitcast(F32)
        bq_b = AR3[:, 5 * SD:5 * SD + 3 * D]
        x_f = [AR3[:, 0:2 * D].bitcast(F32),
               AR3[:, 2 * D:4 * D].bitcast(F32)]
        o_raw = [AR3[:, 4 * D:4 * D + DF],
                 AR3[:, 4 * D + DF:4 * D + 2 * DF]]
        iti_p = [iti[:, 0:DF], iti2[:, 0:DF]]
        att_p = [AR3[:, 0:D], AR3[:, D:2 * D]]
        att_fm_p = [AR3[:, 2 * D:3 * D], AR3[:, 3 * D:4 * D]]
        z_p = [AR3[:, 4 * D:5 * D], AR3[:, 5 * D:6 * D]]
        ffn_p = [AR3[:, 6 * D:8 * D].bitcast(F32),
                 AR3[:, 8 * D:10 * D].bitcast(F32)]
        resst_p = [AR3[:, 10 * D:10 * D + 2080].bitcast(F32),
                   AR3[:, 10 * D + 2080:10 * D + 4160].bitcast(F32)]
        den_p = [AR3[:, 10 * D + 4160:10 * D + 4192].bitcast(F32),
                 AR3[:, 10 * D + 4192:10 * D + 4224].bitcast(F32)]
        x_r = [AR3[:, 10 * D + 4224:10 * D + 6272].bitcast(F32),
               AR3[:, 10 * D + 6272:10 * D + 8320].bitcast(F32)]
        vk_bf = AR3[0:65, 0:1024]
        ld2_b = [AR3[:, 2 * T:3 * T], AR3[:, 3 * T:4 * T]]
        st_b = [AR3[:, 4 * T:5 * T], AR3[:, 5 * T:6 * T]]
        sig_f = AR3[:, 2 * T:4 * T].bitcast(F32)
        iti_v = [iti[:, 0:T], AR1[:, 0:2 * T].bitcast(I32)]
        iti2_v = [iti2[:, 0:T], AR1[:, 2 * T:4 * T].bitcast(I32)]
        sig_v = [sig_f[:, 0:T], AR1[:, 4 * T:6 * T].bitcast(F32)]
        wp_f = [AR3[:, 0:2 * DF].bitcast(F32),
                AR3[:, 2 * DF:4 * DF].bitcast(F32)]
        tmp_f = AR1[:, 0:2 * DF].bitcast(F32)
        wq_b = AR1[:, 2 * DF:3 * DF]
        wq_b2 = AR1[:, 3 * DF:4 * DF]
        big_a = AR1
        wdr_b = AR1[:, 0:OCF * D]
        o1_b = [AR1[:, OCF * D:OCF * D + DF],
                AR1[:, OCF * D + DF:OCF * D + 2 * DF]]
        wqr_b = [iti[:, 0:itcols // 2].bitcast(BF16),
                 iti[:, itcols // 2:itcols].bitcast(BF16)]
        wor_b = iti2.bitcast(BF16)
        wkq = [wor_b[:, 0:D], wor_b[:, D:2 * D]]
        wkk = [wor_b[:, 2 * D:3 * D], wor_b[:, 3 * D:4 * D]]
        wk3_p = [wor_b[:, 4 * D:5 * D], wor_b[:, 5 * D:6 * D]]
        ao_b = [wk1_b, wk2_b]
        ar_b = kf_b
        y_f = sq_f

        psA = [en(nc.psum_tensor(f"psA{i}", [P, 512], F32)) for i in range(4)]
        psT = [en(nc.psum_tensor(f"psT{i}", [P, P], BF16)) for i in range(2)]
        psV = [en(nc.psum_tensor(f"psV{i}", [65, 8 * DH], F32)) for i in range(2)]

        sem_names = [
            "c_prol", "c_prolv", "c_bnc0", "c_bnc1", "c_bncv", "c_bncst",
            "c_wp0", "c_wp1", "c_wpv", "c_wpt", "c_wpte", "c_agst", "c_ag",
            "c_x0", "c_x1", "c_lnv", "c_lna", "c_trp", "c_trpe", "c_xnsp",
            "c_qw0", "c_qw1", "c_qkvp", "c_qkve",
            "c_rt0", "c_rt1", "c_tokv", "c_toka", "c_vkmm", "c_qft", "c_qfte",
            "c_vkev", "c_ccst", "c_cc", "c_vkld", "c_vktr", "c_vktre",
            "c_resp", "c_rese", "c_attt", "c_atte", "c_wop", "c_woe", "c_wor",
            "c_aost",
            "c_xnr0", "c_xnr1", "c_x1c",
            "c_wl0", "c_wl1", "c_ffp", "c_ffe", "c_ffv", "c_ffst",
            "c_ld0", "c_ld1", "c_p3a", "c_p3v", "c_p3st",
            "c_ld20", "c_ld21", "c_p4a", "c_p4v", "c_p4st",
            "c_o1a", "c_o1v", "c_o1st", "c_ol0", "c_ol1", "c_wdr",
            "c_dwp", "c_dwe", "c_xr0", "c_xr1", "c_ar0", "c_ar1",
            "c_res2", "c_yst", "c_misc", "c_scal", "c_gmax", "c_ffst1", "c_p3st1", "c_p4st1", "c_o1st1", "c_yst1", "c_bncst1", "c_p3v1", "c_p4v1", "c_o1v1",
            "c_resd", "c_zcp", "c_aoz", "c_rlq", "c_rlk",
        ]
        sems = {n: en(nc.semaphore(n)) for n in sem_names}
        block = en(nc.Block())


        def stage_gmax(k, mcol, row):
            k.wait(GP, "c_lnv")
            k.wait(GP, "c_ffv")
            k.wait(GP, "c_p3v")
            k.wait(GP, "c_p4v")
            k.wait(GP, "c_p3v1")
            k.wait(GP, "c_p4v1")
            k.do(GP, lambda g, m_=mcol, r_=row: g.tensor_reduce(
                colst[0:1, r_:r_ + 1], maxacc[:, m_:m_ + 1], AX.C, ALU.max),
                inc="c_gmax")
            k.wait(VE, "c_gmax")
            k.do(VE, lambda v, r_=row: v.tensor_scalar(
                rowst[r_][:, :], onesr_f[:, :], colst[0:1, r_:r_ + 1],
                None, ALU.mult), inc="c_gmax")

        def rsqrt_refined(k, src, dst, n):
            k.do(VE, lambda v: v.reciprocal(stats[:, dst:dst + n],
                                            stats[:, src:src + n]),
                 inc="c_lnv", dr=True)
            k.wait(SC, "c_lnv")
            k.do(SC, lambda s: s.activation(stats[:, dst:dst + n],
                                            stats[:, dst:dst + n], AF.Sqrt),
                 inc="c_lna")
            k.wait(VE, "c_lna")
            k.do(VE, lambda v: v.tensor_tensor(
                stats[:, 56:56 + n], stats[:, dst:dst + n],
                stats[:, dst:dst + n], ALU.mult), dr=True)
            k.do(VE, lambda v: v.tensor_tensor(
                stats[:, 56:56 + n], stats[:, 56:56 + n],
                stats[:, src:src + n], ALU.mult), dr=True)
            k.do(VE, lambda v: v.tensor_scalar(
                stats[:, 56:56 + n], stats[:, 56:56 + n], -0.5, 1.5,
                ALU.mult, ALU.add), dr=True)
            k.do(VE, lambda v: v.tensor_tensor(
                stats[:, dst:dst + n], stats[:, dst:dst + n],
                stats[:, 56:56 + n], ALU.mult), inc="c_lnv", dr=True)

        def program(k: K):
            for (src, dst, cols) in [(ap["ident"], ident_b, P),
                                     (ap["gq_rep"], gq_b, D),
                                     (ap["gk_rep"], gk_b, D),
                                     (ap["gout_rep"], gout_b, D),
                                     (ap["bqkv_rep"], bq_b, 3 * D),
                                     (ap["bout_rep"], bo_b, D)]:
                k.wait(SY, "c_prolv")
                k.dma(SY, tmp_f[:, 0:cols], src, "c_prol")
                k.wait(VE, "c_prol")
                k.do(VE, lambda v, d_=dst, c_=cols: v.tensor_copy(
                    d_[:, 0:c_], tmp_f[:, 0:c_]), inc="c_prolv")
            k.do(VE, lambda v: v.memset(maxacc[:], 0.0))
            k.do(VE, lambda v: v.memset(onesr_f[:], 1.0), inc="c_prolv")

            for (orig, agi, I, nOT) in [
                    (ap["wu_slice"], ap["wuag_i"], D, WU_OT),
                    (ap["wg_slice"], ap["wgag_i"], D, WU_OT),
                    (ap["wd_slice"], ap["wdag_i"], DF, WD_OT)]:
                nblk = I // 32
                nich = I // P
                for ot in range(nOT):
                    sl = ot % 2
                    cw = "c_wp0" if sl == 0 else "c_wp1"
                    k.wait(SY, "c_wpv")
                    k.dma(SY, wp_f[sl][:, 0:I], orig[ot * P:(ot + 1) * P, :],
                          cw)
                    k.wait(VE, cw)
                    k.wait(VE, "c_prolv")
                    k.wait(VE, "c_agst")
                    k.do(VE, lambda v, s_=sl, I_=I, nb=nblk: v.tensor_reduce(
                        wps_f[:, 0:nb],
                        wp_f[s_][:, 0:I_].rearrange("p (a b) -> p a b", b=32),
                        AX.X, ALU.max, apply_absolute_value=True), dr=True)
                    k.do(VE, lambda v, nb=nblk: v.tensor_scalar(
                        wps_f[:, 0:nb], wps_f[:, 0:nb], WCLIP, 1.0 / WQMAX,
                        ALU.max, ALU.mult), dr=True)
                    k.do(VE, lambda v, nb=nblk: v.reciprocal(
                        wpi_f[:, 0:nb], wps_f[:, 0:nb]), dr=True)
                    for b in range(nblk):
                        k.do(VE, lambda v, s_=sl, b_=b: v.tensor_scalar(
                            tmp_f[:, 32 * b_:32 * (b_ + 1)],
                            wp_f[s_][:, 32 * b_:32 * (b_ + 1)],
                            wpi_f[:, b_:b_ + 1], None, ALU.mult),
                            dr=(b == nblk - 1))
                    k.do(VE, lambda v, I_=I: v.tensor_copy(
                        iti2[:, 0:I_], tmp_f[:, 0:I_]), dr=True)
                    for b in range(nblk):
                        k.do(VE, lambda v, b_=b: v.tensor_scalar(
                            wq_b[:, 32 * b_:32 * (b_ + 1)],
                            iti2[:, 32 * b_:32 * (b_ + 1)],
                            wps_f[:, b_:b_ + 1], None, ALU.mult),
                            dr=(b == nblk - 1))
                    k.do(VE, lambda v: v.nop(), inc="c_wpv")
                    for c in range(nich):
                        k.wait(TE, "c_wpv")
                        nev = k.count("c_wpte")
                        if nev >= 2:
                            k.wait(TE, "c_wpte", nev - 1)
                        k.do(TE, lambda t, c_=c: t.transpose(
                            psT[c_ % 2][:, :], wq_b[:, c_ * P:(c_ + 1) * P],
                            ident_b[:]), inc="c_wpt")
                        k.wait(VE, "c_wpt")
                        k.do(VE, lambda v, c_=c: v.tensor_copy(
                            wag_b[:, :], psT[c_ % 2][:, :]), inc="c_wpte")
                        k.wait(SC, "c_wpte")
                        k.do(SC, lambda s, c_=c, o_=ot, a_=agi: s.dma_start(
                            out=a_[c_ * P:(c_ + 1) * P, o_ * P:(o_ + 1) * P],
                            in_=wag_b[:, :]), inc="c_agst", incn=16)
                        k.wait(VE, "c_agst")
            k.wait(GP, "c_agst")
            for nm in ("wu", "wg", "wd"):
                k.do(GP, lambda g, n_=nm: g.collective_compute(
                    "AllGather", ALU.bypass, all_group,
                    ins=[ap[n_ + "ag_i"]], outs=[ap[n_ + "ag_o"]]),
                    inc="c_ag")

            for (src, dst, cols) in [(ap["wqkvT"], ap["wq_bf"], 3 * D),
                                     (ap["woutT"], ap["wo_bf"], D)]:
                for r in range(NC8):
                    sl = r % 2
                    cb = "c_bnc0" if sl == 0 else "c_bnc1"
                    k.wait(SY, "c_bncst")
                    k.wait(SY, "c_wpv")
                    k.dma(SY, wp_f[sl][:, 0:cols], src[r * P:(r + 1) * P, :],
                          cb)
                    k.wait(VE, cb)
                    k.wait(VE, "c_wpt")
                    k.wait(VE, "c_bncst")
                    k.do(VE, lambda v, s_=sl, c_=cols: v.tensor_copy(
                        wq_b[:, 0:c_], wp_f[s_][:, 0:c_]), inc="c_bncv")
                    k.wait(SC, "c_bncv")
                    k.do(SC, lambda s, r_=r, c_=cols, d_=dst: s.dma_start(
                        out=d_[r_ * P:(r_ + 1) * P, :], in_=wq_b[:, 0:c_]),
                        inc="c_bncst", incn=16)

            for eng in (SY, SC, VE, TE):
                k.wait(eng, "c_bncst")
                k.wait(eng, "c_bncst1")
                k.wait(eng, "c_agst")
            for s in range(NS):
                for j in range(stiles):
                    jt = s * stiles + j
                    sl = jt % 2
                    cx = "c_x0" if sl == 0 else "c_x1"
                    if s > 0 and j == 0:
                        k.wait(SY, "c_lnv")
                    k.dma(SY, x_s[:, j * D:(j + 1) * D],
                          ap["x"][jt * P:(jt + 1) * P, :], cx)
                    k.wait(VE, cx)
                    k.do(VE, lambda v, j_=j: v.reduce_sum(
                        stats[:, j_:j_ + 1], x_s[:, j_ * D:(j_ + 1) * D],
                        AX.X))
                    k.wait(SC, cx)
                    k.wait(SC, "c_lnv")
                    k.do(SC, lambda sc, s_=sl, j_=j: sc.activation(
                        sq_f[s_][:], x_s[:, j_ * D:(j_ + 1) * D], AF.Square),
                        inc="c_lna")
                    k.wait(VE, "c_lna")
                    k.do(VE, lambda v, s_=sl, j_=j: v.reduce_sum(
                        stats[:, 8 + j_:9 + j_], sq_f[s_][:], AX.X),
                        inc="c_lnv")
                n = stiles
                k.do(VE, lambda v: v.tensor_scalar(
                    stats[:, 16:16 + n], stats[:, 0:n], 1.0 / D, None,
                    ALU.mult), dr=True)
                k.do(VE, lambda v: v.tensor_scalar(
                    stats[:, 24:24 + n], stats[:, 8:8 + n], 1.0 / D, None,
                    ALU.mult), dr=True)
                k.do(VE, lambda v: v.tensor_tensor(
                    stats[:, 48:48 + n], stats[:, 16:16 + n],
                    stats[:, 16:16 + n], ALU.mult), dr=True)
                k.do(VE, lambda v: v.tensor_sub(
                    stats[:, 24:24 + n], stats[:, 24:24 + n],
                    stats[:, 48:48 + n]), dr=True)
                k.do(VE, lambda v: v.tensor_scalar(
                    stats[:, 24:24 + n], stats[:, 24:24 + n], LN_EPS, None,
                    ALU.add), inc="c_lnv", dr=True)
                rsqrt_refined(k, 24, 32, n)
                k.do(VE, lambda v: v.tensor_tensor(
                    stats[:, 40:40 + n], stats[:, 16:16 + n],
                    stats[:, 32:32 + n], ALU.mult), dr=True)
                k.do(VE, lambda v: v.tensor_scalar(
                    stats[:, 40:40 + n], stats[:, 40:40 + n], -1.0, None,
                    ALU.mult), inc="c_lnv", dr=True)

                if dbg and s == 0:
                    k.wait(SC, "c_lnv")
                    k.do(SC, lambda sc: sc.dma_start(
                        out=ap["dbg_st"], in_=stats[:, :]),
                        inc="c_misc", incn=16)
                    k.do(SC, lambda sc: sc.dma_start(
                        out=ap["dbg_xs"], in_=x_s[:, 0:D]),
                        inc="c_misc", incn=16)
                    k.wait(SC, "c_misc")
                for j in range(stiles):
                    jt = s * stiles + j
                    sl = jt % 2
                    k.wait(VE, "c_lnv")
                    if jt >= 2:
                        k.wait(VE, "c_trpe")
                    k.do(VE, lambda v, s_=sl, j_=j: v.tensor_scalar(
                        xn_tm[s_][:], x_s[:, j_ * D:(j_ + 1) * D],
                        stats[:, 32 + j_:33 + j_],
                        stats[:, 40 + j_:41 + j_],
                        ALU.mult, ALU.add), inc="c_lna")
                    if dbg and jt == 0:
                        k.wait(SC, "c_lna")
                        k.do(SC, lambda sc: sc.dma_start(
                            out=ap["dbg_xntm"], in_=xn_tm[0][:, :]),
                            inc="c_misc", incn=16)
                        k.wait(SC, "c_misc")
                    k.do(VE, lambda v, s_=sl: v.tensor_reduce(
                        stats[:, 48:49], xn_tm[s_][:], AX.X, ALU.max,
                        apply_absolute_value=True), dr=True)
                    k.do(VE, lambda v: v.tensor_tensor(
                        maxacc[:, 0:1], maxacc[:, 0:1], stats[:, 48:49],
                        ALU.max), inc="c_lnv", dr=True)
                    for c in range(NC8):
                        nev = k.count("c_trpe")
                        if nev >= 2:
                            k.wait(TE, "c_trpe", nev - 1)
                        k.wait(TE, "c_lna")
                        k.do(TE, lambda t, s_=sl, c_=c: t.transpose(
                            psT[c_ % 2][:, :],
                            xn_tm[s_][:, c_ * P:(c_ + 1) * P], ident_b[:]),
                            inc="c_trp")
                        k.wait(VE, "c_trp")
                        k.do(VE, lambda v, c_=c, j_=j: v.tensor_copy(
                            AR4[:, (c_ * stiles + j_) * P:
                                (c_ * stiles + j_ + 1) * P],
                            psT[c_ % 2][:, :]), inc="c_trpe")
                k.wait(SC, "c_trpe")
                for c in range(NC8):
                    k.do(SC, lambda sc, c_=c, s_=s: sc.dma_start(
                        out=ap["xn_sp"][c_ * P:(c_ + 1) * P,
                                        s_ * stiles * P:(s_ + 1) * stiles * P],
                        in_=AR4[:, c_ * stiles * P:(c_ + 1) * stiles * P]),
                        inc="c_xnsp", incn=16)

                for og in range(6):
                    sl = og % 2
                    cqw = "c_qw0" if sl == 0 else "c_qw1"
                    if s * 6 + og >= 2:
                        k.wait(SY, "c_qkvp", k.count("c_qkvp") - stiles)
                    for c in range(NC8):
                        k.dma(SY, wqr_b[sl][:, c * 512:(c + 1) * 512],
                              ap["wq_bf"][c * P:(c + 1) * P,
                                          og * 512:(og + 1) * 512], cqw)
                    k.wait(TE, cqw)
                    for j in range(stiles):
                        g = og * stiles + j
                        bank = psA[g % 4]
                        nev = k.count("c_qkve")
                        if nev >= 4:
                            k.wait(TE, "c_qkve", nev - 3)
                        for c in range(NC8):
                            k.do(TE, lambda t, b=bank, s_=sl, c_=c, j_=j:
                                 t.matmul(
                                     b[:, :],
                                     AR4[:, (c_ * stiles + j_) * P:
                                         (c_ * stiles + j_ + 1) * P],
                                     wqr_b[s_][:, c_ * 512:(c_ + 1) * 512],
                                     start=(c_ == 0), stop=(c_ == NC8 - 1)),
                                 inc=("c_qkvp" if c == NC8 - 1 else None))
                        dst = (q_tm, q_tm, k_tm, k_tm, v_tm, v_tm)[og]
                        off = (og % 2) * 512
                        k.wait(VE, "c_qkvp")
                        k.do(VE, lambda v, b=bank, d=dst, j_=j, o_=off,
                             og_=og: v.tensor_tensor(
                                 d[:, j_ * D + o_:j_ * D + o_ + 512], b[:, :],
                                 bq_b[:, og_ * 512:(og_ + 1) * 512], ALU.add),
                             inc="c_qkve")

                if dbg and s == 0:
                    k.wait(SC, "c_qkve")
                    for nm, src in [("dbg_q", q_tm), ("dbg_k", k_tm),
                                    ("dbg_v", v_tm)]:
                        k.do(SC, lambda sc, n_=nm, s2=src: sc.dma_start(
                            out=ap[n_], in_=s2), inc="c_misc", incn=16)
                    k.wait(SC, "c_misc")

                for j in range(stiles):
                    qs = q_tm[:, j * D:(j + 1) * D]
                    ks = k_tm[:, j * D:(j + 1) * D]
                    k.wait(SC, "c_qkve")
                    k.wait(SC, "c_lnv")
                    k.do(SC, lambda sc, q_=qs: sc.activation(
                        sq_f[0][:], q_, AF.Square))
                    k.do(SC, lambda sc, k_=ks: sc.activation(
                        sq_f[1][:], k_, AF.Square), inc="c_lna")
                    k.wait(VE, "c_lna")
                    k.do(VE, lambda v, j_=j: v.reduce_sum(
                        stats[:, 16 + 2 * j_:17 + 2 * j_], sq_f[0][:], AX.X))
                    k.do(VE, lambda v, j_=j: v.reduce_sum(
                        stats[:, 17 + 2 * j_:18 + 2 * j_], sq_f[1][:], AX.X),
                        inc="c_lnv", dr=True)
                k.do(VE, lambda v: v.tensor_scalar(
                    stats[:, 16:16 + 2 * stiles], stats[:, 16:16 + 2 * stiles],
                    1.0 / D, RMS_EPS, ALU.mult, ALU.add), inc="c_lnv",
                    dr=True)
                rsqrt_refined(k, 16, 24, 2 * stiles)

                for j in range(stiles):
                    jt = s * stiles + j
                    sl = jt % 2
                    crt = "c_rt0" if sl == 0 else "c_rt1"
                    if jt >= 2:
                        k.wait(SY, "c_toka", k.count("c_toka") - 1)
                    k.dma(SY, rc_f[sl][:], ap["rc"][jt * P:(jt + 1) * P, :],
                          crt)
                    k.dma(SY, rs_f[sl][:], ap["rs"][jt * P:(jt + 1) * P, :],
                          crt)
                    qs = q_tm[:, j * D:(j + 1) * D]
                    ks = k_tm[:, j * D:(j + 1) * D]
                    vs = v_tm[:, j * D:(j + 1) * D]
                    for (src_tm, gg, dst_hm, ridx, w1, w2, crl) in [
                            (qs, gq_b, wk3_p[sl], 24 + 2 * j,
                             wkq[0], wkq[1], "c_rlq"),
                            (ks, gk_b, kf_b[sl][:], 25 + 2 * j,
                             wkk[0], wkk[1], "c_rlk")]:
                        k.wait(VE, crt)
                        k.wait(VE, crl)
                        if crl == "c_rlq":
                            if jt >= 2:
                                k.wait(SC, "c_qft", k.count("c_qft") - NC8)
                        else:
                            if jt >= 2:
                                k.wait(SC, "c_vkmm", k.count("c_vkmm") - 1)
                        k.do(VE, lambda v, s_=src_tm, g_=gg, w_=w1:
                             v.tensor_tensor(w_, s_, g_[:, :], ALU.mult),
                             dr=True)
                        k.do(VE, lambda v, s_=sl, w1_=w1, w2_=w2:
                             v.tensor_tensor(
                                 w2_[:, 0:512], w1_[:, 0:512], rc_f[s_][:],
                                 ALU.mult))
                        k.do(VE, lambda v, s_=sl, w1_=w1: v.tensor_tensor(
                            sq_bv[s_][:, 0:512], w1_[:, 512:1024],
                            rs_f[s_][:], ALU.mult), dr=True)
                        k.do(VE, lambda v, s_=sl, w2_=w2: v.tensor_sub(
                            w2_[:, 0:512], w2_[:, 0:512],
                            sq_bv[s_][:, 0:512]))
                        k.do(VE, lambda v, s_=sl, w1_=w1: v.tensor_tensor(
                            sq_bv[s_][:, 512:1024], w1_[:, 0:512],
                            rs_f[s_][:], ALU.mult), dr=True)
                        k.do(VE, lambda v, s_=sl, w1_=w1, w2_=w2:
                             v.tensor_tensor(
                                 w2_[:, 512:1024], w1_[:, 512:1024],
                                 rc_f[s_][:], ALU.mult), dr=True)
                        k.do(VE, lambda v, s_=sl, w2_=w2: v.tensor_add(
                            w2_[:, 512:1024], w2_[:, 512:1024],
                            sq_bv[s_][:, 512:1024]), inc="c_lnv", dr=True)
                        k.wait(SC, "c_lnv")
                        if dbg and jt == 0 and crl == "c_rlk":
                            k.do(SC, lambda sc, w_=w1: sc.dma_start(
                                out=ap["dbg_wk1"], in_=w_[:, :]),
                                inc="c_misc", incn=16)
                            k.do(SC, lambda sc, w_=w2: sc.dma_start(
                                out=ap["dbg_wk2"], in_=w_[:, :]),
                                inc="c_misc", incn=16)
                            k.wait(SC, "c_misc")
                        k.do(SC, lambda sc, d_=dst_hm, r_=ridx, w2_=w2:
                             sc.activation(
                                 A(d_.tensor, d_.offset, [DH, NH], [1, 32]),
                                 A(w2_.tensor, w2_.offset, [1, NH], [NH, 32]),
                                 AF.Relu, scale=stats[:, r_:r_ + 1]))
                        k.do(SC, lambda sc, d_=dst_hm, r_=ridx, w2_=w2:
                             sc.activation(
                                 A(d_.tensor, d_.offset + 32,
                                   [DH, NH], [1, 32]),
                                 A(w2_.tensor, w2_.offset + 512,
                                   [1, NH], [NH, 32]),
                                 AF.Relu, scale=stats[:, r_:r_ + 1]),
                             inc=crl)
                    k.do(VE, lambda v: v.nop(), inc="c_toka")
                    k.wait(VE, "c_qkve")
                    if jt >= 2:
                        k.wait(VE, "c_vkmm", k.count("c_vkmm") - 1)
                    k.do(VE, lambda v, s_=sl: v.memset(vp_b[s_][:], 1.0),
                         dr=True)
                    k.do(VE, lambda v, s_=sl, v_=vs: v.tensor_copy(
                        A(vp_b[s_], 0, [65, NH], [1, 32]),
                        A(v_.tensor, v_.offset, [1, NH], [NH, 32])))
                    k.do(VE, lambda v, s_=sl, v_=vs: v.tensor_copy(
                        A(vp_b[s_], 32, [65, NH], [1, 32]),
                        A(v_.tensor, v_.offset + 512, [1, NH], [NH, 32])), inc="c_tokv")
                    if dbg and jt == 0:
                        k.wait(SC, "c_tokv")
                        k.wait(SC, "c_lna")
                        k.do(SC, lambda sc, s_=sl: sc.dma_start(
                            out=ap["dbg_kf"], in_=kf_b[s_][:, :]),
                            inc="c_misc", incn=16)
                        k.do(SC, lambda sc, s_=sl: sc.dma_start(
                            out=ap["dbg_vp"], in_=vp_b[s_][:, :]),
                            inc="c_misc", incn=16)
                        k.do(SC, lambda sc, s_=sl: sc.dma_start(
                            out=ap["dbg_qf"], in_=wk3_p[s_][:, :]),
                            inc="c_misc", incn=16)
                        k.wait(SC, "c_misc")
                    k.wait(TE, "c_tokv")
                    k.wait(TE, "c_rlk")
                    for h in range(NH):
                        k.do(TE, lambda t, s_=sl, h_=h,
                             f=(jt == 0 and h % 8 == 0),
                             l=(jt == NT - 1 and h % 8 == 7): t.matmul(
                                 psV[h_ // 8][:, (h_ % 8) * DH:
                                              (h_ % 8 + 1) * DH],
                                 vp_b[s_][:, h_ * 65:(h_ + 1) * 65],
                                 kf_b[s_][:, h_ * DH:(h_ + 1) * DH],
                                 start=f, stop=l),
                             inc=("c_vkmm" if h == NH - 1 else None))
                    k.wait(TE, "c_rlq")
                    for c in range(NC8):
                        nev = k.count("c_qfte")
                        if nev >= 2:
                            k.wait(TE, "c_qfte", nev - 1)
                        k.do(TE, lambda t, c_=c, s_=sl: t.transpose(
                            psT[c_ % 2][:, :],
                            wk3_p[s_][:, c_ * P:(c_ + 1) * P],
                            ident_b[:]), inc="c_qft")
                        k.wait(VE, "c_qft")
                        k.do(VE, lambda v, c_=c, j_=jt: v.tensor_copy(
                            big_a[:, c_ * T + j_ * P:c_ * T + (j_ + 1) * P],
                            psT[c_ % 2][:, :]), inc="c_qfte")


            stage_gmax(k, 0, 0)
            k.wait(SC, "c_gmax")
            k.do(SC, lambda sc: sc.dma_start(
                out=ap["cc1_i"], in_=rowst[0][:, :]), inc="c_ccst", incn=16)
            k.wait(GP, "c_ag")
            k.wait(GP, "c_ccst")
            k.do(GP, lambda g: g.collective_compute(
                "AllReduce", ALU.max, all_group,
                ins=[ap["cc1_i"]], outs=[ap["cc1_o"]]), inc="c_cc")
            k.wait(SC, "c_vkmm")
            k.do(SC, lambda sc: sc.activation(vk_f[:, 0:8 * DH], psV[0][:, :],
                                              AF.Copy))
            k.do(SC, lambda sc: sc.activation(vk_f[:, 8 * DH:NH * DH],
                                              psV[1][:, :], AF.Copy),
                 inc="c_vkev")
            k.wait(SC, "c_vkev")
            k.do(SC, lambda sc: sc.dma_start(out=ap["vk_i"], in_=vk_f[:, :]),
                 inc="c_ccst", incn=16)
            k.wait(GP, "c_ccst")
            k.do(GP, lambda g: g.collective_compute(
                "AllReduce", ALU.add, pair_groups,
                ins=[ap["vk_i"]], outs=[ap["vk_o"]]), inc="c_cc")
            k.wait(SY, "c_cc")
            k.dma(SY, vk_f[:, :], ap["vk_o"], "c_vkld")
            k.dma(SY, scal[:, 0:1], bass.AP(ap["cc1_o"].tensor, 0, [[1, P], [1, 1]]), "c_scal", slow=True)
            k.wait(VE, "c_vkld")
            k.do(VE, lambda v: v.tensor_copy(vk_bf[:, :], vk_f[:, :]),
                 inc="c_vkld")
            for h in range(NH):
                k.wait(TE, "c_vkld")
                nev = k.count("c_vktre")
                if nev >= 2:
                    k.wait(TE, "c_vktre", nev - 1)
                k.do(TE, lambda t, h_=h: t.transpose(
                    psT[h_ % 2][0:DH, 0:65], vk_bf[:, h_ * DH:(h_ + 1) * DH],
                    ident_b[0:65, 0:65]), inc="c_vktr")
                k.wait(VE, "c_vktr")
                k.do(VE, lambda v, h_=h: v.tensor_copy(
                    vkT_b[DH * (h_ % 2):DH * (h_ % 2) + DH,
                          (h_ // 2) * 65:(h_ // 2) * 65 + 65],
                    psT[h_ % 2][0:DH, 0:65]), inc="c_vktre")
            k.wait(VE, "c_scal")
            k.do(VE, lambda v: v.tensor_scalar(
                scal[:, 1:2], scal[:, 0:1], ACLIP, None, ALU.max), dr=True)
            k.do(VE, lambda v: v.reciprocal(scal[:, 2:3], scal[:, 1:2]), dr=True)
            k.do(VE, lambda v: v.tensor_scalar(
                scal[:, 3:4], scal[:, 2:3], QMAX, None, ALU.mult), dr=True)
            k.do(VE, lambda v: v.tensor_scalar(
                scal[:, 4:5], scal[:, 1:2], 1.0 / QMAX, None, ALU.mult),
                inc="c_scal", dr=True)

            k.wait(SY, "c_bncst")
            k.wait(SY, "c_bncst1")
            k.wait(SY, "c_p3v")
            k.wait(SY, "c_rlq")
            k.wait(SY, "c_rlk")
            k.wait(SY, "c_qft")
            for c in range(NC8):
                k.dma(SY, wor_b[:, c * D:(c + 1) * D],
                      ap["wo_bf"][c * P:(c + 1) * P, :], "c_wor")
            def res_A(jt):
                sl = jt % 2
                for h in range(NH):
                    bank = psA[h % 2]
                    nev = k.count("c_rese")
                    if nev >= 2:
                        k.wait(TE, "c_rese", nev - 1)
                    if jt == 0 and h == 0:
                        k.wait(TE, "c_vktre")
                        k.wait(TE, "c_qfte")
                        k.wait(TE, "c_qkve")
                    k.do(TE, lambda t, h_=h, j_=jt, b=bank: t.matmul(
                        b[:, 0:65],
                        big_a[DH * (h_ % 2):DH * (h_ % 2) + DH,
                              (h_ // 2) * T + j_ * P:
                              (h_ // 2) * T + (j_ + 1) * P],
                        vkT_b[DH * (h_ % 2):DH * (h_ % 2) + DH,
                              (h_ // 2) * 65:(h_ // 2) * 65 + 65],
                        start=True, stop=True), inc="c_resp")
                    k.wait(SC, "c_resp")
                    if jt >= 2 and h == 0:
                        k.wait(SC, "c_resd", k.count("c_resd") - 1)
                    k.do(SC, lambda sc, b=bank, h_=h, s_=sl: sc.activation(
                        resst_p[s_][:, h_ * 65:(h_ + 1) * 65], b[:, 0:65],
                        AF.Copy))
                    k.do(SC, lambda sc, b=bank, h_=h, s_=sl: sc.activation(
                        den_p[s_][:, h_:h_ + 1], b[:, 64:65], AF.Copy),
                        inc="c_rese")
                k.wait(VE, "c_rese")
                k.do(VE, lambda v, s_=sl: v.tensor_scalar(
                    den_p[s_][:, 0:NH], den_p[s_][:, 0:NH], EPS_LIN, None,
                    ALU.add), dr=True)
                k.do(VE, lambda v, s_=sl: v.reciprocal(
                    den_p[s_][:, 0:NH], den_p[s_][:, 0:NH]), dr=True)
                if jt >= 2:
                    k.wait(VE, "c_attt")
                for h in range(NH):
                    k.do(VE, lambda v, h_=h, s_=sl: v.tensor_scalar(
                        att_p[s_][:, h_ * DH:(h_ + 1) * DH],
                        resst_p[s_][:, h_ * 65:h_ * 65 + DH],
                        den_p[s_][:, h_:h_ + 1], None, ALU.mult),
                        inc=("c_resd" if h == NH - 1 else None),
                        dr=(h == NH - 1))

            def res_B(jt):
                sl = jt % 2
                cxr = "c_xr0" if sl == 0 else "c_xr1"
                if jt >= 2:
                    k.wait(SY, "c_aoz", k.count("c_aoz") - 1)
                k.dma(SY, x_r[sl][:], ap["x"][jt * P:(jt + 1) * P, :], cxr)
                if dbg and jt == 0:
                    k.wait(SC, "c_resd")
                    k.do(SC, lambda sc: sc.dma_start(
                        out=ap["dbg_att"], in_=att_p[0][:, 0:D]),
                        inc="c_misc", incn=16)
                    k.wait(SC, "c_misc")
                for c in range(NC8):
                    nev = k.count("c_atte")
                    if nev >= 2:
                        k.wait(TE, "c_atte", nev - 1)
                    if c == 0:
                        nrd = k.count("c_resd")
                        k.wait(TE, "c_resd", nrd - 1 if jt < NT - 1 else nrd)
                    k.do(TE, lambda t, c_=c, s_=sl: t.transpose(
                        psT[c_ % 2][:, :], att_p[s_][:, c_ * P:(c_ + 1) * P],
                        ident_b[:]), inc="c_attt")
                    k.wait(SC, "c_attt")
                    k.do(SC, lambda sc, c_=c, s_=sl: sc.activation(
                        att_fm_p[s_][:, c_ * P:(c_ + 1) * P],
                        psT[c_ % 2][:, :], AF.Copy), inc="c_atte")
                if jt == 0:
                    k.wait(TE, "c_wor")
                for og in range(2):
                    bank = psA[2 + og]
                    nev = k.count("c_woe")
                    if nev >= 2:
                        k.wait(TE, "c_woe", nev - 1)
                    if og == 0:
                        k.wait(TE, "c_atte")
                    for c in range(NC8):
                        k.do(TE, lambda t, b=bank, c_=c, og_=og, s_=sl:
                             t.matmul(
                                 b[:, :], att_fm_p[s_][:, c_ * P:(c_ + 1) * P],
                                 wor_b[:, c_ * D + og_ * 512:
                                       c_ * D + (og_ + 1) * 512],
                                 start=(c_ == 0), stop=(c_ == NC8 - 1)),
                             inc=("c_wop" if c == NC8 - 1 else None))
                    k.wait(VE, "c_wop")
                    if og == 0 and jt >= 2:
                        k.wait(VE, "c_aost", k.count("c_aost") - 16)
                    k.do(VE, lambda v, b=bank, og_=og, s_=sl: v.tensor_tensor(
                        ffn_p[s_][:, og_ * 512:(og_ + 1) * 512], b[:, :],
                        bo_b[:, og_ * 512:(og_ + 1) * 512], ALU.add),
                        inc="c_woe", dr=(og == 1))
                if dbg and jt == 0:
                    k.wait(SC, "c_woe")
                    k.do(SC, lambda sc: sc.dma_start(
                        out=ap["dbg_z"], in_=ffn_p[0][:, 0:D]),
                        inc="c_misc", incn=16)
                    k.wait(SC, "c_misc")
                k.do(VE, lambda v, s_=sl: v.tensor_tensor(
                    sq_f[s_][:], ffn_p[s_][:, 0:D], ffn_p[s_][:, 0:D],
                    ALU.mult))
                k.do(VE, lambda v, s_=sl: v.tensor_tensor(
                    z_p[s_][:, 0:D], ffn_p[s_][:, 0:D], gout_b[:, :],
                    ALU.mult), dr=True)
                k.do(VE, lambda v, s_=sl: v.reduce_sum(
                    stats[:, 49 + s_:50 + s_], sq_f[s_][:], AX.X), dr=True)
                k.do(VE, lambda v, s_=sl: v.tensor_scalar(
                    stats[:, 49 + s_:50 + s_], stats[:, 49 + s_:50 + s_],
                    1.0 / D, RMS_EPS, ALU.mult, ALU.add), inc="c_lnv",
                    dr=True)
                rsqrt_refined(k, 49 + sl, 53 + sl, 1)
                k.do(VE, lambda v, s_=sl: v.tensor_scalar(
                    ao_b[s_][:], z_p[s_][:, 0:D],
                    stats[:, 53 + s_:54 + s_], None, ALU.mult), dr=True)
                k.wait(VE, cxr)
                k.do(VE, lambda v, s_=sl: v.tensor_tensor(
                    ffn_p[s_][:, 0:D], x_r[s_][:], ao_b[s_][:], ALU.add),
                    inc="c_aoz")
                k.wait(SC, "c_aoz")
                k.do(SC, lambda sc, s_=sl, j_=jt: sc.dma_start(
                    out=ap["ao_sp"][j_ * P:(j_ + 1) * P, :],
                    in_=ffn_p[s_][:, 0:D]), inc="c_aost", incn=16)

            for it in range(NT + 1):
                if it < NT:
                    res_A(it)
                if it >= 1:
                    res_B(it - 1)

            for eng in (SY, SC, VE, TE):
                k.wait(eng, "c_aost")
                k.wait(eng, "c_qfte")
                k.wait(eng, "c_xnsp")
            k.wait(SC, "c_qkvp")
            k.wait(VE, "c_resp")
            k.wait(VE, "c_wop")
            for c in range(NC8):
                sl = c % 2
                cx = "c_xnr0" if sl == 0 else "c_xnr1"
                if c >= 2:
                    k.wait(SY, "c_x1c")
                k.dma(SY, ld_b[sl], ap["xn_sp"][c * P:(c + 1) * P, :], cx)
                k.wait(SC, cx)
                k.wait(SC, "c_scal")
                k.wait(SC, "c_x1c")
                k.do(SC, lambda sc, s_=sl: sc.activation(
                    iti[:, 0:T], ld_b[s_], AF.Copy, scale=scal[:, 3:4]),
                    inc="c_lna")
                k.wait(VE, "c_lna")
                k.do(VE, lambda v, c_=c: v.tensor_copy(
                    big_a[:, c_ * T:(c_ + 1) * T], iti[:, 0:T]), inc="c_x1c")
            for ti, (agname, spname, mcol) in enumerate(
                    [("wuag_o", "sp_up", 1), ("wgag_o", "sp_gt", 2)]):
                for oc in range(OCF):
                    g = ti * OCF + oc
                    sl = g % 2
                    cw = "c_wl0" if sl == 0 else "c_wl1"
                    rank = oc // WU_OT
                    within = oc % WU_OT
                    if g >= 2:
                        k.wait(SY, "c_ffp", k.count("c_ffp") - NO5)
                    if g == 0:
                        k.wait(SY, "c_ag")
                    W_ = WU_OT * P
                    k.do(SY, lambda e, s_=sl, r_=rank, w_=within, a_=agname:
                         e.dma_start(
                             out=wl_b[s_][:, :].rearrange(
                                 "p (c q) -> p c q", c=NC8),
                             in_=bass.AP(ap[a_].tensor,
                                         r_ * D * W_ + w_ * P,
                                         [[W_, P], [P * W_, NC8], [1, P]])),
                         inc=cw, incn=16)
                    k.wait(TE, cw)
                    k.wait(TE, "c_x1c")
                    for t5 in range(NO5):
                        bank = psA[k.count("c_ffp") % 4]
                        nev = k.count("c_ffe")
                        if nev >= 4:
                            k.wait(TE, "c_ffe", nev - 3)
                        for c in range(NC8):
                            k.do(TE, lambda t, b=bank, s_=sl, c_=c, t_=t5:
                                 t.matmul(
                                     b[:, :],
                                     wl_b[s_][:, c_ * P:(c_ + 1) * P],
                                     big_a[:, c_ * T + t_ * 512:
                                           c_ * T + (t_ + 1) * 512],
                                     start=(c_ == 0), stop=(c_ == NC8 - 1)),
                                 inc=("c_ffp" if c == NC8 - 1 else None))
                        k.wait(SC, "c_ffp")
                        if t5 == 0:
                            k.wait(SC, "c_ffst" if sl == 0 else "c_ffst1")
                        k.do(SC, lambda sc, b=bank, s_=sl, t_=t5:
                             sc.activation(
                                 A(st_b[s_].tensor, st_b[s_].offset + t_ * 512, [1, 512]),
                                 b[:, :], AF.Copy), inc="c_ffe")
                    k.wait(VE, "c_ffe")
                    k.do(VE, lambda v, s_=sl: v.tensor_reduce(
                        stats[:, 60:61], st_b[s_], AX.X, ALU.max,
                        apply_absolute_value=True), dr=True)
                    k.do(VE, lambda v, m_=mcol: v.tensor_tensor(
                        maxacc[:, m_:m_ + 1], maxacc[:, m_:m_ + 1],
                        stats[:, 60:61], ALU.max), inc="c_ffv", dr=True)
                    k.wait(SC, "c_ffv")
                    cst = "c_ffst" if sl == 0 else "c_ffst1"
                    k.do(SC, lambda sc, s_=sl, o_=oc, sp_=spname:
                         sc.dma_start(out=ap[sp_][o_ * P:(o_ + 1) * P, :],
                                      in_=st_b[s_]), inc=cst, incn=16)

            stage_gmax(k, 1, 0)
            stage_gmax(k, 2, 1)
            k.wait(SC, "c_gmax")
            k.do(SC, lambda sc: sc.dma_start(
                out=ap["cc3_i"][0:1, :], in_=rowst[0][:, :]),
                inc="c_ccst", incn=16)
            k.do(SC, lambda sc: sc.dma_start(
                out=ap["cc3_i"][1:2, :], in_=rowst[1][:, :]),
                inc="c_ccst", incn=16)
            k.wait(GP, "c_ccst")
            k.do(GP, lambda g: g.collective_compute(
                "AllReduce", ALU.max, all_group,
                ins=[ap["cc3_i"]], outs=[ap["cc3_o"]]), inc="c_cc")
            k.wait(SY, "c_cc")
            k.dma(SY, scal[:, 5:6], bass.AP(ap["cc3_o"].tensor, 0, [[1, P], [1, 1]]), "c_scal", slow=True)
            k.dma(SY, scal[:, 6:7], bass.AP(ap["cc3_o"].tensor, P, [[1, P], [1, 1]]), "c_scal", slow=True)
            k.wait(VE, "c_scal")
            k.do(VE, lambda v: v.tensor_scalar(
                scal[:, 5:7], scal[:, 5:7], ACLIP, None, ALU.max), dr=True)
            k.do(VE, lambda v: v.reciprocal(scal[:, 7:9], scal[:, 5:7]), dr=True)
            k.do(VE, lambda v: v.tensor_scalar(
                scal[:, 7:9], scal[:, 7:9], QMAX, None, ALU.mult), dr=True)
            k.do(VE, lambda v: v.tensor_tensor(
                scal[:, 9:10], scal[:, 4:5], scal[:, 6:7], ALU.mult), dr=True)
            k.do(VE, lambda v: v.tensor_scalar(
                scal[:, 9:10], scal[:, 9:10], 1.0 / QMAX, None, ALU.mult),
                inc="c_scal", dr=True)

            for eng in (SY, SC, VE):
                k.wait(eng, "c_ffst")
                k.wait(eng, "c_ffst1")
                k.wait(eng, "c_x1c")
            k.wait(VE, "c_ffp")
            for oc in range(OCF):
                sl = oc % 2
                cl = "c_ld0" if sl == 0 else "c_ld1"
                if oc >= 2:
                    k.wait(SY, "c_p3st" if sl == 0 else "c_p3st1")
                k.dma(SY, ld_b[sl], ap["sp_gt"][oc * P:(oc + 1) * P, :], cl)
                k.wait(SC, cl)
                k.wait(SC, "c_scal")
                k.wait(SC, "c_p3v" if sl == 0 else "c_p3v1")
                k.do(SC, lambda sc, s_=sl: sc.activation(
                    iti_v[s_], ld_b[s_], AF.Copy, scale=scal[:, 8:9]))
                k.do(SC, lambda sc, s_=sl: sc.activation(
                    sig_v[s_], iti_v[s_], AF.Sigmoid,
                    scale=scal[:, 9:10]), inc="c_p3a")
                k.wait(VE, "c_p3a")
                k.do(VE, lambda v, s_=sl: v.tensor_scalar(
                    iti2_v[s_], sig_v[s_], 65536.0, None, ALU.mult), dr=True)
                k.do(VE, lambda v, s_=sl: v.tensor_tensor(
                    st_b[s_], iti_v[s_], iti2_v[s_], ALU.mult))
                k.do(VE, lambda v, s_=sl: v.tensor_reduce(
                    stats[:, 61:62], st_b[s_], AX.X, ALU.max,
                    apply_absolute_value=True), dr=True)
                k.do(VE, lambda v: v.tensor_tensor(
                    maxacc[:, 3:4], maxacc[:, 3:4], stats[:, 61:62],
                    ALU.max), inc=("c_p3v" if sl == 0 else "c_p3v1"),
                    dr=True)
                k.wait(SC, "c_p3v" if sl == 0 else "c_p3v1")
                k.do(SC, lambda sc, s_=sl, o_=oc: sc.dma_start(
                    out=ap["sp_g2"][o_ * P:(o_ + 1) * P, :], in_=st_b[s_]),
                    inc=("c_p3st" if sl == 0 else "c_p3st1"), incn=16)

            k.wait(GP, "c_p3v1")
            stage_gmax(k, 3, 0)
            k.wait(SC, "c_gmax")
            k.do(SC, lambda sc: sc.dma_start(
                out=ap["cc4_i"], in_=rowst[0][:, :]), inc="c_ccst", incn=16)
            k.wait(GP, "c_ccst")
            k.do(GP, lambda g: g.collective_compute(
                "AllReduce", ALU.max, all_group,
                ins=[ap["cc4_i"]], outs=[ap["cc4_o"]]), inc="c_cc")
            k.wait(SY, "c_cc")
            k.dma(SY, scal[:, 10:11], bass.AP(ap["cc4_o"].tensor, 0, [[1, P], [1, 1]]), "c_scal", slow=True)
            k.wait(VE, "c_scal")
            k.do(VE, lambda v: v.tensor_scalar(
                scal[:, 10:11], scal[:, 10:11], ACLIP, None, ALU.max), dr=True)
            k.do(VE, lambda v: v.reciprocal(scal[:, 11:12], scal[:, 10:11]), dr=True)
            k.do(VE, lambda v: v.tensor_scalar(
                scal[:, 11:12], scal[:, 11:12], QMAX, None, ALU.mult),
                inc="c_scal", dr=True)

            k.wait(SY, "c_p3st")
            k.wait(SY, "c_p3st1")
            for oc in range(OCF):
                sl = oc % 2
                cl = "c_ld0" if sl == 0 else "c_ld1"
                cl2 = "c_ld20" if sl == 0 else "c_ld21"
                if oc >= 2:
                    k.wait(SY, "c_p4st" if sl == 0 else "c_p4st1")
                k.dma(SY, ld_b[sl], ap["sp_g2"][oc * P:(oc + 1) * P, :], cl)
                k.dma(SY, ld2_b[sl], ap["sp_up"][oc * P:(oc + 1) * P, :], cl2)
                k.wait(SC, cl)
                k.wait(SC, cl2)
                k.wait(SC, "c_scal")
                k.wait(SC, "c_p4v" if sl == 0 else "c_p4v1")
                k.do(SC, lambda sc, s_=sl: sc.activation(
                    iti_v[s_], ld_b[s_], AF.Copy, scale=scal[:, 11:12]))
                k.do(SC, lambda sc, s_=sl: sc.activation(
                    iti2_v[s_], ld2_b[s_], AF.Copy, scale=scal[:, 7:8]),
                    inc="c_p4a")
                k.wait(VE, "c_p4a")
                k.do(VE, lambda v, s_=sl: v.tensor_tensor(
                    st_b[s_], iti_v[s_], iti2_v[s_], ALU.mult))
                k.do(VE, lambda v, s_=sl: v.tensor_reduce(
                    stats[:, 61:62], st_b[s_], AX.X, ALU.max,
                    apply_absolute_value=True), dr=True)
                k.do(VE, lambda v: v.tensor_tensor(
                    maxacc[:, 4:5], maxacc[:, 4:5], stats[:, 61:62],
                    ALU.max), inc=("c_p4v" if sl == 0 else "c_p4v1"),
                    dr=True)
                k.wait(SC, "c_p4v" if sl == 0 else "c_p4v1")
                k.do(SC, lambda sc, s_=sl, o_=oc: sc.dma_start(
                    out=ap["sp_o"][o_ * P:(o_ + 1) * P, :], in_=st_b[s_]),
                    inc=("c_p4st" if sl == 0 else "c_p4st1"), incn=16)

            k.wait(GP, "c_p4v1")
            stage_gmax(k, 4, 0)
            k.wait(SC, "c_gmax")
            k.do(SC, lambda sc: sc.dma_start(
                out=ap["cc5_i"], in_=rowst[0][:, :]), inc="c_ccst", incn=16)
            k.wait(GP, "c_ccst")
            k.do(GP, lambda g: g.collective_compute(
                "AllReduce", ALU.max, all_group,
                ins=[ap["cc5_i"]], outs=[ap["cc5_o"]]), inc="c_cc")
            k.wait(SY, "c_cc")
            k.dma(SY, scal[:, 12:13], bass.AP(ap["cc5_o"].tensor, 0, [[1, P], [1, 1]]), "c_scal", slow=True)
            k.wait(VE, "c_scal")
            k.do(VE, lambda v: v.tensor_scalar(
                scal[:, 12:13], scal[:, 12:13], ACLIP, None, ALU.max), dr=True)
            k.do(VE, lambda v: v.reciprocal(scal[:, 13:14], scal[:, 12:13]), dr=True)
            k.do(VE, lambda v: v.tensor_scalar(
                scal[:, 13:14], scal[:, 13:14], QMAX, None, ALU.mult), dr=True)
            k.do(VE, lambda v: v.tensor_tensor(
                scal[:, 14:15], scal[:, 4:5], scal[:, 5:6], ALU.mult), dr=True)
            k.do(VE, lambda v: v.tensor_scalar(
                scal[:, 14:15], scal[:, 14:15], 1.0 / QMAX, None, ALU.mult), dr=True)
            k.do(VE, lambda v: v.tensor_tensor(
                scal[:, 15:16], scal[:, 9:10], scal[:, 10:11], ALU.mult), dr=True)
            k.do(VE, lambda v: v.tensor_scalar(
                scal[:, 15:16], scal[:, 15:16], 1.0 / (65536.0 * QMAX), None,
                ALU.mult), dr=True)
            k.do(VE, lambda v: v.tensor_tensor(
                scal[:, 16:17], scal[:, 14:15], scal[:, 15:16], ALU.mult), dr=True)
            k.do(VE, lambda v: v.tensor_tensor(
                scal[:, 16:17], scal[:, 16:17], scal[:, 12:13], ALU.mult), dr=True)
            k.do(VE, lambda v: v.tensor_scalar(
                scal[:, 16:17], scal[:, 16:17], 1.0 / QMAX, None, ALU.mult),
                inc="c_scal", dr=True)

            for eng in (SY, SC, VE, TE):
                k.wait(eng, "c_ffp")
            k.wait(TE, "c_ffe")
            k.wait(SY, "c_p4st")
            k.wait(SY, "c_p4st1")
            W_ = WD_OT * P
            for r in range(n_cores):
                k.do(SY, lambda e, r_=r: e.dma_start(
                    out=A(wdr_b.tensor, wdr_b.offset + r_ * W_,
                          [D, OCF], [1, W_]),
                    in_=bass.AP(ap["wdag_o"].tensor, r_ * DF * W_,
                                [[W_, P], [P * W_, OCF], [1, W_]])),
                    inc="c_wdr", incn=16)
            for jt in range(NT):
                sl = jt % 2
                co = "c_ol0" if sl == 0 else "c_ol1"
                cxr = "c_xr0" if sl == 0 else "c_xr1"
                car = "c_ar0" if sl == 0 else "c_ar1"
                if jt >= 2:
                    k.wait(SY, "c_yst" if sl == 0 else "c_yst1")
                    k.wait(SY, "c_o1a", k.count("c_o1a") - 1)
                for (eng, q0, qn) in ((SY, 0, 2 * OCF // 3),
                                      (SC, 2 * OCF // 3, OCF // 3)):
                    k.do(eng, lambda e, s_=sl, j_=jt, a_=q0, n_=qn:
                         e.dma_start(
                             out=o_raw[s_][:, a_ * P:(a_ + n_) * P]
                             .rearrange("p (a b) -> p a b", a=n_),
                             in_=bass.AP(ap["sp_o"].tensor,
                                         j_ * P + a_ * P * T,
                                         [[T, P], [P * T, n_], [1, P]])),
                         inc=co, incn=16)
                k.dma(SY, x_f[sl][:], ap["ao_sp"][jt * P:(jt + 1) * P, :],
                      cxr)
                k.wait(SC, co)
                k.wait(SC, "c_scal")
                if jt >= 2:
                    k.wait(SC, "c_o1v" if sl == 0 else "c_o1v1")
                k.do(SC, lambda sc, s_=sl: sc.activation(
                    iti_p[s_], o_raw[s_], AF.Copy, scale=scal[:, 13:14]),
                    inc="c_o1a")
                k.wait(VE, "c_o1a")
                if jt >= 2:
                    k.wait(VE, "c_dwp", k.count("c_dwp") - 2)
                k.do(VE, lambda v, s_=sl: v.tensor_copy(
                    o1_b[s_], iti_p[s_]),
                    inc=("c_o1v" if sl == 0 else "c_o1v1"))
                k.wait(TE, "c_o1v" if sl == 0 else "c_o1v1")
                k.wait(TE, "c_wdr")
                for og in range(2):
                    bank = psA[2 * (jt % 2) + og]
                    nev = k.count("c_dwe")
                    if nev >= 4:
                        k.wait(TE, "c_dwe", nev - 2)
                    for fc in range(OCF):
                        k.do(TE, lambda t, b=bank, s_=sl, f_=fc, og_=og:
                             t.matmul(b[:, :],
                                      o1_b[s_][:, f_ * P:(f_ + 1) * P],
                                      wdr_b[:, f_ * D + og_ * 512:
                                            f_ * D + (og_ + 1) * 512],
                                      start=(f_ == 0), stop=(f_ == OCF - 1)),
                             inc=("c_dwp" if fc == OCF - 1 else None))
                    k.wait(SC, "c_dwp")
                    k.wait(SC, "c_scal")
                    k.do(SC, lambda sc, b=bank, og_=og: sc.activation(
                        ffn_f[:, og_ * 512:(og_ + 1) * 512], b[:, :],
                        AF.Copy, scale=scal[:, 16:17]), inc="c_dwe")
                k.wait(VE, "c_dwe")
                k.wait(VE, cxr)
                k.do(VE, lambda v, s_=sl: v.tensor_tensor(
                    y_f[s_][:, :], x_f[s_][:], ffn_f[:, 0:D], ALU.add),
                    inc="c_res2")
                k.wait(SC, "c_res2")
                k.do(SC, lambda sc, s_=sl, j_=jt: sc.dma_start(
                    out=yout[j_ * P:(j_ + 1) * P, :], in_=y_f[s_][:, :]),
                    inc=("c_yst" if sl == 0 else "c_yst1"), incn=16)
            k.wait(SY, "c_yst")
            k.wait(SY, "c_yst1")
            if dbg:
                k.dma(SY, ap["dbg_xn"], ap["xn_sp"], "c_misc")
                k.dma(SY, ap["dbg_vki"], ap["vk_i"], "c_misc")
                k.dma(SY, ap["dbg_ao"], ap["ao_sp"], "c_misc")
                k.dma(SY, ap["dbg_up"], ap["sp_up"], "c_misc")
                k.do(SY, lambda e: e.dma_start(out=ap["dbg_scal"],
                                               in_=scal[:, :]),
                     inc="c_misc", incn=16)
                k.wait(SY, "c_misc")

        @block.sync
        def _(eng):
            program(K(SY, eng, sems))

        @block.scalar
        def _(eng):
            program(K(SC, eng, sems))

        @block.vector
        def _(eng):
            program(K(VE, eng, sems))

        @block.tensor
        def _(eng):
            program(K(TE, eng, sems))

        @block.gpsimd
        def _(eng):
            program(K(GP, eng, sems))

    return nc



def _sigma_perm():
    perm = np.empty(D, dtype=np.int64)
    for h in range(NH):
        for i in range(32):
            perm[i * 16 + h] = h * DH + 2 * i
            perm[512 + i * 16 + h] = h * DH + 2 * i + 1
    return perm


def _tau_perm():
    perm = np.empty(D, dtype=np.int64)
    for h in range(NH):
        for j2 in range(DH):
            orig = 2 * j2 if j2 < 32 else 2 * (j2 - 32) + 1
            perm[h * DH + j2] = h * DH + orig
    return perm


def _rope_tables(pos0, T):
    inv_freq = 1.0 / (10000.0 ** (np.arange(0, DH, 2, dtype=np.float64) / DH))
    t = np.arange(pos0, pos0 + T, dtype=np.float64)
    freqs = np.outer(t, inv_freq)
    cos_e = np.repeat(np.cos(freqs), 16, axis=1).astype(ml_dtypes.bfloat16)
    sin_e = np.repeat(np.sin(freqs), 16, axis=1).astype(ml_dtypes.bfloat16)
    return np.ascontiguousarray(cos_e), np.ascontiguousarray(sin_e)


_BUILD_CACHE = {}


def kernel(L0_attn_x, L0_attn_Wqkv, L0_attn_bqkv, L0_attn_gq, L0_attn_gk,
           L0_attn_Wout, L0_attn_bout, L0_attn_gout, L0_ffn_w_gate,
           L0_ffn_w_up, L0_ffn_w_down, _n_cores=8, _T=None, _stiles=4,
           _dbg=False):
    B, L, _ = L0_attn_x.shape
    n_cores = _n_cores
    T = _T if _T is not None else (B * L) // n_cores
    key = (T, n_cores, _stiles, _dbg)
    if key not in _BUILD_CACHE:
        _BUILD_CACHE[key] = build_nc(T, n_cores, _stiles, dbg=_dbg)
    nc = _BUILD_CACHE[key]

    sig = _sigma_perm()
    tau = _tau_perm()
    f32 = np.float32

    wq = np.ascontiguousarray(np.asarray(L0_attn_Wqkv, dtype=f32).T)
    wqp = np.empty_like(wq)
    bq = np.asarray(L0_attn_bqkv, dtype=f32)
    bqp = np.empty_like(bq)
    for g in range(3):
        wqp[:, g * D:(g + 1) * D] = wq[:, g * D:(g + 1) * D][:, sig]
        bqp[g * D:(g + 1) * D] = bq[g * D:(g + 1) * D][sig]
    wo = np.ascontiguousarray(np.asarray(L0_attn_Wout, dtype=f32).T)
    wop = np.ascontiguousarray(wo[tau, :])
    gq = np.asarray(L0_attn_gq, dtype=f32)[sig]
    gk = np.asarray(L0_attn_gk, dtype=f32)[sig]
    gout = np.asarray(L0_attn_gout, dtype=f32)
    bout = np.asarray(L0_attn_bout, dtype=f32)

    def rep(vec):
        return np.ascontiguousarray(np.broadcast_to(vec, (P, vec.shape[0])))

    wu_o = np.asarray(L0_ffn_w_up, dtype=f32)
    wg_o = np.asarray(L0_ffn_w_gate, dtype=f32)
    wd_o = np.asarray(L0_ffn_w_down, dtype=f32)
    ident = np.eye(P, dtype=f32)

    x = np.asarray(L0_attn_x, dtype=f32)
    wu_rows = DF // n_cores
    wd_rows = D // n_cores

    in_maps = []
    for c in range(n_cores):
        b, h = c // 2, c % 2
        if n_cores > 1:
            xs = np.ascontiguousarray(x[b, h * T:(h + 1) * T, :])
            pos0 = h * T
        else:
            xs = np.ascontiguousarray(x.reshape(-1, D)[:T])
            pos0 = 0
        cos_e, sin_e = _rope_tables(pos0, T)
        in_maps.append({
            "x": xs, "rc": cos_e, "rs": sin_e,
            "wqkvT": wqp, "bqkv_rep": rep(bqp),
            "gq_rep": rep(gq), "gk_rep": rep(gk),
            "woutT": wop, "bout_rep": rep(bout), "gout_rep": rep(gout),
            "wu_slice": np.ascontiguousarray(
                wu_o[c * wu_rows:(c + 1) * wu_rows]),
            "wg_slice": np.ascontiguousarray(
                wg_o[c * wu_rows:(c + 1) * wu_rows]),
            "wd_slice": np.ascontiguousarray(
                wd_o[c * wd_rows:(c + 1) * wd_rows]),
            "ident": ident,
        })

    res = run_bass_kernel_spmd(nc, in_maps, core_ids=list(range(n_cores)))
    outs = [res.results[c]["y"] for c in range(n_cores)]
    y = np.empty((B, L, D), dtype=np.float32)
    if n_cores > 1:
        for c in range(n_cores):
            b, h = c // 2, c % 2
            y[b, h * T:(h + 1) * T, :] = outs[c]
    else:
        y.reshape(-1, D)[:T] = outs[0]
    return y
